# version 04
# baseline (speedup 1.0000x reference)
import numpy as np
import concourse.bass as bass
import concourse.mybir as mybir
from concourse.bass_utils import run_bass_kernel_spmd
from concourse.alu_op_type import AluOpType as OP

F32 = mybir.dt.float32
BF16 = mybir.dt.bfloat16
I32 = mybir.dt.int32
AF = mybir.ActivationFunctionType
AX = mybir.AxisListType.X

ENGINES = ("tensor", "vector", "scalar", "sync", "gpsimd")
NH = 32
NIC = 86
ISQ = 0.07216878364870322
OWN_QQ0 = 12
DPARTS = [(0, 32), (32, 32), (64, 22)]
QK = 6
QUEUES = ("sd", "gw", "sc")


class Prog:
    def __init__(self, nc):
        self.nc = nc
        self.ops = {e: [] for e in ENGINES}
        self.sems = {}
        self.counts = {}
        names = ["tq", "vq", "cq"]
        for q in QUEUES:
            names += ["%s%d" % (q, i) for i in range(QK)]
        for name in names:
            self.sems[name] = nc.alloc_semaphore(name)
            self.counts[name] = 0
        self.esem = dict(tensor="tq", vector="vq", scalar="cq")
        self.rr = {q: 0 for q in QUEUES}

    def emit(self, eng, fn, waits=(), inc=None):
        waits = dict(waits)
        if inc in QUEUES:
            k = self.rr[inc]
            self.rr[inc] = (k + 1) % QK
            name = "%s%d" % (inc, k)
            waits[name] = max(waits.get(name, 0), self.counts[name])
            self.counts[name] += 16
            rec = (name, self.counts[name])
        elif inc is not None:
            self.counts[inc] += 1
            rec = (inc, self.counts[inc])
        else:
            rec = None
        self.ops[eng].append([fn, waits, rec])
        return {rec[0]: rec[1]} if rec else {}

    def ev(self, sem):
        if sem in QUEUES:
            return {"%s%d" % (sem, k): self.counts["%s%d" % (sem, k)]
                    for k in range(QK)
                    if self.counts["%s%d" % (sem, k)] > 0}
        return {sem: self.counts[sem]}

    def force_inc(self, eng):
        sem = self.esem[eng]
        for op in reversed(self.ops[eng]):
            if op[0] is None:
                continue
            if op[2] is None:
                self.counts[sem] += 1
                op[2] = (sem, self.counts[sem])
            return {op[2][0]: op[2][1]}
        return {}

    def join(self):
        evs = {}
        for e in ("tensor", "vector", "scalar"):
            evs.update(self.force_inc(e))
        evs = merge(evs, self.ev("sd"), self.ev("gw"), self.ev("sc"))
        for e in ENGINES:
            self.emit(e, None, waits=evs)
        return evs

    def realize(self):
        nc = self.nc
        with nc.Block() as block:
            for name in ENGINES:
                def body(eng, ops=self.ops[name], sems=self.sems):
                    for fn, waits, inc in ops:
                        for s, v in waits.items():
                            eng.wait_ge(sems[s], v)
                        if fn is None:
                            continue
                        ins = fn(eng)
                        if inc is not None:
                            ins.then_inc(sems[inc[0]],
                                         1 if inc[0] in ("tq", "vq", "cq") else 16)
                getattr(block, name)(body)


def merge(*evs):
    out = {}
    for e in evs:
        for k, v in e.items():
            out[k] = max(out.get(k, 0), v)
    return out


def build_program():
    nc = bass.Bass("TRN2", target_bir_lowering=False)

    dr = {}
    def din(name, shape, dt):
        dr[name] = nc.dram_tensor(name, shape, dt, kind="ExternalInput").ap()
    din("ax", [16, 128, 32, 256], F32)
    din("aqc", [12, 128, 32, 128], F32)
    din("akvc", [5, 128, 32, 128], F32)
    din("aqup", [32, 128, 12, 192], F32)
    din("akvup", [32, 128, 4, 256], F32)
    din("awo", [128, 4096], F32)
    din("bqc2", [128, 12], F32)
    din("bquph", [128, 64], F32)
    din("bkvc2", [128, 5], F32)
    din("bkvuk", [128, 64], F32)
    din("bvrow", [1, 4096], F32)
    din("bo2", [128, 32], F32)
    din("ag", [86, 128, 32, 128], I32)
    din("au", [86, 128, 32, 128], I32)
    din("ad", [32, 128, 86, 128], I32)
    din("sg", [32, 11008], F32)
    din("zg", [32, 11008], F32)
    din("su", [32, 11008], F32)
    din("zu", [32, 11008], F32)
    din("sdn", [86, 4096], F32)
    din("zdn", [86, 4096], F32)
    outT = nc.dram_tensor("outT", [4096, 1024], F32, kind="ExternalOutput").ap()

    hactD = nc.dram_tensor("hactD", [128, 88064], BF16).ap()
    z1D = nc.dram_tensor("z1D", [32, 128, 1024], BF16).ap()
    z2D = nc.dram_tensor("z2D", [32, 128, 1024], BF16).ap()
    y1D = nc.dram_tensor("y1D", [32, 128, 1024], BF16).ap()
    daccD = nc.dram_tensor("daccD", [32, 128, 1024], F32).ap()

    U = nc.alloc_sbuf_tensor("U", [128, 32768], BF16)
    AR = nc.alloc_sbuf_tensor("AR", [128, 32768], BF16)
    ARF = nc.alloc_sbuf_tensor("ARF", [128, 5120], F32)
    WOT = nc.alloc_sbuf_tensor("WOT", [128, 4096], BF16)
    OS = nc.alloc_sbuf_tensor("OS", [128, 1024], F32)
    RBUF = nc.alloc_sbuf_tensor("RBUF", [128, 1024], F32)
    RSTDB = nc.alloc_sbuf_tensor("RSTDB", [1, 1024], BF16)
    NMRB = nc.alloc_sbuf_tensor("NMRB", [1, 1024], BF16)
    RECB = nc.alloc_sbuf_tensor("RECB", [1, 1024], BF16)
    STGB = nc.alloc_sbuf_tensor("STGB", [1, 1024], BF16)
    RN = nc.alloc_sbuf_tensor("RN", [1, 4096], BF16)
    BVROW = nc.alloc_sbuf_tensor("BVROW", [1, 4096], BF16)
    YG = nc.alloc_sbuf_tensor("YG", [32, 1024], BF16)
    YH = nc.alloc_sbuf_tensor("YH", [86, 1024], BF16)
    BQC = nc.alloc_sbuf_tensor("BQC", [128, 12], F32)
    BQUP = nc.alloc_sbuf_tensor("BQUP", [128, 64], F32)
    BKVC = nc.alloc_sbuf_tensor("BKVC", [128, 5], F32)
    BKVUK = nc.alloc_sbuf_tensor("BKVUK", [128, 64], F32)
    BO = nc.alloc_sbuf_tensor("BO", [128, 32], F32)
    ONES = nc.alloc_sbuf_tensor("ONES", [128, 1], BF16)
    ONESR = nc.alloc_sbuf_tensor("ONESR", [1, 128], BF16)
    EPS1 = nc.alloc_sbuf_tensor("EPS1", [1, 1], F32)
    MISC = nc.alloc_sbuf_tensor("MISC", [1, 16], F32)
    MISCB = nc.alloc_sbuf_tensor("MISCB", [1, 16], BF16)
    NEGC = nc.alloc_sbuf_tensor("NEGC", [128, 2], F32)
    ZSG = nc.alloc_sbuf_tensor("ZSG", [32, 512], BF16)
    ZSF = nc.alloc_sbuf_tensor("ZSF", [32, 1024], F32)
    ZSD = nc.alloc_sbuf_tensor("ZSD", [86, 256], BF16)
    ZSDF = nc.alloc_sbuf_tensor("ZSDF", [86, 512], F32)
    SQZ2 = nc.alloc_sbuf_tensor("SQZ2", [128, 1024], BF16)

    MROW = ARF[0:1, 0:1024]
    MSQ = ARF[0:1, 1024:2048]
    TROW = ARF[0:1, 2048:3072]

    PA = nc.alloc_psum_tensor("PA", [128, 1024], F32)
    PB = nc.alloc_psum_tensor("PB", [128, 1024], F32)
    PC = nc.alloc_psum_tensor("PC", [128, 1024], F32)
    PD = nc.alloc_psum_tensor("PD", [128, 512], F32)
    PE_ = nc.alloc_psum_tensor("PE", [33, 512], F32)

    p = Prog(nc)

    def mm(ps, lhs, rhs, start, stop, waits=()):
        p.emit("tensor",
               lambda e, a=ps, b=lhs, c=rhs, s1=start, s2=stop:
               e.matmul(a, b, c, start=s1, stop=s2), waits=waits)

    p.emit("vector", lambda e: e.memset(ONES[:], 1.0))
    p.emit("vector", lambda e: e.memset(ONESR[:], 1.0))
    p.emit("vector", lambda e: e.memset(EPS1[:], 1e-5))
    for nm, tgt in (("bqc2", BQC), ("bquph", BQUP), ("bkvc2", BKVC),
                    ("bkvuk", BKVUK), ("bo2", BO)):
        p.emit("sync", lambda e, a=tgt, b=dr[nm]: e.dma_start(a[:], b), inc="sd")
    ev_bias = p.ev("sd")
    ev_bq = p.emit("vector",
                   lambda e: e.tensor_scalar(BQUP[:], BQUP[:], ISQ, None, OP.mult),
                   waits=ev_bias, inc="vq")
    p.emit("gpsimd", lambda e: e.dma_start(BVROW[:], dr["bvrow"]), inc="gw")
    ev_wo = p.emit("gpsimd", lambda e: e.dma_start(WOT[:], dr["awo"]), inc="gw")

    KVC = lambda m: AR[:, m * 4096:(m + 1) * 4096]
    KVCR = AR[0:64, 16384:20480]
    QCC = lambda c: AR[:, 20480 + c * 1024: 20480 + (c + 1) * 1024]
    XQ = [U[:, 0:8192], U[:, 8192:16384]]
    P1W = [U[:, 16384:20480], U[:, 20480:24576]]

    xq_ev = [{}, {}]
    xq_cons = [{}, {}]
    w_cons = {}
    psl_cons = {}
    wslot = 0
    pslot = 0
    for qq in range(16):
        bx = qq % 2
        xq_ev[bx] = p.emit("gpsimd",
                           lambda e, b=bx, q=qq: e.dma_start(
                               XQ[b].rearrange("p (c t) -> p c t", t=256), dr["ax"][q]),
                           waits=xq_cons[bx], inc="gw")
        own = qq >= OWN_QQ0
        mlist = list(range(5)) + (list(range(100, 112)) if own else [])
        for m in mlist:
            isq = m >= 100
            ws = wslot % 2
            wslot += 1
            src = dr["aqc"][m - 100] if isq else dr["akvc"][m]
            wt = P1W[ws]
            w_ev = p.emit("gpsimd",
                          lambda e, a=wt, b=src: e.dma_start(
                              a.rearrange("p (c j) -> p c j", j=128), b),
                          waits=w_cons.get(ws, {}), inc="gw")
            sl = pslot % 4
            pslot += 1
            ps = [PA[:, 0:512], PA[:, 512:1024], PB[:, 0:512], PB[:, 512:1024]][sl]
            wv = merge(w_ev, xq_ev[bx], psl_cons.get(sl, {}))
            for c in range(32):
                mm(ps[:, 0:256], wt[:, c * 128:(c + 1) * 128],
                   XQ[bx][:, c * 256:(c + 1) * 256],
                   start=(c == 0), stop=(c == 31), waits=wv if c == 0 else ())
            ev_t = p.force_inc("tensor")
            w_cons[ws] = ev_t
            xq_cons[bx] = merge(xq_cons[bx], ev_t)
            if isq:
                c = m - 100
                dst = QCC(c)[:, (qq - OWN_QQ0) * 256:(qq - OWN_QQ0 + 1) * 256]
                fn = (lambda e, a=dst, b=ps, bb=BQC[:, c:c + 1]:
                      e.activation(a, b[:, 0:256], AF.Identity, bias=bb))
            elif m < 4:
                dst = KVC(m)[:, qq * 256:(qq + 1) * 256]
                fn = (lambda e, a=dst, b=ps, bb=BKVC[:, m:m + 1]:
                      e.activation(a, b[:, 0:256], AF.Identity, bias=bb))
            else:
                dst = KVCR[:, qq * 256:(qq + 1) * 256]
                fn = (lambda e, a=dst, b=ps, bb=BKVC[0:64, 4:5]:
                      e.activation(a, b[0:64, 0:256], AF.Identity, bias=bb))
            psl_cons[sl] = p.emit("scalar", fn, waits=merge(ev_t, ev_bias), inc="cq")

    p.join()

    SQ = U[:, 22528:23552]
    rn_done = {}
    for sl4 in range(4):
        evq = p.emit("vector",
                     lambda e, s=sl4: e.tensor_tensor(
                         SQ[0:64, :], KVCR[:, s * 1024:(s + 1) * 1024],
                         KVCR[:, s * 1024:(s + 1) * 1024], OP.mult),
                     waits=rn_done, inc="vq")
        for nh2 in range(2):
            mm(PD[0:1, :], ONES[0:64, :], SQ[0:64, nh2 * 512:(nh2 + 1) * 512],
               start=True, stop=True, waits=merge(evq, rn_done))
            evt = p.force_inc("tensor")
            rn_done = p.emit("vector",
                             lambda e, s=sl4, n=nh2: e.tensor_copy(
                                 RN[0:1, s * 1024 + n * 512:s * 1024 + (n + 1) * 512],
                                 PD[0:1, :]),
                             waits=evt, inc="vq")
    p.join()

    VT = [U[:, 0:4096], U[:, 4096:8192]]
    KH = [U[:, 8192:12288], U[:, 12288:16384]]
    QH = [U[:, 16384:17408], U[:, 17408:18432]]
    QR = [U[0:64, 18432:19456], U[0:64, 19456:20480]]
    ET = [U[:, 20480:21504], U[:, 21504:22528]]
    QUPW = [U[:, 23552:25856], U[:, 25856:28160]]
    KVUPW = [U[:, 28160:29184], U[:, 29184:30208]]
    TMPB = U[:, 30208:31232]

    qupw_ev = [{}, {}]
    kvupw_ev = [{}, {}]
    qupw_cons = [{}, {}]
    kvupw_cons = [{}, {}]
    build_cons = [{}, {}]
    pd_cons = {}
    pa_cons = [{}, {}]
    pc_cons = {}
    pe_cons = {}
    et_cons = [{}, {}]
    rbuf_cons = {}
    os_ev = {}

    for hh in range(min(2, NH)):
        ph = hh % 2
        qupw_ev[ph] = p.emit("gpsimd",
                             lambda e, b=ph, h=hh: e.dma_start(
                                 QUPW[b].rearrange("p (c j) -> p c j", j=192),
                                 dr["aqup"][h]), inc="gw")
        kvupw_ev[ph] = p.emit("gpsimd",
                              lambda e, b=ph, h=hh: e.dma_start(
                                  KVUPW[b].rearrange("p (c j) -> p c j", j=256),
                                  dr["akvup"][h]), inc="gw")

    for h in range(NH):
        ph = h % 2
        for ks in range(8):
            wv = merge(kvupw_ev[ph], pd_cons, build_cons[ph] if ks == 0 else {})
            for c in range(4):
                mm(PD[:], KVUPW[ph][:, c * 256:c * 256 + 128],
                   KVC(c)[:, ks * 512:(ks + 1) * 512],
                   start=(c == 0), stop=(c == 3), waits=wv if c == 0 else ())
            evt = p.force_inc("tensor")
            pd_cons = p.emit("scalar",
                             lambda e, s=ks, b=ph, hh=h: e.activation(
                                 KH[b][:, s * 512:(s + 1) * 512], PD[:], AF.Identity,
                                 bias=BKVUK[:, 2 * hh:2 * hh + 1]),
                             waits=evt, inc="cq")
        ev_kh = pd_cons
        for vu in range(8):
            wv = merge(kvupw_ev[ph], pd_cons)
            for tc4 in range(4):
                tc = vu * 4 + tc4
                dst = PD[:, tc4 * 128:(tc4 + 1) * 128]
                for c in range(4):
                    mm(dst, AR[:, c * 4096 + tc * 128:c * 4096 + (tc + 1) * 128],
                       KVUPW[ph][:, c * 256 + 128:(c + 1) * 256],
                       start=(tc4 == 0 and c == 0), stop=False,
                       waits=wv if (tc4 == 0 and c == 0) else ())
                mm(dst, ONESR[:], BVROW[0:1, h * 128:(h + 1) * 128],
                   start=False, stop=(tc4 == 3))
            evt = p.force_inc("tensor")
            pd_cons = p.emit("scalar",
                             lambda e, b=ph, u=vu: e.activation(
                                 VT[b][:, u * 512:(u + 1) * 512], PD[:], AF.Identity),
                             waits=evt, inc="cq")
        ev_vt = pd_cons
        kvupw_cons[ph] = p.force_inc("tensor")
        for nhf in range(2):
            wv = merge(qupw_ev[ph], pd_cons)
            for c in range(12):
                mm(PD[:], QUPW[ph][:, c * 192:c * 192 + 128],
                   QCC(c)[:, nhf * 512:(nhf + 1) * 512],
                   start=(c == 0), stop=(c == 11), waits=wv if c == 0 else ())
            evt = p.force_inc("tensor")
            pd_cons = p.emit("scalar",
                             lambda e, b=ph, n=nhf, hh=h: e.activation(
                                 QH[b][:, n * 512:(n + 1) * 512], PD[:], AF.Identity,
                                 bias=BQUP[:, 2 * hh:2 * hh + 1], scale=ISQ),
                             waits=merge(evt, ev_bq), inc="cq")
            wv = merge(qupw_ev[ph], pd_cons)
            for c in range(12):
                mm(PD[0:64, :], QUPW[ph][:, c * 192 + 128:c * 192 + 192][0:64, :]
                   if False else QUPW[ph][:, c * 192 + 128:c * 192 + 192],
                   QCC(c)[:, nhf * 512:(nhf + 1) * 512],
                   start=(c == 0), stop=(c == 11), waits=wv if c == 0 else ())
            evt = p.force_inc("tensor")
            pd_cons = p.emit("scalar",
                             lambda e, b=ph, n=nhf, hh=h: e.activation(
                                 QR[b][:, n * 512:(n + 1) * 512], PD[0:64, :], AF.Identity,
                                 bias=BQUP[0:64, 2 * hh + 1:2 * hh + 2], scale=ISQ),
                             waits=merge(evt, ev_bq), inc="cq")
        ev_qh = pd_cons
        qupw_cons[ph] = p.force_inc("tensor")

        if h + 2 < NH:
            qupw_ev[ph] = p.emit("gpsimd",
                                 lambda e, b=ph, hh=h + 2: e.dma_start(
                                     QUPW[b].rearrange("p (c j) -> p c j", j=192),
                                     dr["aqup"][hh]),
                                 waits=qupw_cons[ph], inc="gw")
            kvupw_ev[ph] = p.emit("gpsimd",
                                  lambda e, b=ph, hh=h + 2: e.dma_start(
                                      KVUPW[b].rearrange("p (c j) -> p c j", j=256),
                                      dr["akvup"][hh]),
                                  waits=kvupw_cons[ph], inc="gw")

        ev1 = p.emit("vector",
                     lambda e, b=ph: e.tensor_tensor(SQ[:], QH[b][:], QH[b][:], OP.mult),
                     waits=merge(ev_qh, pe_cons), inc="vq")
        mm(PD[0:1, :], ONES[:], SQ[:, 0:512], start=True, stop=False,
           waits=merge(ev1, pd_cons))
        mm(PD[32:33, :], ONES[:], SQ[:, 512:1024], start=True, stop=False)
        evt = p.force_inc("tensor")
        ev2 = p.emit("vector",
                     lambda e, b=ph: e.tensor_tensor(SQ[0:64, :], QR[b][:], QR[b][:], OP.mult),
                     waits=merge(evt, ev_qh), inc="vq")
        mm(PD[0:1, :], ONES[0:64, :], SQ[0:64, 0:512], start=False, stop=True, waits=ev2)
        mm(PD[32:33, :], ONES[0:64, :], SQ[0:64, 512:1024], start=False, stop=True)
        evt = p.force_inc("tensor")
        ev3 = p.emit("vector", lambda e: e.reduce_max(MISC[0:1, 0:1], PD[0:1, :], axis=AX),
                     waits=evt, inc="vq")
        ev4 = p.emit("vector", lambda e: e.reduce_max(MISC[0:1, 1:2], PD[32:33, :], axis=AX),
                     waits=evt, inc="vq")
        qn_done = p.emit("vector",
                         lambda e: e.tensor_tensor(MISC[0:1, 0:1], MISC[0:1, 0:1],
                                                   MISC[0:1, 1:2], OP.max),
                         waits=merge(ev3, ev4), inc="vq")
        pd_cons = qn_done
        p.emit("vector", lambda e: e.memset(MISC[0:1, 2:3], 0.0))
        ev_kn = {}
        for sl4 in range(4):
            ev5 = p.emit("vector",
                         lambda e, b=ph, s=sl4: e.tensor_tensor(
                             SQ[:], KH[b][:, s * 1024:(s + 1) * 1024],
                             KH[b][:, s * 1024:(s + 1) * 1024], OP.mult),
                         waits=merge(ev_kh, qn_done), inc="vq")
            mm(PD[0:1, :], ONES[:], SQ[:, 0:512], start=True, stop=True,
               waits=merge(ev5, pd_cons))
            mm(PD[32:33, :], ONES[:], SQ[:, 512:1024], start=True, stop=True)
            evt = p.force_inc("tensor")
            e6 = p.emit("vector",
                        lambda e, s=sl4: e.tensor_tensor(
                            TROW[0:1, 0:512], PD[0:1, :],
                            RN[0:1, s * 1024:s * 1024 + 512], OP.add),
                        waits=evt, inc="vq")
            e7 = p.emit("vector",
                        lambda e, s=sl4: e.tensor_tensor(
                            TROW[0:1, 512:1024], PD[32:33, :],
                            RN[0:1, s * 1024 + 512:(s + 1) * 1024], OP.add),
                        waits=evt, inc="vq")
            pd_cons = merge(e6, e7)
            e8 = p.emit("vector",
                        lambda e: e.reduce_max(MISC[0:1, 3:4], TROW[0:1, :], axis=AX),
                        waits=merge(e6, e7), inc="vq")
            ev_kn = p.emit("vector",
                           lambda e: e.tensor_tensor(MISC[0:1, 2:3], MISC[0:1, 2:3],
                                                     MISC[0:1, 3:4], OP.max),
                           waits=e8, inc="vq")
        ev9 = p.emit("vector",
                     lambda e: e.tensor_tensor(MISC[0:1, 4:5], MISC[0:1, 0:1],
                                               MISC[0:1, 2:3], OP.mult),
                     waits=ev_kn, inc="vq")
        ev10 = p.emit("scalar",
                      lambda e: e.activation(MISC[0:1, 5:6], MISC[0:1, 4:5], AF.Ln,
                                             bias=EPS1[:]),
                      waits=ev9, inc="cq")
        ev11 = p.emit("scalar",
                      lambda e: e.activation(MISC[0:1, 6:7], MISC[0:1, 5:6], AF.Exp,
                                             scale=0.5),
                      waits=ev10, inc="cq")
        ev12 = p.emit("vector",
                      lambda e: e.tensor_scalar(MISCB[0:1, 0:1], MISC[0:1, 6:7],
                                                -1.0, None, OP.mult),
                      waits=ev11, inc="vq")
        mm(PD[:, 0:1], ONESR[:], MISCB[0:1, 0:1], start=True, stop=True,
           waits=merge(ev12, pd_cons))
        evt = p.force_inc("tensor")
        ev_negc = p.emit("scalar",
                         lambda e, b=ph: e.activation(NEGC[:, b:b + 1], PD[:, 0:1],
                                                      AF.Identity),
                         waits=evt, inc="cq")
        pd_cons = ev_negc

        exp_ev = {}
        scores_ev = {}
        for kt in range(33):
            ep = kt % 2
            if kt < 32:
                psc = PA if ep == 0 else PB
                wv = merge(ev_kh, ev_qh, ev_negc, pa_cons[ep],
                           build_cons[ph] if kt == 0 else {})
                for nhf in range(2):
                    sl = psc[:, nhf * 512:(nhf + 1) * 512]
                    mm(sl, KH[ph][:, kt * 128:(kt + 1) * 128],
                       QH[ph][:, nhf * 512:(nhf + 1) * 512],
                       start=True, stop=False, waits=wv if nhf == 0 else ())
                    mm(sl, KVCR[:, kt * 128:(kt + 1) * 128],
                       QR[ph][:, nhf * 512:(nhf + 1) * 512],
                       start=False, stop=True)
                scores_ev = p.force_inc("tensor")
            if kt >= 1:
                kp = (kt - 1) % 2
                wz = merge(exp_ev, pe_cons if kt == 1 else {})
                mm(PE_[0:1, :], ONES[:], ET[kp][:, 0:512],
                   start=(kt == 1), stop=(kt == 32), waits=wz)
                mm(PE_[32:33, :], ONES[:], ET[kp][:, 512:1024],
                   start=(kt == 1), stop=(kt == 32))
                for nhf in range(2):
                    mm(PC[:, nhf * 512:(nhf + 1) * 512],
                       VT[ph][:, (kt - 1) * 128:kt * 128],
                       ET[kp][:, nhf * 512:(nhf + 1) * 512],
                       start=(kt == 1), stop=(kt == 32),
                       waits=merge(ev_vt, pc_cons) if (kt == 1 and nhf == 0) else ())
                et_cons[kp] = p.force_inc("tensor")
            if kt < 32:
                exp_ev = p.emit("scalar",
                                lambda e, b=ph, ee=ep: e.activation(
                                    ET[ee][:], (PA if ee == 0 else PB)[:], AF.Exp,
                                    bias=NEGC[:, b:b + 1]),
                                waits=merge(scores_ev, et_cons[ep]), inc="cq")
                pa_cons[ep] = exp_ev
        build_cons[ph] = p.force_inc("tensor")

        ez = build_cons[ph]
        r1 = p.emit("vector", lambda e: e.reciprocal(TROW[0:1, 0:512], PE_[0:1, :]),
                    waits=ez, inc="vq")
        r2 = p.emit("vector", lambda e: e.reciprocal(TROW[0:1, 512:1024], PE_[32:33, :]),
                    waits=ez, inc="vq")
        pe_cons = merge(r1, r2)
        r3 = p.emit("vector", lambda e: e.tensor_copy(RECB[:], TROW[:]),
                    waits=pe_cons, inc="vq")
        for nhf in range(2):
            mm(PD[:], ONESR[:], RECB[0:1, nhf * 512:(nhf + 1) * 512],
               start=True, stop=True,
               waits=merge(r3, pd_cons))
            evt = p.force_inc("tensor")
            pd_cons = p.emit("scalar",
                             lambda e, n=nhf: e.activation(
                                 RBUF[:, n * 512:(n + 1) * 512], PD[:], AF.Identity),
                             waits=merge(evt, rbuf_cons), inc="cq")
        if h == 0:
            ev13 = p.emit("vector",
                          lambda e: e.tensor_tensor(OS[:], PC[:], RBUF[:], OP.mult),
                          waits=merge(pd_cons, ez), inc="vq")
            pc_cons = ev13
            rbuf_cons = ev13
            os_ev = ev13
        else:
            ev13 = p.emit("vector",
                          lambda e: e.tensor_tensor(TMPB, PC[:], RBUF[:], OP.mult),
                          waits=merge(pd_cons, ez), inc="vq")
            pc_cons = ev13
            rbuf_cons = ev13
            os_ev = p.emit("vector",
                           lambda e: e.tensor_tensor(OS[:], OS[:], TMPB, OP.add),
                           waits=ev13, inc="vq")

    p.join()

    OSBF = AR[:, 0:1024]
    Z1STG = [AR[:, 1024:2048], AR[:, 2048:3072]]
    SQZ1 = AR[:, 3072:4096]
    TMPB2 = AR[:, 4096:5120]
    XF = [ARF[:, 0:1024], ARF[:, 1024:2048]]

    ev_osbf = p.emit("vector", lambda e: e.tensor_copy(OSBF, OS[:]),
                     waits=os_ev, inc="vq")
    xf_cons = [{}, {}]
    z1_cons = [{}, {}]
    pa2_cons = [{}, {}]
    for hc in range(32):
        pp = hc % 2
        evs = []
        for k4 in range(4):
            evs.append(p.emit("sync",
                              lambda e, b=pp, q=OWN_QQ0 + k4, c=hc, kk=k4: e.dma_start(
                                  XF[b][:, kk * 256:(kk + 1) * 256], dr["ax"][q, :, c, :]),
                              waits=xf_cons[pp] if k4 == 0 else (), inc="sd"))
        xf_ev = merge(*evs)
        psc = PA if pp == 0 else PB
        wv = merge(ev_osbf, ev_wo, pa2_cons[pp])
        for nhf in range(2):
            mm(psc[:, nhf * 512:(nhf + 1) * 512], WOT[:, hc * 128:(hc + 1) * 128],
               OSBF[:, nhf * 512:(nhf + 1) * 512], start=True, stop=True,
               waits=wv if nhf == 0 else ())
        evt = p.force_inc("tensor")
        ev_z1 = p.emit("vector",
                       lambda e, b=pp, c=hc: e.scalar_tensor_tensor(
                           Z1STG[b], (PA if b == 0 else PB)[:], BO[:, c:c + 1], XF[b],
                           OP.add, OP.add),
                       waits=merge(evt, xf_ev, z1_cons[pp]), inc="vq")
        xf_cons[pp] = ev_z1
        pa2_cons[pp] = ev_z1
        esp = p.emit("scalar", lambda e, b=pp, c=hc: e.dma_start(z1D[c], Z1STG[b]),
                     waits=ev_z1, inc="sc")
        ev_sq = p.emit("vector",
                       lambda e, b=pp: e.tensor_tensor(SQZ1, Z1STG[b], Z1STG[b], OP.mult),
                       waits=ev_z1, inc="vq")
        st = hc == 0
        sp = hc == 31
        mm(PD[0:1, :], ONES[:], Z1STG[pp][:, 0:512], start=st, stop=sp,
           waits=merge(ev_z1, pd_cons) if st else ev_z1)
        mm(PD[32:33, :], ONES[:], Z1STG[pp][:, 512:1024], start=st, stop=sp)
        mm(PD[64:65, :], ONES[:], SQZ1[:, 0:512], start=st, stop=sp,
           waits=merge(ev_sq, pe_cons) if st else ev_sq)
        mm(PE_[0:1, :], ONES[:], SQZ1[:, 512:1024], start=st, stop=sp)
        z1_cons[pp] = merge(esp, p.force_inc("tensor"))

    def ln_finalize(pm_sq_hi):
        evt = p.force_inc("tensor")
        e1 = p.emit("vector", lambda e: e.tensor_scalar(
            MROW[0:1, 0:512], PD[0:1, :], 1.0 / 4096, None, OP.mult), waits=evt, inc="vq")
        e2 = p.emit("vector", lambda e: e.tensor_scalar(
            MROW[0:1, 512:1024], PD[32:33, :], 1.0 / 4096, None, OP.mult), waits=evt, inc="vq")
        e3 = p.emit("vector", lambda e: e.tensor_scalar(
            MSQ[0:1, 0:512], PD[64:65, :], 1.0 / 4096, None, OP.mult), waits=evt, inc="vq")
        e4 = p.emit("vector", lambda e: e.tensor_scalar(
            MSQ[0:1, 512:1024], pm_sq_hi, 1.0 / 4096, None, OP.mult), waits=evt, inc="vq")
        e5 = p.emit("vector", lambda e: e.tensor_tensor(TROW[:], MROW[:], MROW[:], OP.mult),
                    waits=merge(e1, e2), inc="vq")
        e6 = p.emit("vector", lambda e: e.tensor_tensor(TROW[:], MSQ[:], TROW[:], OP.subtract),
                    waits=merge(e3, e4, e5), inc="vq")
        e7 = p.emit("scalar", lambda e: e.activation(TROW[:], TROW[:], AF.Ln, bias=EPS1[:]),
                    waits=e6, inc="cq")
        e8 = p.emit("scalar", lambda e: e.activation(TROW[:], TROW[:], AF.Exp, scale=-0.5),
                    waits=e7, inc="cq")
        e9 = p.emit("vector", lambda e: e.tensor_copy(RSTDB[:], TROW[:]), waits=e8, inc="vq")
        e10 = p.emit("vector", lambda e: e.tensor_tensor(MROW[:], MROW[:], TROW[:], OP.mult),
                     waits=e8, inc="vq")
        e11 = p.emit("vector", lambda e: e.tensor_scalar(NMRB[:], MROW[:], -1.0, None, OP.mult),
                     waits=e10, inc="vq")
        return merge(e9, e11), p.ev("tq")

    ev_rows, _ = ln_finalize(PE_[0:1, :])
    wv = merge(ev_rows, pa2_cons[0], pa2_cons[1])
    for nhf in range(2):
        mm(PA[:, nhf * 512:(nhf + 1) * 512], ONESR[:],
           RSTDB[0:1, nhf * 512:(nhf + 1) * 512], start=True, stop=True,
           waits=wv if nhf == 0 else ())
        mm(PB[:, nhf * 512:(nhf + 1) * 512], ONESR[:],
           NMRB[0:1, nhf * 512:(nhf + 1) * 512], start=True, stop=True)
    ev_bc = p.force_inc("tensor")
    y1_last = {}
    tmpb2_cons = {}
    for hc in range(32):
        pp = hc % 2
        evr = p.emit("sync", lambda e, b=pp, c=hc: e.dma_start(Z1STG[b], z1D[c]),
                     waits=merge(z1_cons[pp], ev_bc), inc="sd")
        ea = p.emit("vector",
                    lambda e, b=pp: e.tensor_tensor(TMPB2, Z1STG[b], PA[:], OP.mult),
                    waits=merge(evr, tmpb2_cons), inc="vq")
        eb = p.emit("vector",
                    lambda e, c=hc: e.tensor_tensor(U[:, c * 1024:(c + 1) * 1024],
                                                    TMPB2, PB[:], OP.add),
                    waits=ea, inc="vq")
        tmpb2_cons = eb
        z1_cons[pp] = merge(evr, eb)
        y1_last = p.emit("scalar",
                         lambda e, c=hc: e.dma_start(y1D[c], U[:, c * 1024:(c + 1) * 1024]),
                         waits=eb, inc="sc")

    p.join()

    RAWG = [AR[:, 0:4096].bitcast(I32), AR[:, 4096:8192].bitcast(I32)]
    RAWU = [WOT[:, 0:4096].bitcast(I32), ARF[:, 0:2048].bitcast(I32)]
    GBC = AR[:, 8192:12288]
    UBC = AR[:, 12288:16384]
    GW = [AR[:, 16384:20480], AR[:, 20480:24576]]
    UW = [AR[:, 24576:28672], AR[:, 28672:32768]]
    SILB = ARF[:, 2048:2560].bitcast(BF16)
    HACTB = [ARF[:, 2560:3072].bitcast(BF16), ARF[:, 3072:3584].bitcast(BF16)]
    ZROW = {"g": [ZSG[:, 0:128], ZSG[:, 128:256]],
            "u": [ZSG[:, 256:384], ZSG[:, 384:512]]}
    ZOFF = {"g": [0, 256], "u": [512, 768]}

    ev_yg = {}
    for c in range(32):
        mm(PD[0:1, :], ONES[:], U[:, c * 1024:c * 1024 + 512], start=True, stop=True,
           waits=merge(pd_cons, pe_cons) if c == 0 else pd_cons)
        mm(PD[32:33, :], ONES[:], U[:, c * 1024 + 512:(c + 1) * 1024],
           start=True, stop=True)
        evt = p.force_inc("tensor")
        ea = p.emit("scalar", lambda e: e.activation(STGB[0:1, 0:512], PD[0:1, :], AF.Identity,
                                                     scale=-1.0),
                    waits=merge(evt, ev_yg), inc="cq")
        eb = p.emit("scalar", lambda e: e.activation(STGB[0:1, 512:1024], PD[32:33, :], AF.Identity,
                                                     scale=-1.0),
                    waits=ea, inc="cq")
        pd_cons = eb
        ev_yg = p.emit("scalar", lambda e, cc=c: e.dma_start(YG[cc:cc + 1, :], STGB[:]),
                       waits=eb, inc="sc")

    PSROT = [PA, PB, PC]
    ps_cons = [{}, {}, pc_cons]
    raw_cons = {"g": [{}, {}], "u": [{}, {}]}
    bc_cons = {"g": {}, "u": {}}
    w_cons2 = {"g": [{}, {}], "u": [{}, {}]}
    zs_cons = {"g": [{}, {}], "u": [{}, {}]}
    hact_cons = [{}, {}]
    yh_done = {}
    sil_cons = {}
    stg_cons = pd_cons

    def gu_weights(mat, ic, pg):
        src = dr["ag" if mat == "g" else "au"]
        ssc = dr["sg" if mat == "g" else "su"]
        szr = dr["zg" if mat == "g" else "zu"]
        RAW = RAWG if mat == "g" else RAWU
        BCT = GBC if mat == "g" else UBC
        WT = (GW if mat == "g" else UW)[pg]
        ZR = ZROW[mat][pg]
        off = ZOFF[mat][pg]
        qeng, qsem = ("sync", "sd") if mat == "g" else ("gpsimd", "gw")
        ers = []
        for j in range(2):
            ers.append(p.emit(qeng, lambda e, i=ic, q=j, R=RAW[j]: e.dma_start(
                R.rearrange("p (c j) -> p c j", j=128), src[i, :, q * 16:(q + 1) * 16, :]),
                waits=raw_cons[mat][j], inc=qsem))
        ez1 = p.emit("scalar", lambda e, i=ic, o=off, z=szr: e.dma_start(
            ZSF[:, o:o + 128], z[:, i * 128:(i + 1) * 128]),
            waits=zs_cons[mat][pg], inc="sc")
        ez2 = p.emit("scalar", lambda e, i=ic, o=off, s=ssc: e.dma_start(
            ZSF[:, o + 128:o + 256], s[:, i * 128:(i + 1) * 128]),
            waits=zs_cons[mat][pg], inc="sc")
        ezz = p.emit("vector", lambda e, Z=ZR, o=off: e.scalar_tensor_tensor(
            Z, ZSF[:, o:o + 128], -8.0, ZSF[:, o + 128:o + 256], OP.add, OP.mult),
            waits=merge(ez1, ez2), inc="vq")
        zs_cons[mat][pg] = ezz
        ebc = p.emit("gpsimd", lambda e, i=ic, s=ssc, B=BCT: e.dma_start(
            B.rearrange("p (c j) -> p c j", j=128),
            s[0:32, i * 128:(i + 1) * 128].unsqueeze(0).broadcast_to([128, 32, 128])),
            waits=bc_cons[mat], inc="gw")
        evw = []
        for j in range(2):
            em = p.emit("vector", lambda e, q=j, R=RAW[j], B=BCT, W=WT: e.scalar_tensor_tensor(
                W[:, q * 2048:(q + 1) * 2048], R[:], -8.0,
                B[:, q * 2048:(q + 1) * 2048], OP.add, OP.mult),
                waits=merge(ers[j], ebc, w_cons2[mat][pg]), inc="vq")
            raw_cons[mat][j] = em
            evw.append(em)
        bc_cons[mat] = merge(*evw)
        return merge(*evw), ezz

    for ic in range(NIC):
        pg = ic % 2
        evg, ezg = gu_weights("g", ic, pg)
        evu, ezu = gu_weights("u", ic, pg)
        psg = PSROT[(2 * ic) % 3]
        psu = PSROT[(2 * ic + 1) % 3]
        wv = merge(evg, ps_cons[(2 * ic) % 3])
        for c in range(32):
            for nhf in range(2):
                mm(psg[:, nhf * 512:(nhf + 1) * 512], GW[pg][:, c * 128:(c + 1) * 128],
                   U[:, c * 1024 + nhf * 512:c * 1024 + (nhf + 1) * 512],
                   start=(c == 0), stop=False,
                   waits=wv if (c == 0 and nhf == 0) else ())
        for nhf in range(2):
            mm(psg[:, nhf * 512:(nhf + 1) * 512], ZROW["g"][pg],
               YG[0:32, nhf * 512:(nhf + 1) * 512], start=False, stop=True,
               waits=merge(ezg, ev_yg) if nhf == 0 else ())
        ev_gt = p.force_inc("tensor")
        w_cons2["g"][pg] = ev_gt
        zs_cons["g"][pg] = merge(zs_cons["g"][pg], ev_gt)
        wv = merge(evu, ps_cons[(2 * ic + 1) % 3])
        for c in range(32):
            for nhf in range(2):
                mm(psu[:, nhf * 512:(nhf + 1) * 512], UW[pg][:, c * 128:(c + 1) * 128],
                   U[:, c * 1024 + nhf * 512:c * 1024 + (nhf + 1) * 512],
                   start=(c == 0), stop=False,
                   waits=wv if (c == 0 and nhf == 0) else ())
        for nhf in range(2):
            mm(psu[:, nhf * 512:(nhf + 1) * 512], ZROW["u"][pg],
               YG[0:32, nhf * 512:(nhf + 1) * 512], start=False, stop=True,
               waits=ezu if nhf == 0 else ())
        ev_ut = p.force_inc("tensor")
        w_cons2["u"][pg] = ev_ut
        zs_cons["u"][pg] = merge(zs_cons["u"][pg], ev_ut)
        ev_sil = p.emit("scalar", lambda e, P=psg: e.activation(SILB, P[:], AF.Silu),
                        waits=merge(ev_gt, sil_cons), inc="cq")
        ev_h = p.emit("vector",
                      lambda e, P=psu, b=pg: e.tensor_tensor(HACTB[b], SILB, P[:], OP.mult),
                      waits=merge(ev_sil, ev_ut, hact_cons[pg]), inc="vq")
        sil_cons = ev_h
        ps_cons[(2 * ic) % 3] = ev_h
        ps_cons[(2 * ic + 1) % 3] = ev_h
        ehs = p.emit("scalar", lambda e, i=ic, b=pg: e.dma_start(
            hactD[:, i * 1024:(i + 1) * 1024], HACTB[b]),
            waits=ev_h, inc="sc")
        mm(PD[0:1, :], ONES[:], HACTB[pg][:, 0:512], start=True, stop=True,
           waits=merge(ev_h, pd_cons))
        mm(PD[32:33, :], ONES[:], HACTB[pg][:, 512:1024], start=True, stop=True)
        evt = p.force_inc("tensor")
        ea = p.emit("scalar", lambda e: e.activation(STGB[0:1, 0:512], PD[0:1, :], AF.Identity,
                                                     scale=-1.0),
                    waits=merge(evt, yh_done), inc="cq")
        eb = p.emit("scalar", lambda e: e.activation(STGB[0:1, 512:1024], PD[32:33, :], AF.Identity,
                                                     scale=-1.0),
                    waits=ea, inc="cq")
        pd_cons = eb
        yh_done = p.emit("scalar", lambda e, i=ic: e.dma_start(YH[i:i + 1, :], STGB[:]),
                         waits=eb, inc="sc")
        hact_cons[pg] = merge(ehs, evt)

    p.join()

    DW = [AR[:, 0:4096], AR[:, 4096:8192]]
    DBC = AR[:, 8192:12288]
    RAWD = [AR[:, 12288:16384].bitcast(I32), AR[:, 20480:24576].bitcast(I32),
            AR[:, 24576:28672].bitcast(I32), AR[:, 28672:32768].bitcast(I32)]
    Z2STG = [AR[:, 16384:17408], AR[:, 17408:18432]]
    Y1STG = [AR[:, 18432:19456], AR[:, 19456:20480]]
    DACC = [ARF[:, 0:1024], ARF[:, 1024:2048]]
    DACC2 = [ARF[:, 2048:3072], ARF[:, 3072:4096]]
    TMPF = ARF[:, 4096:5120]

    raw_cons_d = [{}, {}, {}, {}]
    rawslot = [0]
    bc_cons_d = {}
    dw_cons = [{}, {}]
    zsd_cons = [{}, {}]
    dacc_cons = [{}, {}]
    dacc2_cons = [{}, {}]
    z2_cons = [{}, {}]
    y1g_cons = [{}, {}]
    pp_cons = [{}, {}]
    slab_cons = {}
    stats2_ev = {}
    tmpf_cons = {}
    daccD_ev = [{} for _ in range(32)]

    for part in range(3):
        g0, nch = DPARTS[part]
        evs = []
        spl = [(i * nch) // 4 for i in range(5)]
        for s in range(4):
            l0, l1 = spl[s], spl[s + 1]
            evs.append(p.emit("sync",
                              lambda e, a=l0, b=l1, g=g0: e.dma_start(
                                  U[:, a * 1024:b * 1024],
                                  hactD[:, (g + a) * 1024:(g + b) * 1024]),
                              waits=slab_cons if s == 0 else (), inc="sd"))
        hact_loaded = merge(*evs)
        for hc in range(32):
            pp = hc % 2
            psd = PA if pp == 0 else PB
            zb = pp * 256
            evw = []
            ers = []
            prs = []
            for j in range(2):
                c0, c1 = j * 16, min((j + 1) * 16, nch)
                pr = rawslot[0] % 4
                rawslot[0] += 1
                prs.append(pr)
                ers.append(p.emit("sync", lambda e, c=hc, a=c0, b=c1, R=RAWD[pr], g=g0: e.dma_start(
                    R[:, 0:(b - a) * 128].rearrange("p (c j) -> p c j", j=128),
                    dr["ad"][c, :, g + a:g + b, :]),
                    waits=raw_cons_d[pr], inc="sd"))
            ez1 = p.emit("scalar", lambda e, g=g0, n=nch, c=hc, o=zb: e.dma_start(
                ZSDF[g:g + n, o:o + 128], dr["zdn"][g:g + n, c * 128:(c + 1) * 128]),
                waits=zsd_cons[pp], inc="sc")
            ez2 = p.emit("scalar", lambda e, g=g0, n=nch, c=hc, o=zb: e.dma_start(
                ZSDF[g:g + n, o + 128:o + 256], dr["sdn"][g:g + n, c * 128:(c + 1) * 128]),
                waits=zsd_cons[pp], inc="sc")
            ezz = p.emit("vector", lambda e, b=pp, g=g0, n=nch, o=zb: e.scalar_tensor_tensor(
                ZSD[g:g + n, b * 128:(b + 1) * 128], ZSDF[g:g + n, o:o + 128], -8.0,
                ZSDF[g:g + n, o + 128:o + 256], OP.add, OP.mult),
                waits=merge(ez1, ez2), inc="vq")
            zsd_cons[pp] = ezz
            ebc = p.emit("gpsimd", lambda e, g=g0, n=nch, c=hc: e.dma_start(
                DBC[:, 0:n * 128].rearrange("p (c j) -> p c j", j=128),
                dr["sdn"][g:g + n, c * 128:(c + 1) * 128].unsqueeze(0)
                .broadcast_to([128, n, 128])),
                waits=bc_cons_d, inc="gw")
            for j in range(2):
                c0, c1 = j * 16, min((j + 1) * 16, nch)
                pr = prs[j]
                em = p.emit("vector", lambda e, a=c0, b=c1, R=RAWD[pr], W=DW[pp]: e.scalar_tensor_tensor(
                    W[:, a * 128:b * 128], R[:, 0:(b - a) * 128], -8.0,
                    DBC[:, a * 128:b * 128], OP.add, OP.mult),
                    waits=merge(ers[j], ebc, dw_cons[pp]), inc="vq")
                raw_cons_d[pr] = em
                evw.append(em)
            bc_cons_d = merge(*evw)
            ev_dw = merge(*evw)
            wv = merge(ev_dw, hact_loaded, pp_cons[pp])
            for c in range(nch):
                for nhf in range(2):
                    mm(psd[:, nhf * 512:(nhf + 1) * 512], DW[pp][:, c * 128:(c + 1) * 128],
                       U[:, c * 1024 + nhf * 512:c * 1024 + (nhf + 1) * 512],
                       start=(c == 0), stop=False,
                       waits=wv if (c == 0 and nhf == 0) else ())
            for nhf in range(2):
                mm(psd[:, nhf * 512:(nhf + 1) * 512], ZSD[g0:g0 + nch, pp * 128:(pp + 1) * 128],
                   YH[g0:g0 + nch, nhf * 512:(nhf + 1) * 512], start=False, stop=True,
                   waits=merge(ezz, yh_done) if nhf == 0 else ())
            evt = p.force_inc("tensor")
            dw_cons[pp] = evt
            zsd_cons[pp] = merge(zsd_cons[pp], evt)
            if part == 0:
                ea = p.emit("vector", lambda e, b=pp, P=psd: e.tensor_copy(DACC[b], P[:]),
                            waits=merge(evt, dacc_cons[pp]), inc="vq")
                pp_cons[pp] = ea
                dacc_cons[pp] = p.emit("scalar",
                                       lambda e, b=pp, c=hc: e.dma_start(daccD[c], DACC[b]),
                                       waits=ea, inc="sc")
                daccD_ev[hc] = dacc_cons[pp]
            elif part == 1:
                er = p.emit("gpsimd", lambda e, b=pp, c=hc: e.dma_start(DACC2[b], daccD[c]),
                            waits=merge(dacc2_cons[pp], daccD_ev[hc]), inc="gw")
                ea = p.emit("vector",
                            lambda e, b=pp, P=psd: e.tensor_tensor(DACC[b], P[:], DACC2[b], OP.add),
                            waits=merge(evt, er, dacc_cons[pp]), inc="vq")
                pp_cons[pp] = ea
                dacc2_cons[pp] = ea
                dacc_cons[pp] = p.emit("scalar",
                                       lambda e, b=pp, c=hc: e.dma_start(daccD[c], DACC[b]),
                                       waits=ea, inc="sc")
                daccD_ev[hc] = dacc_cons[pp]
            else:
                er = p.emit("gpsimd", lambda e, b=pp, c=hc: e.dma_start(DACC2[b], daccD[c]),
                            waits=merge(dacc2_cons[pp], daccD_ev[hc]), inc="gw")
                ey = p.emit("gpsimd", lambda e, b=pp, c=hc: e.dma_start(Y1STG[b], y1D[c]),
                            waits=y1g_cons[pp], inc="gw")
                ea = p.emit("vector",
                            lambda e, b=pp, P=psd: e.tensor_tensor(TMPF, P[:], DACC2[b], OP.add),
                            waits=merge(evt, er, tmpf_cons), inc="vq")
                pp_cons[pp] = ea
                dacc2_cons[pp] = ea
                eb = p.emit("vector",
                            lambda e, b=pp: e.tensor_tensor(Z2STG[b], TMPF, Y1STG[b], OP.add),
                            waits=merge(ea, ey, z2_cons[pp]), inc="vq")
                tmpf_cons = eb
                y1g_cons[pp] = eb
                ez = p.emit("scalar", lambda e, b=pp, c=hc: e.dma_start(z2D[c], Z2STG[b]),
                            waits=eb, inc="sc")
                esq = p.emit("vector",
                             lambda e, b=pp: e.tensor_tensor(SQZ2[:], Z2STG[b], Z2STG[b], OP.mult),
                             waits=eb, inc="vq")
                st = hc == 0
                sp = hc == 31
                mm(PD[0:1, :], ONES[:], Z2STG[pp][:, 0:512], start=st, stop=sp,
                   waits=merge(eb, pd_cons) if st else eb)
                mm(PD[32:33, :], ONES[:], Z2STG[pp][:, 512:1024], start=st, stop=sp)
                mm(PD[64:65, :], ONES[:], SQZ2[:, 0:512], start=st, stop=sp,
                   waits=merge(esq, pe_cons) if st else esq)
                mm(PE_[0:1, :], ONES[:], SQZ2[:, 512:1024], start=st, stop=sp)
                stats2_ev = p.force_inc("tensor")
                z2_cons[pp] = merge(ez, stats2_ev)
        slab_cons = p.force_inc("tensor")

    ev_rows2, _ = ln_finalize(PE_[0:1, :])
    wv = merge(ev_rows2, pp_cons[0], pp_cons[1], pc_cons)
    for nhf in range(2):
        mm(PC[:, nhf * 512:(nhf + 1) * 512], ONESR[:],
           RSTDB[0:1, nhf * 512:(nhf + 1) * 512], start=True, stop=True,
           waits=wv if nhf == 0 else ())
        mm(PA[:, nhf * 512:(nhf + 1) * 512], ONESR[:],
           NMRB[0:1, nhf * 512:(nhf + 1) * 512], start=True, stop=True)
    ev_bc2 = p.force_inc("tensor")
    for hc in range(32):
        pp = hc % 2
        evr = p.emit("sync", lambda e, b=pp, c=hc: e.dma_start(Z2STG[b], z2D[c]),
                     waits=merge(z2_cons[pp], ev_bc2), inc="sd")
        ea = p.emit("vector",
                    lambda e, b=pp: e.tensor_tensor(TMPF, Z2STG[b], PC[:], OP.mult),
                    waits=merge(evr, tmpf_cons), inc="vq")
        tmpf_cons = ea
        eb = p.emit("vector",
                    lambda e, b=pp: e.tensor_tensor(DACC[b], TMPF, PA[:], OP.add),
                    waits=merge(ea, dacc_cons[pp]), inc="vq")
        tmpf_cons = eb
        z2_cons[pp] = merge(evr, eb)
        dacc_cons[pp] = p.emit("scalar",
                               lambda e, b=pp, c=hc: e.dma_start(
                                   outT[c * 128:(c + 1) * 128, :], DACC[b]),
                               waits=eb, inc="sc")

    p.emit("sync", None, waits=p.ev("sd"))
    p.emit("gpsimd", None, waits=p.ev("gw"))
    p.emit("scalar", None, waits=p.ev("sc"))

    p.realize()
    return nc


_CACHED = {}


def _host_stage(inputs):
    f32 = np.float32
    x = np.asarray(inputs["L0_attn_hidden_size"], f32)
    Wq_c = np.asarray(inputs["L0_attn_Wq_c"], f32)
    bq_c = np.asarray(inputs["L0_attn_bq_c"], f32)
    Wq_up = np.asarray(inputs["L0_attn_Wq_up"], f32)
    bq_up = np.asarray(inputs["L0_attn_bq_up"], f32)
    Wkv_c = np.asarray(inputs["L0_attn_Wkv_c"], f32)
    bkv_c = np.asarray(inputs["L0_attn_bkv_c"], f32)
    Wkv_up = np.asarray(inputs["L0_attn_Wkv_up"], f32)
    bkv_up = np.asarray(inputs["L0_attn_bkv_up"], f32)
    Wo = np.asarray(inputs["L0_attn_Wo"], f32)
    bo = np.asarray(inputs["L0_attn_bo"], f32)
    gq = np.asarray(inputs["L0_ffn_gate_qweight"], np.int32)
    gs = np.asarray(inputs["L0_ffn_gate_scales"], f32)
    gz = np.asarray(inputs["L0_ffn_gate_zeros"], f32)
    uq = np.asarray(inputs["L0_ffn_up_qweight"], np.int32)
    us = np.asarray(inputs["L0_ffn_up_scales"], f32)
    uz = np.asarray(inputs["L0_ffn_up_zeros"], f32)
    dq = np.asarray(inputs["L0_ffn_down_qweight"], np.int32)
    ds = np.asarray(inputs["L0_ffn_down_scales"], f32)
    dz = np.asarray(inputs["L0_ffn_down_zeros"], f32)

    C = np.ascontiguousarray
    sh = {}
    sh["aqc"] = C(Wq_c.reshape(12, 128, 32, 128).transpose(0, 3, 2, 1))
    sh["aqup"] = C(Wq_up.reshape(32, 192, 12, 128).transpose(0, 3, 2, 1))
    wkvc_p = np.zeros((640, 4096), f32)
    wkvc_p[0:576] = Wkv_c
    sh["akvc"] = C(wkvc_p.reshape(5, 128, 32, 128).transpose(0, 3, 2, 1))
    sh["akvup"] = C(Wkv_up.reshape(32, 256, 4, 128).transpose(0, 3, 2, 1))
    sh["awo"] = C(Wo.T)
    sh["bqc2"] = C(bq_c.reshape(12, 128).T)
    bquph = np.zeros((128, 64), f32)
    for h in range(32):
        bquph[:, 2 * h] = bq_up[h * 192:h * 192 + 128]
        bquph[0:64, 2 * h + 1] = bq_up[h * 192 + 128:(h + 1) * 192]
    sh["bquph"] = bquph
    bkvc_p = np.zeros(640, f32)
    bkvc_p[0:576] = bkv_c
    sh["bkvc2"] = C(bkvc_p.reshape(5, 128).T)
    bk = bkv_up.reshape(32, 2, 128)
    bkvuk = np.zeros((128, 64), f32)
    for h in range(32):
        bkvuk[:, 2 * h] = bk[h, 0]
    sh["bkvuk"] = bkvuk
    sh["bvrow"] = C(bk[:, 1, :].reshape(1, 4096))
    sh["bo2"] = C(bo.reshape(32, 128).T)
    sh["ag"] = C(gq.reshape(86, 128, 32, 128).transpose(0, 3, 2, 1))
    sh["au"] = C(uq.reshape(86, 128, 32, 128).transpose(0, 3, 2, 1))
    sh["ad"] = C(dq.reshape(32, 128, 86, 128).transpose(0, 3, 2, 1))
    sh["sg"] = C(gs.T)
    sh["zg"] = C(gz.T)
    sh["su"] = C(us.T)
    sh["zu"] = C(uz.T)
    sh["sdn"] = C(ds.T)
    sh["zdn"] = C(dz.T)

    in_maps = []
    for core in range(8):
        seq = core // 4
        tb = core % 4
        d = dict(sh)
        xt = C(x[seq].T)
        ax = xt.reshape(32, 128, 16, 256).transpose(2, 1, 0, 3)
        rot = np.roll(np.arange(16), OWN_QQ0 - tb * 4)
        d["ax"] = C(ax[rot])
        in_maps.append(d)
    return in_maps


def kernel(**inputs):
    if "nc" not in _CACHED:
        _CACHED["nc"] = build_program()
    nc = _CACHED["nc"]
    in_maps = _host_stage(inputs)
    res = run_bass_kernel_spmd(nc, in_maps, core_ids=list(range(8)))
    out = np.empty((2, 4096, 4096), np.float32)
    for core in range(8):
        seq = core // 4
        tb = core % 4
        out[seq, tb * 1024:(tb + 1) * 1024, :] = res.results[core]["outT"].T
    return out


# version 05
# speedup vs baseline: 1.0093x; 1.0093x over previous
import numpy as np
import concourse.bass as bass
import concourse.mybir as mybir
from concourse.bass_utils import run_bass_kernel_spmd
from concourse.alu_op_type import AluOpType as OP

F32 = mybir.dt.float32
BF16 = mybir.dt.bfloat16
I32 = mybir.dt.int32
AF = mybir.ActivationFunctionType
AX = mybir.AxisListType.X

ENGINES = ("tensor", "vector", "scalar", "sync", "gpsimd")
NH = 32
NIC = 86
ISQ = 0.07216878364870322
OWN_QQ0 = 12
DPARTS = [(0, 32), (32, 32), (64, 22)]
QK = 6
QUEUES = ("sd", "gw", "sc")


class Prog:
    def __init__(self, nc):
        self.nc = nc
        self.ops = {e: [] for e in ENGINES}
        self.sems = {}
        self.counts = {}
        names = ["tq", "vq", "cq"]
        for q in QUEUES:
            names += ["%s%d" % (q, i) for i in range(QK)]
        for name in names:
            self.sems[name] = nc.alloc_semaphore(name)
            self.counts[name] = 0
        self.esem = dict(tensor="tq", vector="vq", scalar="cq")
        self.rr = {q: 0 for q in QUEUES}

    def emit(self, eng, fn, waits=(), inc=None):
        waits = dict(waits)
        if inc in QUEUES:
            k = self.rr[inc]
            self.rr[inc] = (k + 1) % QK
            name = "%s%d" % (inc, k)
            waits[name] = max(waits.get(name, 0), self.counts[name])
            self.counts[name] += 16
            rec = (name, self.counts[name])
        elif inc is not None:
            self.counts[inc] += 1
            rec = (inc, self.counts[inc])
        else:
            rec = None
        self.ops[eng].append([fn, waits, rec])
        return {rec[0]: rec[1]} if rec else {}

    def ev(self, sem):
        if sem in QUEUES:
            return {"%s%d" % (sem, k): self.counts["%s%d" % (sem, k)]
                    for k in range(QK)
                    if self.counts["%s%d" % (sem, k)] > 0}
        return {sem: self.counts[sem]}

    def force_inc(self, eng):
        sem = self.esem[eng]
        for op in reversed(self.ops[eng]):
            if op[0] is None:
                continue
            if op[2] is None:
                self.counts[sem] += 1
                op[2] = (sem, self.counts[sem])
            return {op[2][0]: op[2][1]}
        return {}

    def join(self):
        evs = {}
        for e in ("tensor", "vector", "scalar"):
            evs.update(self.force_inc(e))
        evs = merge(evs, self.ev("sd"), self.ev("gw"), self.ev("sc"))
        for e in ENGINES:
            self.emit(e, None, waits=evs)
        return evs

    def realize(self):
        nc = self.nc
        with nc.Block() as block:
            for name in ENGINES:
                def body(eng, ops=self.ops[name], sems=self.sems):
                    for fn, waits, inc in ops:
                        for s, v in waits.items():
                            eng.wait_ge(sems[s], v)
                        if fn is None:
                            continue
                        ins = fn(eng)
                        if inc is not None:
                            ins.then_inc(sems[inc[0]],
                                         1 if inc[0] in ("tq", "vq", "cq") else 16)
                getattr(block, name)(body)


def merge(*evs):
    out = {}
    for e in evs:
        for k, v in e.items():
            out[k] = max(out.get(k, 0), v)
    return out


def build_program():
    nc = bass.Bass("TRN2", target_bir_lowering=False)

    dr = {}
    def din(name, shape, dt):
        dr[name] = nc.dram_tensor(name, shape, dt, kind="ExternalInput").ap()
    din("ax", [16, 128, 32, 256], F32)
    din("aqc", [12, 128, 32, 128], F32)
    din("akvc", [5, 128, 32, 128], F32)
    din("aqup", [32, 128, 12, 192], F32)
    din("akvup", [32, 128, 4, 256], F32)
    din("awo", [128, 4096], F32)
    din("bqc2", [128, 12], F32)
    din("bquph", [128, 64], F32)
    din("bkvc2", [128, 5], F32)
    din("bkvuk", [128, 64], F32)
    din("bvrow", [1, 4096], F32)
    din("bo2", [128, 32], F32)
    din("ag", [86, 128, 32, 128], I32)
    din("au", [86, 128, 32, 128], I32)
    din("ad", [32, 128, 86, 128], I32)
    din("sg", [32, 11008], F32)
    din("zg", [32, 11008], F32)
    din("su", [32, 11008], F32)
    din("zu", [32, 11008], F32)
    din("sdn", [86, 4096], F32)
    din("zdn", [86, 4096], F32)
    outT = nc.dram_tensor("outT", [4096, 1024], F32, kind="ExternalOutput").ap()

    hactD = nc.dram_tensor("hactD", [128, 88064], BF16).ap()
    z1D = nc.dram_tensor("z1D", [32, 128, 1024], BF16).ap()
    z2D = nc.dram_tensor("z2D", [32, 128, 1024], BF16).ap()
    y1D = nc.dram_tensor("y1D", [32, 128, 1024], BF16).ap()
    daccD = nc.dram_tensor("daccD", [32, 128, 1024], F32).ap()

    U = nc.alloc_sbuf_tensor("U", [128, 32768], BF16)
    AR = nc.alloc_sbuf_tensor("AR", [128, 32768], BF16)
    ARF = nc.alloc_sbuf_tensor("ARF", [128, 5120], F32)
    WOT = nc.alloc_sbuf_tensor("WOT", [128, 4096], BF16)
    OS = nc.alloc_sbuf_tensor("OS", [128, 1024], F32)
    RBUF = nc.alloc_sbuf_tensor("RBUF", [128, 1024], F32)
    RSTDB = nc.alloc_sbuf_tensor("RSTDB", [1, 1024], BF16)
    NMRB = nc.alloc_sbuf_tensor("NMRB", [1, 1024], BF16)
    RECB = nc.alloc_sbuf_tensor("RECB", [1, 1024], BF16)
    STGB = nc.alloc_sbuf_tensor("STGB", [1, 1024], BF16)
    RN = nc.alloc_sbuf_tensor("RN", [1, 4096], BF16)
    BVROW = nc.alloc_sbuf_tensor("BVROW", [1, 4096], BF16)
    YG = nc.alloc_sbuf_tensor("YG", [32, 1024], BF16)
    YH = nc.alloc_sbuf_tensor("YH", [86, 1024], BF16)
    BQC = nc.alloc_sbuf_tensor("BQC", [128, 12], F32)
    BQUP = nc.alloc_sbuf_tensor("BQUP", [128, 64], F32)
    BKVC = nc.alloc_sbuf_tensor("BKVC", [128, 5], F32)
    BKVUK = nc.alloc_sbuf_tensor("BKVUK", [128, 64], F32)
    BO = nc.alloc_sbuf_tensor("BO", [128, 32], F32)
    ONES = nc.alloc_sbuf_tensor("ONES", [128, 1], BF16)
    ONESR = nc.alloc_sbuf_tensor("ONESR", [1, 128], BF16)
    EPS1 = nc.alloc_sbuf_tensor("EPS1", [1, 1], F32)
    MISC = nc.alloc_sbuf_tensor("MISC", [1, 16], F32)
    MISCB = nc.alloc_sbuf_tensor("MISCB", [1, 16], BF16)
    NEGC = nc.alloc_sbuf_tensor("NEGC", [128, 2], F32)
    ZSG = nc.alloc_sbuf_tensor("ZSG", [32, 512], BF16)
    ZSF = nc.alloc_sbuf_tensor("ZSF", [32, 1024], F32)
    ZSD = nc.alloc_sbuf_tensor("ZSD", [86, 256], BF16)
    ZSDF = nc.alloc_sbuf_tensor("ZSDF", [86, 512], F32)
    SQZ2 = nc.alloc_sbuf_tensor("SQZ2", [128, 1024], BF16)

    MROW = ARF[0:1, 0:1024]
    MSQ = ARF[0:1, 1024:2048]
    TROW = ARF[0:1, 2048:3072]

    PA = nc.alloc_psum_tensor("PA", [128, 1024], F32)
    PB = nc.alloc_psum_tensor("PB", [128, 1024], F32)
    PC = nc.alloc_psum_tensor("PC", [128, 1024], F32)
    PD = nc.alloc_psum_tensor("PD", [128, 512], F32)
    PE_ = nc.alloc_psum_tensor("PE", [33, 512], F32)

    p = Prog(nc)

    def mm(ps, lhs, rhs, start, stop, waits=()):
        p.emit("tensor",
               lambda e, a=ps, b=lhs, c=rhs, s1=start, s2=stop:
               e.matmul(a, b, c, start=s1, stop=s2), waits=waits)

    p.emit("vector", lambda e: e.memset(ONES[:], 1.0))
    p.emit("vector", lambda e: e.memset(ONESR[:], 1.0))
    p.emit("vector", lambda e: e.memset(EPS1[:], 1e-5))
    for nm, tgt in (("bqc2", BQC), ("bquph", BQUP), ("bkvc2", BKVC),
                    ("bkvuk", BKVUK), ("bo2", BO)):
        p.emit("sync", lambda e, a=tgt, b=dr[nm]: e.dma_start(a[:], b), inc="sd")
    ev_bias = p.ev("sd")
    ev_bq = p.emit("vector",
                   lambda e: e.tensor_scalar(BQUP[:], BQUP[:], ISQ, None, OP.mult),
                   waits=ev_bias, inc="vq")
    p.emit("gpsimd", lambda e: e.dma_start(BVROW[:], dr["bvrow"]), inc="gw")
    ev_wo = p.emit("gpsimd", lambda e: e.dma_start(WOT[:], dr["awo"]), inc="gw")

    KVC = lambda m: AR[:, m * 4096:(m + 1) * 4096]
    KVCR = AR[0:64, 16384:20480]
    QCC = lambda c: AR[:, 20480 + c * 1024: 20480 + (c + 1) * 1024]
    XQ = [U[:, 0:8192], U[:, 8192:16384]]
    P1W = [U[:, 16384:20480], U[:, 20480:24576]]

    xq_ev = [{}, {}]
    xq_cons = [{}, {}]
    w_cons = {}
    psl_cons = {}
    wslot = 0
    pslot = 0
    for qq in range(16):
        bx = qq % 2
        xq_ev[bx] = p.emit("gpsimd",
                           lambda e, b=bx, q=qq: e.dma_start(
                               XQ[b].rearrange("p (c t) -> p c t", t=256), dr["ax"][q]),
                           waits=xq_cons[bx], inc="gw")
        own = qq >= OWN_QQ0
        mlist = list(range(5)) + (list(range(100, 112)) if own else [])
        for m in mlist:
            isq = m >= 100
            ws = wslot % 2
            wslot += 1
            src = dr["aqc"][m - 100] if isq else dr["akvc"][m]
            wt = P1W[ws]
            w_ev = p.emit("gpsimd",
                          lambda e, a=wt, b=src: e.dma_start(
                              a.rearrange("p (c j) -> p c j", j=128), b),
                          waits=w_cons.get(ws, {}), inc="gw")
            sl = pslot % 4
            pslot += 1
            ps = [PA[:, 0:512], PA[:, 512:1024], PB[:, 0:512], PB[:, 512:1024]][sl]
            wv = merge(w_ev, xq_ev[bx], psl_cons.get(sl, {}))
            for c in range(32):
                mm(ps[:, 0:256], wt[:, c * 128:(c + 1) * 128],
                   XQ[bx][:, c * 256:(c + 1) * 256],
                   start=(c == 0), stop=(c == 31), waits=wv if c == 0 else ())
            ev_t = p.force_inc("tensor")
            w_cons[ws] = ev_t
            xq_cons[bx] = merge(xq_cons[bx], ev_t)
            if isq:
                c = m - 100
                dst = QCC(c)[:, (qq - OWN_QQ0) * 256:(qq - OWN_QQ0 + 1) * 256]
                fn = (lambda e, a=dst, b=ps, bb=BQC[:, c:c + 1]:
                      e.activation(a, b[:, 0:256], AF.Identity, bias=bb))
            elif m < 4:
                dst = KVC(m)[:, qq * 256:(qq + 1) * 256]
                fn = (lambda e, a=dst, b=ps, bb=BKVC[:, m:m + 1]:
                      e.activation(a, b[:, 0:256], AF.Identity, bias=bb))
            else:
                dst = KVCR[:, qq * 256:(qq + 1) * 256]
                fn = (lambda e, a=dst, b=ps, bb=BKVC[0:64, 4:5]:
                      e.activation(a, b[0:64, 0:256], AF.Identity, bias=bb))
            psl_cons[sl] = p.emit("scalar", fn, waits=merge(ev_t, ev_bias), inc="cq")

    p.join()

    SQ = U[:, 22528:23552]
    rn_done = {}
    for sl4 in range(4):
        evq = p.emit("vector",
                     lambda e, s=sl4: e.tensor_tensor(
                         SQ[0:64, :], KVCR[:, s * 1024:(s + 1) * 1024],
                         KVCR[:, s * 1024:(s + 1) * 1024], OP.mult),
                     waits=rn_done, inc="vq")
        for nh2 in range(2):
            mm(PD[0:1, :], ONES[0:64, :], SQ[0:64, nh2 * 512:(nh2 + 1) * 512],
               start=True, stop=True, waits=merge(evq, rn_done))
            evt = p.force_inc("tensor")
            rn_done = p.emit("vector",
                             lambda e, s=sl4, n=nh2: e.tensor_copy(
                                 RN[0:1, s * 1024 + n * 512:s * 1024 + (n + 1) * 512],
                                 PD[0:1, :]),
                             waits=evt, inc="vq")
    p.join()

    VT = [U[:, 0:4096], U[:, 4096:8192]]
    KH = [U[:, 8192:12288], U[:, 12288:16384]]
    QH = [U[:, 16384:17408], U[:, 17408:18432]]
    QR = [U[0:64, 18432:19456], U[0:64, 19456:20480]]
    ET = [U[:, 20480:21504], U[:, 21504:22528]]
    QUPW = [U[:, 23552:25856], U[:, 25856:28160]]
    KVUPW = [U[:, 28160:29184], U[:, 29184:30208]]
    TMPB = U[:, 30208:31232]

    qupw_ev = [{}, {}]
    kvupw_ev = [{}, {}]
    qupw_cons = [{}, {}]
    kvupw_cons = [{}, {}]
    build_cons = [{}, {}]
    pd_cons = {}
    pa_cons = [{}, {}]
    pc_cons = {}
    pe_cons = {}
    et_cons = [{}, {}]
    rbuf_cons = {}
    os_ev = {}

    for hh in range(min(2, NH)):
        ph = hh % 2
        qupw_ev[ph] = p.emit("gpsimd",
                             lambda e, b=ph, h=hh: e.dma_start(
                                 QUPW[b].rearrange("p (c j) -> p c j", j=192),
                                 dr["aqup"][h]), inc="gw")
        kvupw_ev[ph] = p.emit("gpsimd",
                              lambda e, b=ph, h=hh: e.dma_start(
                                  KVUPW[b].rearrange("p (c j) -> p c j", j=256),
                                  dr["akvup"][h]), inc="gw")

    for h in range(NH):
        ph = h % 2
        for ks in range(8):
            wv = merge(kvupw_ev[ph], pd_cons, build_cons[ph] if ks == 0 else {})
            for c in range(4):
                mm(PD[:], KVUPW[ph][:, c * 256:c * 256 + 128],
                   KVC(c)[:, ks * 512:(ks + 1) * 512],
                   start=(c == 0), stop=(c == 3), waits=wv if c == 0 else ())
            evt = p.force_inc("tensor")
            pd_cons = p.emit("scalar",
                             lambda e, s=ks, b=ph, hh=h: e.activation(
                                 KH[b][:, s * 512:(s + 1) * 512], PD[:], AF.Identity,
                                 bias=BKVUK[:, 2 * hh:2 * hh + 1]),
                             waits=evt, inc="cq")
        ev_kh = pd_cons
        for vu in range(8):
            wv = merge(kvupw_ev[ph], pd_cons)
            for tc4 in range(4):
                tc = vu * 4 + tc4
                dst = PD[:, tc4 * 128:(tc4 + 1) * 128]
                for c in range(4):
                    mm(dst, AR[:, c * 4096 + tc * 128:c * 4096 + (tc + 1) * 128],
                       KVUPW[ph][:, c * 256 + 128:(c + 1) * 256],
                       start=(tc4 == 0 and c == 0), stop=False,
                       waits=wv if (tc4 == 0 and c == 0) else ())
                mm(dst, ONESR[:], BVROW[0:1, h * 128:(h + 1) * 128],
                   start=False, stop=(tc4 == 3))
            evt = p.force_inc("tensor")
            pd_cons = p.emit("scalar",
                             lambda e, b=ph, u=vu: e.activation(
                                 VT[b][:, u * 512:(u + 1) * 512], PD[:], AF.Identity),
                             waits=evt, inc="cq")
        ev_vt = pd_cons
        kvupw_cons[ph] = p.force_inc("tensor")
        for nhf in range(2):
            wv = merge(qupw_ev[ph], pd_cons)
            for c in range(12):
                mm(PD[:], QUPW[ph][:, c * 192:c * 192 + 128],
                   QCC(c)[:, nhf * 512:(nhf + 1) * 512],
                   start=(c == 0), stop=(c == 11), waits=wv if c == 0 else ())
            evt = p.force_inc("tensor")
            pd_cons = p.emit("scalar",
                             lambda e, b=ph, n=nhf, hh=h: e.activation(
                                 QH[b][:, n * 512:(n + 1) * 512], PD[:], AF.Identity,
                                 bias=BQUP[:, 2 * hh:2 * hh + 1], scale=ISQ),
                             waits=merge(evt, ev_bq), inc="cq")
            wv = merge(qupw_ev[ph], pd_cons)
            for c in range(12):
                mm(PD[0:64, :], QUPW[ph][:, c * 192 + 128:c * 192 + 192][0:64, :]
                   if False else QUPW[ph][:, c * 192 + 128:c * 192 + 192],
                   QCC(c)[:, nhf * 512:(nhf + 1) * 512],
                   start=(c == 0), stop=(c == 11), waits=wv if c == 0 else ())
            evt = p.force_inc("tensor")
            pd_cons = p.emit("scalar",
                             lambda e, b=ph, n=nhf, hh=h: e.activation(
                                 QR[b][:, n * 512:(n + 1) * 512], PD[0:64, :], AF.Identity,
                                 bias=BQUP[0:64, 2 * hh + 1:2 * hh + 2], scale=ISQ),
                             waits=merge(evt, ev_bq), inc="cq")
        ev_qh = pd_cons
        qupw_cons[ph] = p.force_inc("tensor")

        if h + 2 < NH:
            qupw_ev[ph] = p.emit("gpsimd",
                                 lambda e, b=ph, hh=h + 2: e.dma_start(
                                     QUPW[b].rearrange("p (c j) -> p c j", j=192),
                                     dr["aqup"][hh]),
                                 waits=qupw_cons[ph], inc="gw")
            kvupw_ev[ph] = p.emit("gpsimd",
                                  lambda e, b=ph, hh=h + 2: e.dma_start(
                                      KVUPW[b].rearrange("p (c j) -> p c j", j=256),
                                      dr["akvup"][hh]),
                                  waits=kvupw_cons[ph], inc="gw")

        ev1 = p.emit("vector",
                     lambda e, b=ph: e.tensor_tensor(SQ[:], QH[b][:], QH[b][:], OP.mult),
                     waits=merge(ev_qh, pe_cons), inc="vq")
        mm(PD[0:1, :], ONES[:], SQ[:, 0:512], start=True, stop=False,
           waits=merge(ev1, pd_cons))
        mm(PD[32:33, :], ONES[:], SQ[:, 512:1024], start=True, stop=False)
        evt = p.force_inc("tensor")
        ev2 = p.emit("vector",
                     lambda e, b=ph: e.tensor_tensor(SQ[0:64, :], QR[b][:], QR[b][:], OP.mult),
                     waits=merge(evt, ev_qh), inc="vq")
        mm(PD[0:1, :], ONES[0:64, :], SQ[0:64, 0:512], start=False, stop=True, waits=ev2)
        mm(PD[32:33, :], ONES[0:64, :], SQ[0:64, 512:1024], start=False, stop=True)
        evt = p.force_inc("tensor")
        ev3 = p.emit("vector", lambda e: e.reduce_max(MISC[0:1, 0:1], PD[0:1, :], axis=AX),
                     waits=evt, inc="vq")
        ev4 = p.emit("vector", lambda e: e.reduce_max(MISC[0:1, 1:2], PD[32:33, :], axis=AX),
                     waits=evt, inc="vq")
        qn_done = p.emit("vector",
                         lambda e: e.tensor_tensor(MISC[0:1, 0:1], MISC[0:1, 0:1],
                                                   MISC[0:1, 1:2], OP.max),
                         waits=merge(ev3, ev4), inc="vq")
        pd_cons = qn_done
        p.emit("vector", lambda e: e.memset(MISC[0:1, 2:3], 0.0))
        ev_kn = {}
        for sl4 in range(4):
            ev5 = p.emit("vector",
                         lambda e, b=ph, s=sl4: e.tensor_tensor(
                             SQ[:], KH[b][:, s * 1024:(s + 1) * 1024],
                             KH[b][:, s * 1024:(s + 1) * 1024], OP.mult),
                         waits=merge(ev_kh, qn_done), inc="vq")
            mm(PD[0:1, :], ONES[:], SQ[:, 0:512], start=True, stop=True,
               waits=merge(ev5, pd_cons))
            mm(PD[32:33, :], ONES[:], SQ[:, 512:1024], start=True, stop=True)
            evt = p.force_inc("tensor")
            e6 = p.emit("vector",
                        lambda e, s=sl4: e.tensor_tensor(
                            TROW[0:1, 0:512], PD[0:1, :],
                            RN[0:1, s * 1024:s * 1024 + 512], OP.add),
                        waits=evt, inc="vq")
            e7 = p.emit("vector",
                        lambda e, s=sl4: e.tensor_tensor(
                            TROW[0:1, 512:1024], PD[32:33, :],
                            RN[0:1, s * 1024 + 512:(s + 1) * 1024], OP.add),
                        waits=evt, inc="vq")
            pd_cons = merge(e6, e7)
            e8 = p.emit("vector",
                        lambda e: e.reduce_max(MISC[0:1, 3:4], TROW[0:1, :], axis=AX),
                        waits=merge(e6, e7), inc="vq")
            ev_kn = p.emit("vector",
                           lambda e: e.tensor_tensor(MISC[0:1, 2:3], MISC[0:1, 2:3],
                                                     MISC[0:1, 3:4], OP.max),
                           waits=e8, inc="vq")
        ev9 = p.emit("vector",
                     lambda e: e.tensor_tensor(MISC[0:1, 4:5], MISC[0:1, 0:1],
                                               MISC[0:1, 2:3], OP.mult),
                     waits=ev_kn, inc="vq")
        ev10 = p.emit("scalar",
                      lambda e: e.activation(MISC[0:1, 5:6], MISC[0:1, 4:5], AF.Ln,
                                             bias=EPS1[:]),
                      waits=ev9, inc="cq")
        ev11 = p.emit("scalar",
                      lambda e: e.activation(MISC[0:1, 6:7], MISC[0:1, 5:6], AF.Exp,
                                             scale=0.5),
                      waits=ev10, inc="cq")
        ev12 = p.emit("vector",
                      lambda e: e.tensor_scalar(MISCB[0:1, 0:1], MISC[0:1, 6:7],
                                                -1.0, None, OP.mult),
                      waits=ev11, inc="vq")
        mm(PD[:, 0:1], ONESR[:], MISCB[0:1, 0:1], start=True, stop=True,
           waits=merge(ev12, pd_cons))
        evt = p.force_inc("tensor")
        ev_negc = p.emit("scalar",
                         lambda e, b=ph: e.activation(NEGC[:, b:b + 1], PD[:, 0:1],
                                                      AF.Identity),
                         waits=evt, inc="cq")
        pd_cons = ev_negc

        exp_ev = {}
        scores_ev = {}
        for kt in range(33):
            ep = kt % 2
            if kt < 32:
                psc = PA if ep == 0 else PB
                wv = merge(ev_kh, ev_qh, ev_negc, pa_cons[ep],
                           build_cons[ph] if kt == 0 else {})
                for nhf in range(2):
                    sl = psc[:, nhf * 512:(nhf + 1) * 512]
                    mm(sl, KH[ph][:, kt * 128:(kt + 1) * 128],
                       QH[ph][:, nhf * 512:(nhf + 1) * 512],
                       start=True, stop=False, waits=wv if nhf == 0 else ())
                    mm(sl, KVCR[:, kt * 128:(kt + 1) * 128],
                       QR[ph][:, nhf * 512:(nhf + 1) * 512],
                       start=False, stop=True)
                scores_ev = p.force_inc("tensor")
            if kt >= 1:
                kp = (kt - 1) % 2
                wz = merge(exp_ev, pe_cons if kt == 1 else {})
                mm(PE_[0:1, :], ONES[:], ET[kp][:, 0:512],
                   start=(kt == 1), stop=(kt == 32), waits=wz)
                mm(PE_[32:33, :], ONES[:], ET[kp][:, 512:1024],
                   start=(kt == 1), stop=(kt == 32))
                for nhf in range(2):
                    mm(PC[:, nhf * 512:(nhf + 1) * 512],
                       VT[ph][:, (kt - 1) * 128:kt * 128],
                       ET[kp][:, nhf * 512:(nhf + 1) * 512],
                       start=(kt == 1), stop=(kt == 32),
                       waits=merge(ev_vt, pc_cons) if (kt == 1 and nhf == 0) else ())
                et_cons[kp] = p.force_inc("tensor")
            if kt < 32:
                exp_ev = p.emit("scalar",
                                lambda e, b=ph, ee=ep: e.activation(
                                    ET[ee][:], (PA if ee == 0 else PB)[:], AF.Exp,
                                    bias=NEGC[:, b:b + 1]),
                                waits=merge(scores_ev, et_cons[ep]), inc="cq")
                pa_cons[ep] = exp_ev
        build_cons[ph] = p.force_inc("tensor")

        ez = build_cons[ph]
        r1 = p.emit("vector", lambda e: e.reciprocal(TROW[0:1, 0:512], PE_[0:1, :]),
                    waits=ez, inc="vq")
        r2 = p.emit("vector", lambda e: e.reciprocal(TROW[0:1, 512:1024], PE_[32:33, :]),
                    waits=ez, inc="vq")
        pe_cons = merge(r1, r2)
        r3 = p.emit("vector", lambda e: e.tensor_copy(RECB[:], TROW[:]),
                    waits=pe_cons, inc="vq")
        for nhf in range(2):
            mm(PD[:], ONESR[:], RECB[0:1, nhf * 512:(nhf + 1) * 512],
               start=True, stop=True,
               waits=merge(r3, pd_cons))
            evt = p.force_inc("tensor")
            pd_cons = p.emit("scalar",
                             lambda e, n=nhf: e.activation(
                                 RBUF[:, n * 512:(n + 1) * 512], PD[:], AF.Identity),
                             waits=merge(evt, rbuf_cons), inc="cq")
        if h == 0:
            ev13 = p.emit("vector",
                          lambda e: e.tensor_tensor(OS[:], PC[:], RBUF[:], OP.mult),
                          waits=merge(pd_cons, ez), inc="vq")
            pc_cons = ev13
            rbuf_cons = ev13
            os_ev = ev13
        else:
            ev13 = p.emit("vector",
                          lambda e: e.tensor_tensor(TMPB, PC[:], RBUF[:], OP.mult),
                          waits=merge(pd_cons, ez), inc="vq")
            pc_cons = ev13
            rbuf_cons = ev13
            os_ev = p.emit("vector",
                           lambda e: e.tensor_tensor(OS[:], OS[:], TMPB, OP.add),
                           waits=ev13, inc="vq")

    p.join()

    OSBF = AR[:, 0:1024]
    Z1STG = [AR[:, 1024:2048], AR[:, 2048:3072]]
    SQZ1 = AR[:, 3072:4096]
    TMPB2 = AR[:, 4096:5120]
    XF = [ARF[:, 0:1024], ARF[:, 1024:2048]]

    ev_osbf = p.emit("vector", lambda e: e.tensor_copy(OSBF, OS[:]),
                     waits=os_ev, inc="vq")
    xf_cons = [{}, {}]
    z1_cons = [{}, {}]
    pa2_cons = [{}, {}]
    for hc in range(32):
        pp = hc % 2
        evs = []
        for k4 in range(4):
            evs.append(p.emit("sync",
                              lambda e, b=pp, q=OWN_QQ0 + k4, c=hc, kk=k4: e.dma_start(
                                  XF[b][:, kk * 256:(kk + 1) * 256], dr["ax"][q, :, c, :]),
                              waits=xf_cons[pp] if k4 == 0 else (), inc="sd"))
        xf_ev = merge(*evs)
        psc = PA if pp == 0 else PB
        wv = merge(ev_osbf, ev_wo, pa2_cons[pp])
        for nhf in range(2):
            mm(psc[:, nhf * 512:(nhf + 1) * 512], WOT[:, hc * 128:(hc + 1) * 128],
               OSBF[:, nhf * 512:(nhf + 1) * 512], start=True, stop=True,
               waits=wv if nhf == 0 else ())
        evt = p.force_inc("tensor")
        ev_z1 = p.emit("vector",
                       lambda e, b=pp, c=hc: e.scalar_tensor_tensor(
                           Z1STG[b], (PA if b == 0 else PB)[:], BO[:, c:c + 1], XF[b],
                           OP.add, OP.add),
                       waits=merge(evt, xf_ev, z1_cons[pp]), inc="vq")
        xf_cons[pp] = ev_z1
        pa2_cons[pp] = ev_z1
        esp = p.emit("scalar", lambda e, b=pp, c=hc: e.dma_start(z1D[c], Z1STG[b]),
                     waits=ev_z1, inc="sc")
        ev_sq = p.emit("vector",
                       lambda e, b=pp: e.tensor_tensor(SQZ1, Z1STG[b], Z1STG[b], OP.mult),
                       waits=ev_z1, inc="vq")
        st = hc == 0
        sp = hc == 31
        mm(PD[0:1, :], ONES[:], Z1STG[pp][:, 0:512], start=st, stop=sp,
           waits=merge(ev_z1, pd_cons) if st else ev_z1)
        mm(PD[32:33, :], ONES[:], Z1STG[pp][:, 512:1024], start=st, stop=sp)
        mm(PD[64:65, :], ONES[:], SQZ1[:, 0:512], start=st, stop=sp,
           waits=merge(ev_sq, pe_cons) if st else ev_sq)
        mm(PE_[0:1, :], ONES[:], SQZ1[:, 512:1024], start=st, stop=sp)
        z1_cons[pp] = merge(esp, p.force_inc("tensor"))

    def ln_finalize(pm_sq_hi):
        evt = p.force_inc("tensor")
        e1 = p.emit("vector", lambda e: e.tensor_scalar(
            MROW[0:1, 0:512], PD[0:1, :], 1.0 / 4096, None, OP.mult), waits=evt, inc="vq")
        e2 = p.emit("vector", lambda e: e.tensor_scalar(
            MROW[0:1, 512:1024], PD[32:33, :], 1.0 / 4096, None, OP.mult), waits=evt, inc="vq")
        e3 = p.emit("vector", lambda e: e.tensor_scalar(
            MSQ[0:1, 0:512], PD[64:65, :], 1.0 / 4096, None, OP.mult), waits=evt, inc="vq")
        e4 = p.emit("vector", lambda e: e.tensor_scalar(
            MSQ[0:1, 512:1024], pm_sq_hi, 1.0 / 4096, None, OP.mult), waits=evt, inc="vq")
        e5 = p.emit("vector", lambda e: e.tensor_tensor(TROW[:], MROW[:], MROW[:], OP.mult),
                    waits=merge(e1, e2), inc="vq")
        e6 = p.emit("vector", lambda e: e.tensor_tensor(TROW[:], MSQ[:], TROW[:], OP.subtract),
                    waits=merge(e3, e4, e5), inc="vq")
        e7 = p.emit("scalar", lambda e: e.activation(TROW[:], TROW[:], AF.Ln, bias=EPS1[:]),
                    waits=e6, inc="cq")
        e8 = p.emit("scalar", lambda e: e.activation(TROW[:], TROW[:], AF.Exp, scale=-0.5),
                    waits=e7, inc="cq")
        e9 = p.emit("vector", lambda e: e.tensor_copy(RSTDB[:], TROW[:]), waits=e8, inc="vq")
        e10 = p.emit("vector", lambda e: e.tensor_tensor(MROW[:], MROW[:], TROW[:], OP.mult),
                     waits=e8, inc="vq")
        e11 = p.emit("vector", lambda e: e.tensor_scalar(NMRB[:], MROW[:], -1.0, None, OP.mult),
                     waits=e10, inc="vq")
        return merge(e9, e11), p.ev("tq")

    ev_rows, _ = ln_finalize(PE_[0:1, :])
    wv = merge(ev_rows, pa2_cons[0], pa2_cons[1])
    for nhf in range(2):
        mm(PA[:, nhf * 512:(nhf + 1) * 512], ONESR[:],
           RSTDB[0:1, nhf * 512:(nhf + 1) * 512], start=True, stop=True,
           waits=wv if nhf == 0 else ())
        mm(PB[:, nhf * 512:(nhf + 1) * 512], ONESR[:],
           NMRB[0:1, nhf * 512:(nhf + 1) * 512], start=True, stop=True)
    ev_bc = p.force_inc("tensor")
    y1_last = {}
    tmpb2_cons = {}
    for hc in range(32):
        pp = hc % 2
        evr = p.emit("sync", lambda e, b=pp, c=hc: e.dma_start(Z1STG[b], z1D[c]),
                     waits=merge(z1_cons[pp], ev_bc), inc="sd")
        ea = p.emit("vector",
                    lambda e, b=pp: e.tensor_tensor(TMPB2, Z1STG[b], PA[:], OP.mult),
                    waits=merge(evr, tmpb2_cons), inc="vq")
        eb = p.emit("vector",
                    lambda e, c=hc: e.tensor_tensor(U[:, c * 1024:(c + 1) * 1024],
                                                    TMPB2, PB[:], OP.add),
                    waits=ea, inc="vq")
        tmpb2_cons = eb
        z1_cons[pp] = merge(evr, eb)
        y1_last = p.emit("scalar",
                         lambda e, c=hc: e.dma_start(y1D[c], U[:, c * 1024:(c + 1) * 1024]),
                         waits=eb, inc="sc")

    p.join()

    RAWG = [AR[:, 0:2048].bitcast(I32), AR[:, 2048:4096].bitcast(I32),
            AR[:, 4096:6144].bitcast(I32), AR[:, 6144:8192].bitcast(I32)]
    RAWU = [WOT[:, 0:2048].bitcast(I32), WOT[:, 2048:4096].bitcast(I32),
            ARF[:, 0:1024].bitcast(I32), ARF[:, 1024:2048].bitcast(I32)]
    GBC = AR[:, 8192:12288]
    UBC = AR[:, 12288:16384]
    GW = [AR[:, 16384:20480], AR[:, 20480:24576]]
    UW = [AR[:, 24576:28672], AR[:, 28672:32768]]
    SILB = ARF[:, 2048:2560].bitcast(BF16)
    HACTB = [ARF[:, 2560:3072].bitcast(BF16), ARF[:, 3072:3584].bitcast(BF16)]
    ZROW = {"g": [ZSG[:, 0:128], ZSG[:, 128:256]],
            "u": [ZSG[:, 256:384], ZSG[:, 384:512]]}
    ZOFF = {"g": [0, 256], "u": [512, 768]}

    ev_yg = {}
    for c in range(32):
        mm(PD[0:1, :], ONES[:], U[:, c * 1024:c * 1024 + 512], start=True, stop=True,
           waits=merge(pd_cons, pe_cons) if c == 0 else pd_cons)
        mm(PD[32:33, :], ONES[:], U[:, c * 1024 + 512:(c + 1) * 1024],
           start=True, stop=True)
        evt = p.force_inc("tensor")
        ea = p.emit("scalar", lambda e: e.activation(STGB[0:1, 0:512], PD[0:1, :], AF.Identity,
                                                     scale=-1.0),
                    waits=merge(evt, ev_yg), inc="cq")
        eb = p.emit("scalar", lambda e: e.activation(STGB[0:1, 512:1024], PD[32:33, :], AF.Identity,
                                                     scale=-1.0),
                    waits=ea, inc="cq")
        pd_cons = eb
        ev_yg = p.emit("scalar", lambda e, cc=c: e.dma_start(YG[cc:cc + 1, :], STGB[:]),
                       waits=eb, inc="sc")

    PSROT = [PA, PB, PC]
    ps_cons = [{}, {}, pc_cons]
    raw_cons = {"g": [{}, {}, {}, {}], "u": [{}, {}, {}, {}]}
    bc_cons = {"g": {}, "u": {}}
    w_cons2 = {"g": [{}, {}], "u": [{}, {}]}
    zs_cons = {"g": [{}, {}], "u": [{}, {}]}
    hact_cons = [{}, {}]
    yh_done = {}
    sil_cons = {}
    stg_cons = pd_cons

    def gu_weights(mat, ic, pg):
        src = dr["ag" if mat == "g" else "au"]
        ssc = dr["sg" if mat == "g" else "su"]
        szr = dr["zg" if mat == "g" else "zu"]
        RAW = RAWG if mat == "g" else RAWU
        BCT = GBC if mat == "g" else UBC
        WT = (GW if mat == "g" else UW)[pg]
        ZR = ZROW[mat][pg]
        off = ZOFF[mat][pg]
        qeng, qsem = ("sync", "sd") if mat == "g" else ("gpsimd", "gw")
        ers = []
        for j in range(4):
            ers.append(p.emit(qeng, lambda e, i=ic, q=j, R=RAW[j]: e.dma_start(
                R.rearrange("p (c j) -> p c j", j=128), src[i, :, q * 8:(q + 1) * 8, :]),
                waits=raw_cons[mat][j], inc=qsem))
        ez1 = p.emit("scalar", lambda e, i=ic, o=off, z=szr: e.dma_start(
            ZSF[:, o:o + 128], z[:, i * 128:(i + 1) * 128]),
            waits=zs_cons[mat][pg], inc="sc")
        ez2 = p.emit("scalar", lambda e, i=ic, o=off, s=ssc: e.dma_start(
            ZSF[:, o + 128:o + 256], s[:, i * 128:(i + 1) * 128]),
            waits=zs_cons[mat][pg], inc="sc")
        ezz = p.emit("vector", lambda e, Z=ZR, o=off: e.scalar_tensor_tensor(
            Z, ZSF[:, o:o + 128], -8.0, ZSF[:, o + 128:o + 256], OP.add, OP.mult),
            waits=merge(ez1, ez2), inc="vq")
        zs_cons[mat][pg] = ezz
        ebc = p.emit("gpsimd", lambda e, i=ic, s=ssc, B=BCT: e.dma_start(
            B.rearrange("p (c j) -> p c j", j=128),
            s[0:32, i * 128:(i + 1) * 128].unsqueeze(0).broadcast_to([128, 32, 128])),
            waits=bc_cons[mat], inc="gw")
        evw = []
        for j in range(4):
            em = p.emit("vector", lambda e, q=j, R=RAW[j], B=BCT, W=WT: e.scalar_tensor_tensor(
                W[:, q * 1024:(q + 1) * 1024], R[:], -8.0,
                B[:, q * 1024:(q + 1) * 1024], OP.add, OP.mult),
                waits=merge(ers[j], ebc, w_cons2[mat][pg]), inc="vq")
            raw_cons[mat][j] = em
            evw.append(em)
        bc_cons[mat] = merge(*evw)
        return merge(*evw), ezz

    for ic in range(NIC):
        pg = ic % 2
        evg, ezg = gu_weights("g", ic, pg)
        evu, ezu = gu_weights("u", ic, pg)
        psg = PSROT[(2 * ic) % 3]
        psu = PSROT[(2 * ic + 1) % 3]
        wv = merge(evg, ps_cons[(2 * ic) % 3])
        for c in range(32):
            for nhf in range(2):
                mm(psg[:, nhf * 512:(nhf + 1) * 512], GW[pg][:, c * 128:(c + 1) * 128],
                   U[:, c * 1024 + nhf * 512:c * 1024 + (nhf + 1) * 512],
                   start=(c == 0), stop=False,
                   waits=wv if (c == 0 and nhf == 0) else ())
        for nhf in range(2):
            mm(psg[:, nhf * 512:(nhf + 1) * 512], ZROW["g"][pg],
               YG[0:32, nhf * 512:(nhf + 1) * 512], start=False, stop=True,
               waits=merge(ezg, ev_yg) if nhf == 0 else ())
        ev_gt = p.force_inc("tensor")
        w_cons2["g"][pg] = ev_gt
        zs_cons["g"][pg] = merge(zs_cons["g"][pg], ev_gt)
        wv = merge(evu, ps_cons[(2 * ic + 1) % 3])
        for c in range(32):
            for nhf in range(2):
                mm(psu[:, nhf * 512:(nhf + 1) * 512], UW[pg][:, c * 128:(c + 1) * 128],
                   U[:, c * 1024 + nhf * 512:c * 1024 + (nhf + 1) * 512],
                   start=(c == 0), stop=False,
                   waits=wv if (c == 0 and nhf == 0) else ())
        for nhf in range(2):
            mm(psu[:, nhf * 512:(nhf + 1) * 512], ZROW["u"][pg],
               YG[0:32, nhf * 512:(nhf + 1) * 512], start=False, stop=True,
               waits=ezu if nhf == 0 else ())
        ev_ut = p.force_inc("tensor")
        w_cons2["u"][pg] = ev_ut
        zs_cons["u"][pg] = merge(zs_cons["u"][pg], ev_ut)
        ev_sil = p.emit("scalar", lambda e, P=psg: e.activation(SILB, P[:], AF.Silu),
                        waits=merge(ev_gt, sil_cons), inc="cq")
        ev_h = p.emit("vector",
                      lambda e, P=psu, b=pg: e.tensor_tensor(HACTB[b], SILB, P[:], OP.mult),
                      waits=merge(ev_sil, ev_ut, hact_cons[pg]), inc="vq")
        sil_cons = ev_h
        ps_cons[(2 * ic) % 3] = ev_h
        ps_cons[(2 * ic + 1) % 3] = ev_h
        ehs = p.emit("scalar", lambda e, i=ic, b=pg: e.dma_start(
            hactD[:, i * 1024:(i + 1) * 1024], HACTB[b]),
            waits=ev_h, inc="sc")
        mm(PD[0:1, :], ONES[:], HACTB[pg][:, 0:512], start=True, stop=True,
           waits=merge(ev_h, pd_cons))
        mm(PD[32:33, :], ONES[:], HACTB[pg][:, 512:1024], start=True, stop=True)
        evt = p.force_inc("tensor")
        ea = p.emit("scalar", lambda e: e.activation(STGB[0:1, 0:512], PD[0:1, :], AF.Identity,
                                                     scale=-1.0),
                    waits=merge(evt, yh_done), inc="cq")
        eb = p.emit("scalar", lambda e: e.activation(STGB[0:1, 512:1024], PD[32:33, :], AF.Identity,
                                                     scale=-1.0),
                    waits=ea, inc="cq")
        pd_cons = eb
        yh_done = p.emit("scalar", lambda e, i=ic: e.dma_start(YH[i:i + 1, :], STGB[:]),
                         waits=eb, inc="sc")
        hact_cons[pg] = merge(ehs, evt)

    p.join()

    DW = [AR[:, 0:4096], AR[:, 4096:8192]]
    DBC = AR[:, 8192:12288]
    RAWD = [AR[:, 12288 + 2048 * i:12288 + 2048 * (i + 1)].bitcast(I32) for i in range(2)] + \
           [AR[:, 20480 + 2048 * i:20480 + 2048 * (i + 1)].bitcast(I32) for i in range(6)]
    Z2STG = [AR[:, 16384:17408], AR[:, 17408:18432]]
    Y1STG = [AR[:, 18432:19456], AR[:, 19456:20480]]
    DACC = [ARF[:, 0:1024], ARF[:, 1024:2048]]
    DACC2 = [ARF[:, 2048:3072], ARF[:, 3072:4096]]
    TMPF = ARF[:, 4096:5120]

    raw_cons_d = [{} for _ in range(8)]
    rawslot = [0]
    bc_cons_d = {}
    dw_cons = [{}, {}]
    zsd_cons = [{}, {}]
    dacc_cons = [{}, {}]
    dacc2_cons = [{}, {}]
    z2_cons = [{}, {}]
    y1g_cons = [{}, {}]
    pp_cons = [{}, {}]
    slab_cons = {}
    stats2_ev = {}
    tmpf_cons = {}
    daccD_ev = [{} for _ in range(32)]

    for part in range(3):
        g0, nch = DPARTS[part]
        evs = []
        spl = [(i * nch) // 8 for i in range(9)]
        for s in range(8):
            l0, l1 = spl[s], spl[s + 1]
            eng2, q2 = ("sync", "sd") if s % 2 == 0 else ("gpsimd", "gw")
            evs.append(p.emit(eng2,
                              lambda e, a=l0, b=l1, g=g0: e.dma_start(
                                  U[:, a * 1024:b * 1024],
                                  hactD[:, (g + a) * 1024:(g + b) * 1024]),
                              waits=slab_cons if s < 2 else (), inc=q2))
        hact_loaded = merge(*evs)
        for hc in range(32):
            pp = hc % 2
            psd = PA if pp == 0 else PB
            zb = pp * 256
            evw = []
            ers = []
            prs = []
            nq = (nch + 7) // 8
            for j in range(nq):
                c0, c1 = j * 8, min((j + 1) * 8, nch)
                pr = rawslot[0] % 8
                rawslot[0] += 1
                prs.append(pr)
                ers.append(p.emit("sync", lambda e, c=hc, a=c0, b=c1, R=RAWD[pr], g=g0: e.dma_start(
                    R[:, 0:(b - a) * 128].rearrange("p (c j) -> p c j", j=128),
                    dr["ad"][c, :, g + a:g + b, :]),
                    waits=raw_cons_d[pr], inc="sd"))
            ez1 = p.emit("scalar", lambda e, g=g0, n=nch, c=hc, o=zb: e.dma_start(
                ZSDF[g:g + n, o:o + 128], dr["zdn"][g:g + n, c * 128:(c + 1) * 128]),
                waits=zsd_cons[pp], inc="sc")
            ez2 = p.emit("scalar", lambda e, g=g0, n=nch, c=hc, o=zb: e.dma_start(
                ZSDF[g:g + n, o + 128:o + 256], dr["sdn"][g:g + n, c * 128:(c + 1) * 128]),
                waits=zsd_cons[pp], inc="sc")
            ezz = p.emit("vector", lambda e, b=pp, g=g0, n=nch, o=zb: e.scalar_tensor_tensor(
                ZSD[g:g + n, b * 128:(b + 1) * 128], ZSDF[g:g + n, o:o + 128], -8.0,
                ZSDF[g:g + n, o + 128:o + 256], OP.add, OP.mult),
                waits=merge(ez1, ez2), inc="vq")
            zsd_cons[pp] = ezz
            ebc = p.emit("gpsimd", lambda e, g=g0, n=nch, c=hc: e.dma_start(
                DBC[:, 0:n * 128].rearrange("p (c j) -> p c j", j=128),
                dr["sdn"][g:g + n, c * 128:(c + 1) * 128].unsqueeze(0)
                .broadcast_to([128, n, 128])),
                waits=bc_cons_d, inc="gw")
            for j in range(nq):
                c0, c1 = j * 8, min((j + 1) * 8, nch)
                pr = prs[j]
                em = p.emit("vector", lambda e, a=c0, b=c1, R=RAWD[pr], W=DW[pp]: e.scalar_tensor_tensor(
                    W[:, a * 128:b * 128], R[:, 0:(b - a) * 128], -8.0,
                    DBC[:, a * 128:b * 128], OP.add, OP.mult),
                    waits=merge(ers[j], ebc, dw_cons[pp]), inc="vq")
                raw_cons_d[pr] = em
                evw.append(em)
            bc_cons_d = merge(*evw)
            ev_dw = merge(*evw)
            wv = merge(ev_dw, hact_loaded, pp_cons[pp])
            for c in range(nch):
                for nhf in range(2):
                    mm(psd[:, nhf * 512:(nhf + 1) * 512], DW[pp][:, c * 128:(c + 1) * 128],
                       U[:, c * 1024 + nhf * 512:c * 1024 + (nhf + 1) * 512],
                       start=(c == 0), stop=False,
                       waits=wv if (c == 0 and nhf == 0) else ())
            for nhf in range(2):
                mm(psd[:, nhf * 512:(nhf + 1) * 512], ZSD[g0:g0 + nch, pp * 128:(pp + 1) * 128],
                   YH[g0:g0 + nch, nhf * 512:(nhf + 1) * 512], start=False, stop=True,
                   waits=merge(ezz, yh_done) if nhf == 0 else ())
            evt = p.force_inc("tensor")
            dw_cons[pp] = evt
            zsd_cons[pp] = merge(zsd_cons[pp], evt)
            if part == 0:
                ea = p.emit("vector", lambda e, b=pp, P=psd: e.tensor_copy(DACC[b], P[:]),
                            waits=merge(evt, dacc_cons[pp]), inc="vq")
                pp_cons[pp] = ea
                dacc_cons[pp] = p.emit("scalar",
                                       lambda e, b=pp, c=hc: e.dma_start(daccD[c], DACC[b]),
                                       waits=ea, inc="sc")
                daccD_ev[hc] = dacc_cons[pp]
            elif part == 1:
                er = p.emit("gpsimd", lambda e, b=pp, c=hc: e.dma_start(DACC2[b], daccD[c]),
                            waits=merge(dacc2_cons[pp], daccD_ev[hc]), inc="gw")
                ea = p.emit("vector",
                            lambda e, b=pp, P=psd: e.tensor_tensor(DACC[b], P[:], DACC2[b], OP.add),
                            waits=merge(evt, er, dacc_cons[pp]), inc="vq")
                pp_cons[pp] = ea
                dacc2_cons[pp] = ea
                dacc_cons[pp] = p.emit("scalar",
                                       lambda e, b=pp, c=hc: e.dma_start(daccD[c], DACC[b]),
                                       waits=ea, inc="sc")
                daccD_ev[hc] = dacc_cons[pp]
            else:
                er = p.emit("gpsimd", lambda e, b=pp, c=hc: e.dma_start(DACC2[b], daccD[c]),
                            waits=merge(dacc2_cons[pp], daccD_ev[hc]), inc="gw")
                ey = p.emit("gpsimd", lambda e, b=pp, c=hc: e.dma_start(Y1STG[b], y1D[c]),
                            waits=y1g_cons[pp], inc="gw")
                ea = p.emit("vector",
                            lambda e, b=pp, P=psd: e.tensor_tensor(TMPF, P[:], DACC2[b], OP.add),
                            waits=merge(evt, er, tmpf_cons), inc="vq")
                pp_cons[pp] = ea
                dacc2_cons[pp] = ea
                eb = p.emit("vector",
                            lambda e, b=pp: e.tensor_tensor(Z2STG[b], TMPF, Y1STG[b], OP.add),
                            waits=merge(ea, ey, z2_cons[pp]), inc="vq")
                tmpf_cons = eb
                y1g_cons[pp] = eb
                ez = p.emit("scalar", lambda e, b=pp, c=hc: e.dma_start(z2D[c], Z2STG[b]),
                            waits=eb, inc="sc")
                esq = p.emit("vector",
                             lambda e, b=pp: e.tensor_tensor(SQZ2[:], Z2STG[b], Z2STG[b], OP.mult),
                             waits=eb, inc="vq")
                st = hc == 0
                sp = hc == 31
                mm(PD[0:1, :], ONES[:], Z2STG[pp][:, 0:512], start=st, stop=sp,
                   waits=merge(eb, pd_cons) if st else eb)
                mm(PD[32:33, :], ONES[:], Z2STG[pp][:, 512:1024], start=st, stop=sp)
                mm(PD[64:65, :], ONES[:], SQZ2[:, 0:512], start=st, stop=sp,
                   waits=merge(esq, pe_cons) if st else esq)
                mm(PE_[0:1, :], ONES[:], SQZ2[:, 512:1024], start=st, stop=sp)
                stats2_ev = p.force_inc("tensor")
                z2_cons[pp] = merge(ez, stats2_ev)
        slab_cons = p.force_inc("tensor")

    ev_rows2, _ = ln_finalize(PE_[0:1, :])
    wv = merge(ev_rows2, pp_cons[0], pp_cons[1], pc_cons)
    for nhf in range(2):
        mm(PC[:, nhf * 512:(nhf + 1) * 512], ONESR[:],
           RSTDB[0:1, nhf * 512:(nhf + 1) * 512], start=True, stop=True,
           waits=wv if nhf == 0 else ())
        mm(PA[:, nhf * 512:(nhf + 1) * 512], ONESR[:],
           NMRB[0:1, nhf * 512:(nhf + 1) * 512], start=True, stop=True)
    ev_bc2 = p.force_inc("tensor")
    for hc in range(32):
        pp = hc % 2
        evr = p.emit("sync", lambda e, b=pp, c=hc: e.dma_start(Z2STG[b], z2D[c]),
                     waits=merge(z2_cons[pp], ev_bc2), inc="sd")
        ea = p.emit("vector",
                    lambda e, b=pp: e.tensor_tensor(TMPF, Z2STG[b], PC[:], OP.mult),
                    waits=merge(evr, tmpf_cons), inc="vq")
        tmpf_cons = ea
        eb = p.emit("vector",
                    lambda e, b=pp: e.tensor_tensor(DACC[b], TMPF, PA[:], OP.add),
                    waits=merge(ea, dacc_cons[pp]), inc="vq")
        tmpf_cons = eb
        z2_cons[pp] = merge(evr, eb)
        dacc_cons[pp] = p.emit("scalar",
                               lambda e, b=pp, c=hc: e.dma_start(
                                   outT[c * 128:(c + 1) * 128, :], DACC[b]),
                               waits=eb, inc="sc")

    p.emit("sync", None, waits=p.ev("sd"))
    p.emit("gpsimd", None, waits=p.ev("gw"))
    p.emit("scalar", None, waits=p.ev("sc"))

    p.realize()
    return nc


_CACHED = {}


def _host_stage(inputs):
    f32 = np.float32
    x = np.asarray(inputs["L0_attn_hidden_size"], f32)
    Wq_c = np.asarray(inputs["L0_attn_Wq_c"], f32)
    bq_c = np.asarray(inputs["L0_attn_bq_c"], f32)
    Wq_up = np.asarray(inputs["L0_attn_Wq_up"], f32)
    bq_up = np.asarray(inputs["L0_attn_bq_up"], f32)
    Wkv_c = np.asarray(inputs["L0_attn_Wkv_c"], f32)
    bkv_c = np.asarray(inputs["L0_attn_bkv_c"], f32)
    Wkv_up = np.asarray(inputs["L0_attn_Wkv_up"], f32)
    bkv_up = np.asarray(inputs["L0_attn_bkv_up"], f32)
    Wo = np.asarray(inputs["L0_attn_Wo"], f32)
    bo = np.asarray(inputs["L0_attn_bo"], f32)
    gq = np.asarray(inputs["L0_ffn_gate_qweight"], np.int32)
    gs = np.asarray(inputs["L0_ffn_gate_scales"], f32)
    gz = np.asarray(inputs["L0_ffn_gate_zeros"], f32)
    uq = np.asarray(inputs["L0_ffn_up_qweight"], np.int32)
    us = np.asarray(inputs["L0_ffn_up_scales"], f32)
    uz = np.asarray(inputs["L0_ffn_up_zeros"], f32)
    dq = np.asarray(inputs["L0_ffn_down_qweight"], np.int32)
    ds = np.asarray(inputs["L0_ffn_down_scales"], f32)
    dz = np.asarray(inputs["L0_ffn_down_zeros"], f32)

    C = np.ascontiguousarray
    sh = {}
    sh["aqc"] = C(Wq_c.reshape(12, 128, 32, 128).transpose(0, 3, 2, 1))
    sh["aqup"] = C(Wq_up.reshape(32, 192, 12, 128).transpose(0, 3, 2, 1))
    wkvc_p = np.zeros((640, 4096), f32)
    wkvc_p[0:576] = Wkv_c
    sh["akvc"] = C(wkvc_p.reshape(5, 128, 32, 128).transpose(0, 3, 2, 1))
    sh["akvup"] = C(Wkv_up.reshape(32, 256, 4, 128).transpose(0, 3, 2, 1))
    sh["awo"] = C(Wo.T)
    sh["bqc2"] = C(bq_c.reshape(12, 128).T)
    bquph = np.zeros((128, 64), f32)
    for h in range(32):
        bquph[:, 2 * h] = bq_up[h * 192:h * 192 + 128]
        bquph[0:64, 2 * h + 1] = bq_up[h * 192 + 128:(h + 1) * 192]
    sh["bquph"] = bquph
    bkvc_p = np.zeros(640, f32)
    bkvc_p[0:576] = bkv_c
    sh["bkvc2"] = C(bkvc_p.reshape(5, 128).T)
    bk = bkv_up.reshape(32, 2, 128)
    bkvuk = np.zeros((128, 64), f32)
    for h in range(32):
        bkvuk[:, 2 * h] = bk[h, 0]
    sh["bkvuk"] = bkvuk
    sh["bvrow"] = C(bk[:, 1, :].reshape(1, 4096))
    sh["bo2"] = C(bo.reshape(32, 128).T)
    sh["ag"] = C(gq.reshape(86, 128, 32, 128).transpose(0, 3, 2, 1))
    sh["au"] = C(uq.reshape(86, 128, 32, 128).transpose(0, 3, 2, 1))
    sh["ad"] = C(dq.reshape(32, 128, 86, 128).transpose(0, 3, 2, 1))
    sh["sg"] = C(gs.T)
    sh["zg"] = C(gz.T)
    sh["su"] = C(us.T)
    sh["zu"] = C(uz.T)
    sh["sdn"] = C(ds.T)
    sh["zdn"] = C(dz.T)

    in_maps = []
    for core in range(8):
        seq = core // 4
        tb = core % 4
        d = dict(sh)
        xt = C(x[seq].T)
        ax = xt.reshape(32, 128, 16, 256).transpose(2, 1, 0, 3)
        rot = np.roll(np.arange(16), OWN_QQ0 - tb * 4)
        d["ax"] = C(ax[rot])
        in_maps.append(d)
    return in_maps


def kernel(**inputs):
    if "nc" not in _CACHED:
        _CACHED["nc"] = build_program()
    nc = _CACHED["nc"]
    in_maps = _host_stage(inputs)
    res = run_bass_kernel_spmd(nc, in_maps, core_ids=list(range(8)))
    out = np.empty((2, 4096, 4096), np.float32)
    for core in range(8):
        seq = core // 4
        tb = core % 4
        out[seq, tb * 1024:(tb + 1) * 1024, :] = res.results[core]["outT"].T
    return out


# version 06
# speedup vs baseline: 1.0535x; 1.0438x over previous
import numpy as np
import concourse.bass as bass
import concourse.mybir as mybir
from concourse.bass_utils import run_bass_kernel_spmd
from concourse.alu_op_type import AluOpType as OP

F32 = mybir.dt.float32
BF16 = mybir.dt.bfloat16
I32 = mybir.dt.int32
AF = mybir.ActivationFunctionType
AX = mybir.AxisListType.X

ENGINES = ("tensor", "vector", "scalar", "sync", "gpsimd")
NH = 32
NIC = 86
ISQ = 0.07216878364870322
OWN_QQ0 = 12
DPARTS = [(0, 32), (32, 32), (64, 22)]
QK = 6
QUEUES = ("sd", "gw", "sc")


class Prog:
    def __init__(self, nc):
        self.nc = nc
        self.ops = {e: [] for e in ENGINES}
        self.sems = {}
        self.counts = {}
        names = ["tq", "vq", "cq"]
        for q in QUEUES:
            names += ["%s%d" % (q, i) for i in range(QK)]
        for name in names:
            self.sems[name] = nc.alloc_semaphore(name)
            self.counts[name] = 0
        self.esem = dict(tensor="tq", vector="vq", scalar="cq")
        self.rr = {q: 0 for q in QUEUES}

    def emit(self, eng, fn, waits=(), inc=None):
        waits = dict(waits)
        if inc in QUEUES:
            k = self.rr[inc]
            self.rr[inc] = (k + 1) % QK
            name = "%s%d" % (inc, k)
            waits[name] = max(waits.get(name, 0), self.counts[name])
            self.counts[name] += 16
            rec = (name, self.counts[name])
        elif inc is not None:
            self.counts[inc] += 1
            rec = (inc, self.counts[inc])
        else:
            rec = None
        self.ops[eng].append([fn, waits, rec])
        return {rec[0]: rec[1]} if rec else {}

    def ev(self, sem):
        if sem in QUEUES:
            return {"%s%d" % (sem, k): self.counts["%s%d" % (sem, k)]
                    for k in range(QK)
                    if self.counts["%s%d" % (sem, k)] > 0}
        return {sem: self.counts[sem]}

    def force_inc(self, eng):
        sem = self.esem[eng]
        for op in reversed(self.ops[eng]):
            if op[0] is None:
                continue
            if op[2] is None:
                self.counts[sem] += 1
                op[2] = (sem, self.counts[sem])
            return {op[2][0]: op[2][1]}
        return {}

    def join(self):
        evs = {}
        for e in ("tensor", "vector", "scalar"):
            evs.update(self.force_inc(e))
        evs = merge(evs, self.ev("sd"), self.ev("gw"), self.ev("sc"))
        for e in ENGINES:
            self.emit(e, None, waits=evs)
        return evs

    def realize(self):
        nc = self.nc
        with nc.Block() as block:
            for name in ENGINES:
                def body(eng, ops=self.ops[name], sems=self.sems):
                    for fn, waits, inc in ops:
                        for s, v in waits.items():
                            eng.wait_ge(sems[s], v)
                        if fn is None:
                            continue
                        ins = fn(eng)
                        if inc is not None:
                            ins.then_inc(sems[inc[0]],
                                         1 if inc[0] in ("tq", "vq", "cq") else 16)
                getattr(block, name)(body)


def merge(*evs):
    out = {}
    for e in evs:
        for k, v in e.items():
            out[k] = max(out.get(k, 0), v)
    return out


def build_program():
    nc = bass.Bass("TRN2", target_bir_lowering=False)

    dr = {}
    def din(name, shape, dt):
        dr[name] = nc.dram_tensor(name, shape, dt, kind="ExternalInput").ap()
    din("ax", [16, 128, 32, 256], F32)
    din("aqc", [12, 128, 32, 128], F32)
    din("akvc", [5, 128, 32, 128], F32)
    din("aqup", [32, 128, 12, 192], F32)
    din("akvup", [32, 128, 4, 256], F32)
    din("awo", [128, 4096], F32)
    din("bqc2", [128, 12], F32)
    din("bquph", [128, 64], F32)
    din("bkvc2", [128, 5], F32)
    din("bkvuk", [128, 64], F32)
    din("bvrow", [1, 4096], F32)
    din("bo2", [128, 32], F32)
    din("ag", [86, 128, 32, 128], I32)
    din("au", [86, 128, 32, 128], I32)
    din("ad", [32, 128, 86, 128], I32)
    din("sg", [32, 11008], F32)
    din("zg", [32, 11008], F32)
    din("su", [32, 11008], F32)
    din("zu", [32, 11008], F32)
    din("sdn", [86, 4096], F32)
    din("zdn", [86, 4096], F32)
    outT = nc.dram_tensor("outT", [4096, 1024], F32, kind="ExternalOutput").ap()

    hactD = nc.dram_tensor("hactD", [128, 88064], BF16).ap()
    z1D = nc.dram_tensor("z1D", [32, 128, 1024], BF16).ap()
    z2D = nc.dram_tensor("z2D", [32, 128, 1024], BF16).ap()
    y1D = nc.dram_tensor("y1D", [32, 128, 1024], BF16).ap()
    daccD = nc.dram_tensor("daccD", [32, 128, 1024], F32).ap()

    U = nc.alloc_sbuf_tensor("U", [128, 32768], BF16)
    AR = nc.alloc_sbuf_tensor("AR", [128, 32768], BF16)
    ARF = nc.alloc_sbuf_tensor("ARF", [128, 5120], F32)
    WOT = nc.alloc_sbuf_tensor("WOT", [128, 4096], BF16)
    OS = nc.alloc_sbuf_tensor("OS", [128, 1024], F32)
    RBUF = nc.alloc_sbuf_tensor("RBUF", [128, 1024], F32)
    RSTDB = nc.alloc_sbuf_tensor("RSTDB", [1, 1024], BF16)
    NMRB = nc.alloc_sbuf_tensor("NMRB", [1, 1024], BF16)
    RECB = nc.alloc_sbuf_tensor("RECB", [1, 1024], BF16)
    STGB = nc.alloc_sbuf_tensor("STGB", [1, 1024], BF16)
    RN = nc.alloc_sbuf_tensor("RN", [1, 4096], BF16)
    BVROW = nc.alloc_sbuf_tensor("BVROW", [1, 4096], BF16)
    YG = nc.alloc_sbuf_tensor("YG", [32, 1024], BF16)
    YH = nc.alloc_sbuf_tensor("YH", [86, 1024], BF16)
    BQC = nc.alloc_sbuf_tensor("BQC", [128, 12], F32)
    BQUP = nc.alloc_sbuf_tensor("BQUP", [128, 64], F32)
    BKVC = nc.alloc_sbuf_tensor("BKVC", [128, 5], F32)
    BKVUK = nc.alloc_sbuf_tensor("BKVUK", [128, 64], F32)
    BO = nc.alloc_sbuf_tensor("BO", [128, 32], F32)
    ONES = nc.alloc_sbuf_tensor("ONES", [128, 1], BF16)
    ONESR = nc.alloc_sbuf_tensor("ONESR", [1, 128], BF16)
    EPS1 = nc.alloc_sbuf_tensor("EPS1", [1, 1], F32)
    MISC = nc.alloc_sbuf_tensor("MISC", [1, 16], F32)
    MISCB = nc.alloc_sbuf_tensor("MISCB", [1, 16], BF16)
    NEGC = nc.alloc_sbuf_tensor("NEGC", [128, 2], F32)
    ZSG = nc.alloc_sbuf_tensor("ZSG", [32, 512], BF16)
    ZSF = nc.alloc_sbuf_tensor("ZSF", [32, 1024], F32)
    ZSD = nc.alloc_sbuf_tensor("ZSD", [86, 256], BF16)
    ZSDF = nc.alloc_sbuf_tensor("ZSDF", [86, 512], F32)
    SQZ2 = nc.alloc_sbuf_tensor("SQZ2", [128, 1024], BF16)

    MROW = ARF[0:1, 0:1024]
    MSQ = ARF[0:1, 1024:2048]
    TROW = ARF[0:1, 2048:3072]

    PA = nc.alloc_psum_tensor("PA", [128, 1024], F32)
    PB = nc.alloc_psum_tensor("PB", [128, 1024], F32)
    PC = nc.alloc_psum_tensor("PC", [128, 1024], F32)
    PD = nc.alloc_psum_tensor("PD", [128, 512], F32)
    PE_ = nc.alloc_psum_tensor("PE", [33, 512], F32)

    p = Prog(nc)

    def mm(ps, lhs, rhs, start, stop, waits=()):
        p.emit("tensor",
               lambda e, a=ps, b=lhs, c=rhs, s1=start, s2=stop:
               e.matmul(a, b, c, start=s1, stop=s2), waits=waits)

    p.emit("vector", lambda e: e.memset(ONES[:], 1.0))
    p.emit("vector", lambda e: e.memset(ONESR[:], 1.0))
    p.emit("vector", lambda e: e.memset(EPS1[:], 1e-5))
    for nm, tgt in (("bqc2", BQC), ("bquph", BQUP), ("bkvc2", BKVC),
                    ("bkvuk", BKVUK), ("bo2", BO)):
        p.emit("sync", lambda e, a=tgt, b=dr[nm]: e.dma_start(a[:], b), inc="sd")
    ev_bias = p.ev("sd")
    ev_bq = p.emit("vector",
                   lambda e: e.tensor_scalar(BQUP[:], BQUP[:], ISQ, None, OP.mult),
                   waits=ev_bias, inc="vq")
    p.emit("gpsimd", lambda e: e.dma_start(BVROW[:], dr["bvrow"]), inc="gw")
    ev_wo = p.emit("gpsimd", lambda e: e.dma_start(WOT[:], dr["awo"]), inc="gw")

    KVC = lambda m: AR[:, m * 4096:(m + 1) * 4096]
    KVCR = AR[0:64, 16384:20480]
    QCC = lambda c: AR[:, 20480 + c * 1024: 20480 + (c + 1) * 1024]
    XQ = [U[:, 0:8192], U[:, 8192:16384]]
    P1W = [U[:, 16384:20480], U[:, 20480:24576]]

    xq_ev = [{}, {}]
    xq_cons = [{}, {}]
    w_cons = {}
    psl_cons = {}
    wslot = 0
    pslot = 0
    for qq in range(16):
        bx = qq % 2
        xq_ev[bx] = p.emit("gpsimd",
                           lambda e, b=bx, q=qq: e.dma_start(
                               XQ[b].rearrange("p (c t) -> p c t", t=256), dr["ax"][q]),
                           waits=xq_cons[bx], inc="gw")
        own = qq >= OWN_QQ0
        mlist = list(range(5)) + (list(range(100, 112)) if own else [])
        for m in mlist:
            isq = m >= 100
            ws = wslot % 2
            wslot += 1
            src = dr["aqc"][m - 100] if isq else dr["akvc"][m]
            wt = P1W[ws]
            w_ev = p.emit("gpsimd",
                          lambda e, a=wt, b=src: e.dma_start(
                              a.rearrange("p (c j) -> p c j", j=128), b),
                          waits=w_cons.get(ws, {}), inc="gw")
            sl = pslot % 4
            pslot += 1
            ps = [PA[:, 0:512], PA[:, 512:1024], PB[:, 0:512], PB[:, 512:1024]][sl]
            wv = merge(w_ev, xq_ev[bx], psl_cons.get(sl, {}))
            for c in range(32):
                mm(ps[:, 0:256], wt[:, c * 128:(c + 1) * 128],
                   XQ[bx][:, c * 256:(c + 1) * 256],
                   start=(c == 0), stop=(c == 31), waits=wv if c == 0 else ())
            ev_t = p.force_inc("tensor")
            w_cons[ws] = ev_t
            xq_cons[bx] = merge(xq_cons[bx], ev_t)
            if isq:
                c = m - 100
                dst = QCC(c)[:, (qq - OWN_QQ0) * 256:(qq - OWN_QQ0 + 1) * 256]
                fn = (lambda e, a=dst, b=ps, bb=BQC[:, c:c + 1]:
                      e.activation(a, b[:, 0:256], AF.Identity, bias=bb))
            elif m < 4:
                dst = KVC(m)[:, qq * 256:(qq + 1) * 256]
                fn = (lambda e, a=dst, b=ps, bb=BKVC[:, m:m + 1]:
                      e.activation(a, b[:, 0:256], AF.Identity, bias=bb))
            else:
                dst = KVCR[:, qq * 256:(qq + 1) * 256]
                fn = (lambda e, a=dst, b=ps, bb=BKVC[0:64, 4:5]:
                      e.activation(a, b[0:64, 0:256], AF.Identity, bias=bb))
            psl_cons[sl] = p.emit("scalar", fn, waits=merge(ev_t, ev_bias), inc="cq")

    p.join()

    SQ = U[:, 22528:23552]
    rn_done = {}
    for sl4 in range(4):
        evq = p.emit("vector",
                     lambda e, s=sl4: e.tensor_tensor(
                         SQ[0:64, :], KVCR[:, s * 1024:(s + 1) * 1024],
                         KVCR[:, s * 1024:(s + 1) * 1024], OP.mult),
                     waits=rn_done, inc="vq")
        for nh2 in range(2):
            mm(PD[0:1, :], ONES[0:64, :], SQ[0:64, nh2 * 512:(nh2 + 1) * 512],
               start=True, stop=True, waits=merge(evq, rn_done))
            evt = p.force_inc("tensor")
            rn_done = p.emit("vector",
                             lambda e, s=sl4, n=nh2: e.tensor_copy(
                                 RN[0:1, s * 1024 + n * 512:s * 1024 + (n + 1) * 512],
                                 PD[0:1, :]),
                             waits=evt, inc="vq")
    p.join()

    VT = [U[:, 0:4096], U[:, 4096:8192]]
    KH = [U[:, 8192:12288], U[:, 12288:16384]]
    QH = [U[:, 16384:17408], U[:, 17408:18432]]
    QR = [U[0:64, 18432:19456], U[0:64, 19456:20480]]
    ET = [U[:, 20480:21504], U[:, 21504:22528]]
    QUPW = [U[:, 23552:25856], U[:, 25856:28160]]
    KVUPW = [U[:, 28160:29184], U[:, 29184:30208]]
    TMPB = U[:, 30208:31232]

    qupw_ev = [{}, {}]
    kvupw_ev = [{}, {}]
    qupw_cons = [{}, {}]
    kvupw_cons = [{}, {}]
    build_cons = [{}, {}]
    pd_cons = {}
    pa_cons = [{}, {}]
    pc_cons = {}
    pe_cons = {}
    et_cons = [{}, {}]
    rbuf_cons = {}
    os_ev = {}

    for hh in range(min(2, NH)):
        ph = hh % 2
        qupw_ev[ph] = p.emit("gpsimd",
                             lambda e, b=ph, h=hh: e.dma_start(
                                 QUPW[b].rearrange("p (c j) -> p c j", j=192),
                                 dr["aqup"][h]), inc="gw")
        kvupw_ev[ph] = p.emit("gpsimd",
                              lambda e, b=ph, h=hh: e.dma_start(
                                  KVUPW[b].rearrange("p (c j) -> p c j", j=256),
                                  dr["akvup"][h]), inc="gw")

    for h in range(NH):
        ph = h % 2
        for ks in range(8):
            wv = merge(kvupw_ev[ph], pd_cons, build_cons[ph] if ks == 0 else {})
            for c in range(4):
                mm(PD[:], KVUPW[ph][:, c * 256:c * 256 + 128],
                   KVC(c)[:, ks * 512:(ks + 1) * 512],
                   start=(c == 0), stop=(c == 3), waits=wv if c == 0 else ())
            evt = p.force_inc("tensor")
            pd_cons = p.emit("scalar",
                             lambda e, s=ks, b=ph, hh=h: e.activation(
                                 KH[b][:, s * 512:(s + 1) * 512], PD[:], AF.Identity,
                                 bias=BKVUK[:, 2 * hh:2 * hh + 1]),
                             waits=evt, inc="cq")
        ev_kh = pd_cons
        for vu in range(8):
            wv = merge(kvupw_ev[ph], pd_cons)
            for tc4 in range(4):
                tc = vu * 4 + tc4
                dst = PD[:, tc4 * 128:(tc4 + 1) * 128]
                for c in range(4):
                    mm(dst, AR[:, c * 4096 + tc * 128:c * 4096 + (tc + 1) * 128],
                       KVUPW[ph][:, c * 256 + 128:(c + 1) * 256],
                       start=(tc4 == 0 and c == 0), stop=False,
                       waits=wv if (tc4 == 0 and c == 0) else ())
                mm(dst, ONESR[:], BVROW[0:1, h * 128:(h + 1) * 128],
                   start=False, stop=(tc4 == 3))
            evt = p.force_inc("tensor")
            pd_cons = p.emit("scalar",
                             lambda e, b=ph, u=vu: e.activation(
                                 VT[b][:, u * 512:(u + 1) * 512], PD[:], AF.Identity),
                             waits=evt, inc="cq")
        ev_vt = pd_cons
        kvupw_cons[ph] = p.force_inc("tensor")
        for nhf in range(2):
            wv = merge(qupw_ev[ph], pd_cons)
            for c in range(12):
                mm(PD[:], QUPW[ph][:, c * 192:c * 192 + 128],
                   QCC(c)[:, nhf * 512:(nhf + 1) * 512],
                   start=(c == 0), stop=(c == 11), waits=wv if c == 0 else ())
            evt = p.force_inc("tensor")
            pd_cons = p.emit("scalar",
                             lambda e, b=ph, n=nhf, hh=h: e.activation(
                                 QH[b][:, n * 512:(n + 1) * 512], PD[:], AF.Identity,
                                 bias=BQUP[:, 2 * hh:2 * hh + 1], scale=ISQ),
                             waits=merge(evt, ev_bq), inc="cq")
            wv = merge(qupw_ev[ph], pd_cons)
            for c in range(12):
                mm(PD[0:64, :], QUPW[ph][:, c * 192 + 128:c * 192 + 192][0:64, :]
                   if False else QUPW[ph][:, c * 192 + 128:c * 192 + 192],
                   QCC(c)[:, nhf * 512:(nhf + 1) * 512],
                   start=(c == 0), stop=(c == 11), waits=wv if c == 0 else ())
            evt = p.force_inc("tensor")
            pd_cons = p.emit("scalar",
                             lambda e, b=ph, n=nhf, hh=h: e.activation(
                                 QR[b][:, n * 512:(n + 1) * 512], PD[0:64, :], AF.Identity,
                                 bias=BQUP[0:64, 2 * hh + 1:2 * hh + 2], scale=ISQ),
                             waits=merge(evt, ev_bq), inc="cq")
        ev_qh = pd_cons
        qupw_cons[ph] = p.force_inc("tensor")

        if h + 2 < NH:
            qupw_ev[ph] = p.emit("gpsimd",
                                 lambda e, b=ph, hh=h + 2: e.dma_start(
                                     QUPW[b].rearrange("p (c j) -> p c j", j=192),
                                     dr["aqup"][hh]),
                                 waits=qupw_cons[ph], inc="gw")
            kvupw_ev[ph] = p.emit("gpsimd",
                                  lambda e, b=ph, hh=h + 2: e.dma_start(
                                      KVUPW[b].rearrange("p (c j) -> p c j", j=256),
                                      dr["akvup"][hh]),
                                  waits=kvupw_cons[ph], inc="gw")

        ev1 = p.emit("vector",
                     lambda e, b=ph: e.tensor_tensor(SQ[:], QH[b][:], QH[b][:], OP.mult),
                     waits=merge(ev_qh, pe_cons), inc="vq")
        mm(PD[0:1, :], ONES[:], SQ[:, 0:512], start=True, stop=False,
           waits=merge(ev1, pd_cons))
        mm(PD[32:33, :], ONES[:], SQ[:, 512:1024], start=True, stop=False)
        evt = p.force_inc("tensor")
        ev2 = p.emit("vector",
                     lambda e, b=ph: e.tensor_tensor(SQ[0:64, :], QR[b][:], QR[b][:], OP.mult),
                     waits=merge(evt, ev_qh), inc="vq")
        mm(PD[0:1, :], ONES[0:64, :], SQ[0:64, 0:512], start=False, stop=True, waits=ev2)
        mm(PD[32:33, :], ONES[0:64, :], SQ[0:64, 512:1024], start=False, stop=True)
        evt = p.force_inc("tensor")
        ev3 = p.emit("vector", lambda e: e.reduce_max(MISC[0:1, 0:1], PD[0:1, :], axis=AX),
                     waits=evt, inc="vq")
        ev4 = p.emit("vector", lambda e: e.reduce_max(MISC[0:1, 1:2], PD[32:33, :], axis=AX),
                     waits=evt, inc="vq")
        qn_done = p.emit("vector",
                         lambda e: e.tensor_tensor(MISC[0:1, 0:1], MISC[0:1, 0:1],
                                                   MISC[0:1, 1:2], OP.max),
                         waits=merge(ev3, ev4), inc="vq")
        pd_cons = qn_done
        p.emit("vector", lambda e: e.memset(MISC[0:1, 2:3], 0.0))
        ev_kn = {}
        for sl4 in range(4):
            ev5 = p.emit("vector",
                         lambda e, b=ph, s=sl4: e.tensor_tensor(
                             SQ[:], KH[b][:, s * 1024:(s + 1) * 1024],
                             KH[b][:, s * 1024:(s + 1) * 1024], OP.mult),
                         waits=merge(ev_kh, qn_done), inc="vq")
            mm(PD[0:1, :], ONES[:], SQ[:, 0:512], start=True, stop=True,
               waits=merge(ev5, pd_cons))
            mm(PD[32:33, :], ONES[:], SQ[:, 512:1024], start=True, stop=True)
            evt = p.force_inc("tensor")
            e6 = p.emit("vector",
                        lambda e, s=sl4: e.tensor_tensor(
                            TROW[0:1, 0:512], PD[0:1, :],
                            RN[0:1, s * 1024:s * 1024 + 512], OP.add),
                        waits=evt, inc="vq")
            e7 = p.emit("vector",
                        lambda e, s=sl4: e.tensor_tensor(
                            TROW[0:1, 512:1024], PD[32:33, :],
                            RN[0:1, s * 1024 + 512:(s + 1) * 1024], OP.add),
                        waits=evt, inc="vq")
            pd_cons = merge(e6, e7)
            e8 = p.emit("vector",
                        lambda e: e.reduce_max(MISC[0:1, 3:4], TROW[0:1, :], axis=AX),
                        waits=merge(e6, e7), inc="vq")
            ev_kn = p.emit("vector",
                           lambda e: e.tensor_tensor(MISC[0:1, 2:3], MISC[0:1, 2:3],
                                                     MISC[0:1, 3:4], OP.max),
                           waits=e8, inc="vq")
        ev9 = p.emit("vector",
                     lambda e: e.tensor_tensor(MISC[0:1, 4:5], MISC[0:1, 0:1],
                                               MISC[0:1, 2:3], OP.mult),
                     waits=ev_kn, inc="vq")
        ev10 = p.emit("scalar",
                      lambda e: e.activation(MISC[0:1, 5:6], MISC[0:1, 4:5], AF.Ln,
                                             bias=EPS1[:]),
                      waits=ev9, inc="cq")
        ev11 = p.emit("scalar",
                      lambda e: e.activation(MISC[0:1, 6:7], MISC[0:1, 5:6], AF.Exp,
                                             scale=0.5),
                      waits=ev10, inc="cq")
        ev12 = p.emit("vector",
                      lambda e: e.tensor_scalar(MISCB[0:1, 0:1], MISC[0:1, 6:7],
                                                -1.0, None, OP.mult),
                      waits=ev11, inc="vq")
        mm(PD[:, 0:1], ONESR[:], MISCB[0:1, 0:1], start=True, stop=True,
           waits=merge(ev12, pd_cons))
        evt = p.force_inc("tensor")
        ev_negc = p.emit("scalar",
                         lambda e, b=ph: e.activation(NEGC[:, b:b + 1], PD[:, 0:1],
                                                      AF.Identity),
                         waits=evt, inc="cq")
        pd_cons = ev_negc

        exp_ev = {}
        scores_ev = {}
        for kt in range(33):
            ep = kt % 2
            if kt < 32:
                psc = PA if ep == 0 else PB
                wv = merge(ev_kh, ev_qh, ev_negc, pa_cons[ep],
                           build_cons[ph] if kt == 0 else {})
                for nhf in range(2):
                    sl = psc[:, nhf * 512:(nhf + 1) * 512]
                    mm(sl, KH[ph][:, kt * 128:(kt + 1) * 128],
                       QH[ph][:, nhf * 512:(nhf + 1) * 512],
                       start=True, stop=False, waits=wv if nhf == 0 else ())
                    mm(sl, KVCR[:, kt * 128:(kt + 1) * 128],
                       QR[ph][:, nhf * 512:(nhf + 1) * 512],
                       start=False, stop=True)
                scores_ev = p.force_inc("tensor")
            if kt >= 1:
                kp = (kt - 1) % 2
                wz = merge(exp_ev, pe_cons if kt == 1 else {})
                mm(PE_[0:1, :], ONES[:], ET[kp][:, 0:512],
                   start=(kt == 1), stop=(kt == 32), waits=wz)
                mm(PE_[32:33, :], ONES[:], ET[kp][:, 512:1024],
                   start=(kt == 1), stop=(kt == 32))
                for nhf in range(2):
                    mm(PC[:, nhf * 512:(nhf + 1) * 512],
                       VT[ph][:, (kt - 1) * 128:kt * 128],
                       ET[kp][:, nhf * 512:(nhf + 1) * 512],
                       start=(kt == 1), stop=(kt == 32),
                       waits=merge(ev_vt, pc_cons) if (kt == 1 and nhf == 0) else ())
                et_cons[kp] = p.force_inc("tensor")
            if kt < 32:
                exp_ev = p.emit("scalar",
                                lambda e, b=ph, ee=ep: e.activation(
                                    ET[ee][:], (PA if ee == 0 else PB)[:], AF.Exp,
                                    bias=NEGC[:, b:b + 1]),
                                waits=merge(scores_ev, et_cons[ep]), inc="cq")
                pa_cons[ep] = exp_ev
        build_cons[ph] = p.force_inc("tensor")

        ez = build_cons[ph]
        r1 = p.emit("vector", lambda e: e.reciprocal(TROW[0:1, 0:512], PE_[0:1, :]),
                    waits=ez, inc="vq")
        r2 = p.emit("vector", lambda e: e.reciprocal(TROW[0:1, 512:1024], PE_[32:33, :]),
                    waits=ez, inc="vq")
        pe_cons = merge(r1, r2)
        r3 = p.emit("vector", lambda e: e.tensor_copy(RECB[:], TROW[:]),
                    waits=pe_cons, inc="vq")
        for nhf in range(2):
            mm(PD[:], ONESR[:], RECB[0:1, nhf * 512:(nhf + 1) * 512],
               start=True, stop=True,
               waits=merge(r3, pd_cons))
            evt = p.force_inc("tensor")
            pd_cons = p.emit("scalar",
                             lambda e, n=nhf: e.activation(
                                 RBUF[:, n * 512:(n + 1) * 512], PD[:], AF.Identity),
                             waits=merge(evt, rbuf_cons), inc="cq")
        if h == 0:
            ev13 = p.emit("vector",
                          lambda e: e.tensor_tensor(OS[:], PC[:], RBUF[:], OP.mult),
                          waits=merge(pd_cons, ez), inc="vq")
            pc_cons = ev13
            rbuf_cons = ev13
            os_ev = ev13
        else:
            ev13 = p.emit("vector",
                          lambda e: e.tensor_tensor(TMPB, PC[:], RBUF[:], OP.mult),
                          waits=merge(pd_cons, ez), inc="vq")
            pc_cons = ev13
            rbuf_cons = ev13
            os_ev = p.emit("vector",
                           lambda e: e.tensor_tensor(OS[:], OS[:], TMPB, OP.add),
                           waits=ev13, inc="vq")

    p.join()

    OSBF = AR[:, 0:1024]
    Z1STG = [AR[:, 1024:2048], AR[:, 2048:3072]]
    SQZ1 = AR[:, 3072:4096]
    TMPB2 = AR[:, 4096:5120]
    XF = [ARF[:, 0:1024], ARF[:, 1024:2048]]

    ev_osbf = p.emit("vector", lambda e: e.tensor_copy(OSBF, OS[:]),
                     waits=os_ev, inc="vq")
    xf_cons = [{}, {}]
    z1_cons = [{}, {}]
    pa2_cons = [{}, {}]
    for hc in range(32):
        pp = hc % 2
        evs = []
        for k4 in range(4):
            evs.append(p.emit("sync",
                              lambda e, b=pp, q=OWN_QQ0 + k4, c=hc, kk=k4: e.dma_start(
                                  XF[b][:, kk * 256:(kk + 1) * 256], dr["ax"][q, :, c, :]),
                              waits=xf_cons[pp] if k4 == 0 else (), inc="sd"))
        xf_ev = merge(*evs)
        psc = PA if pp == 0 else PB
        wv = merge(ev_osbf, ev_wo, pa2_cons[pp])
        for nhf in range(2):
            mm(psc[:, nhf * 512:(nhf + 1) * 512], WOT[:, hc * 128:(hc + 1) * 128],
               OSBF[:, nhf * 512:(nhf + 1) * 512], start=True, stop=True,
               waits=wv if nhf == 0 else ())
        evt = p.force_inc("tensor")
        ev_z1 = p.emit("vector",
                       lambda e, b=pp, c=hc: e.scalar_tensor_tensor(
                           Z1STG[b], (PA if b == 0 else PB)[:], BO[:, c:c + 1], XF[b],
                           OP.add, OP.add),
                       waits=merge(evt, xf_ev, z1_cons[pp]), inc="vq")
        xf_cons[pp] = ev_z1
        pa2_cons[pp] = ev_z1
        esp = p.emit("scalar", lambda e, b=pp, c=hc: e.dma_start(z1D[c], Z1STG[b]),
                     waits=ev_z1, inc="sc")
        ev_sq = p.emit("vector",
                       lambda e, b=pp: e.tensor_tensor(SQZ1, Z1STG[b], Z1STG[b], OP.mult),
                       waits=ev_z1, inc="vq")
        st = hc == 0
        sp = hc == 31
        mm(PD[0:1, :], ONES[:], Z1STG[pp][:, 0:512], start=st, stop=sp,
           waits=merge(ev_z1, pd_cons) if st else ev_z1)
        mm(PD[32:33, :], ONES[:], Z1STG[pp][:, 512:1024], start=st, stop=sp)
        mm(PD[64:65, :], ONES[:], SQZ1[:, 0:512], start=st, stop=sp,
           waits=merge(ev_sq, pe_cons) if st else ev_sq)
        mm(PE_[0:1, :], ONES[:], SQZ1[:, 512:1024], start=st, stop=sp)
        z1_cons[pp] = merge(esp, p.force_inc("tensor"))

    def ln_finalize(pm_sq_hi):
        evt = p.force_inc("tensor")
        e1 = p.emit("vector", lambda e: e.tensor_scalar(
            MROW[0:1, 0:512], PD[0:1, :], 1.0 / 4096, None, OP.mult), waits=evt, inc="vq")
        e2 = p.emit("vector", lambda e: e.tensor_scalar(
            MROW[0:1, 512:1024], PD[32:33, :], 1.0 / 4096, None, OP.mult), waits=evt, inc="vq")
        e3 = p.emit("vector", lambda e: e.tensor_scalar(
            MSQ[0:1, 0:512], PD[64:65, :], 1.0 / 4096, None, OP.mult), waits=evt, inc="vq")
        e4 = p.emit("vector", lambda e: e.tensor_scalar(
            MSQ[0:1, 512:1024], pm_sq_hi, 1.0 / 4096, None, OP.mult), waits=evt, inc="vq")
        e5 = p.emit("vector", lambda e: e.tensor_tensor(TROW[:], MROW[:], MROW[:], OP.mult),
                    waits=merge(e1, e2), inc="vq")
        e6 = p.emit("vector", lambda e: e.tensor_tensor(TROW[:], MSQ[:], TROW[:], OP.subtract),
                    waits=merge(e3, e4, e5), inc="vq")
        e7 = p.emit("scalar", lambda e: e.activation(TROW[:], TROW[:], AF.Ln, bias=EPS1[:]),
                    waits=e6, inc="cq")
        e8 = p.emit("scalar", lambda e: e.activation(TROW[:], TROW[:], AF.Exp, scale=-0.5),
                    waits=e7, inc="cq")
        e9 = p.emit("vector", lambda e: e.tensor_copy(RSTDB[:], TROW[:]), waits=e8, inc="vq")
        e10 = p.emit("vector", lambda e: e.tensor_tensor(MROW[:], MROW[:], TROW[:], OP.mult),
                     waits=e8, inc="vq")
        e11 = p.emit("vector", lambda e: e.tensor_scalar(NMRB[:], MROW[:], -1.0, None, OP.mult),
                     waits=e10, inc="vq")
        return merge(e9, e11), p.ev("tq")

    ev_rows, _ = ln_finalize(PE_[0:1, :])
    wv = merge(ev_rows, pa2_cons[0], pa2_cons[1])
    for nhf in range(2):
        mm(PA[:, nhf * 512:(nhf + 1) * 512], ONESR[:],
           RSTDB[0:1, nhf * 512:(nhf + 1) * 512], start=True, stop=True,
           waits=wv if nhf == 0 else ())
        mm(PB[:, nhf * 512:(nhf + 1) * 512], ONESR[:],
           NMRB[0:1, nhf * 512:(nhf + 1) * 512], start=True, stop=True)
    ev_bc = p.force_inc("tensor")
    y1_last = {}
    tmpb2_cons = {}
    for hc in range(32):
        pp = hc % 2
        evr = p.emit("sync", lambda e, b=pp, c=hc: e.dma_start(Z1STG[b], z1D[c]),
                     waits=merge(z1_cons[pp], ev_bc), inc="sd")
        ea = p.emit("vector",
                    lambda e, b=pp: e.tensor_tensor(TMPB2, Z1STG[b], PA[:], OP.mult),
                    waits=merge(evr, tmpb2_cons), inc="vq")
        eb = p.emit("vector",
                    lambda e, c=hc: e.tensor_tensor(U[:, c * 1024:(c + 1) * 1024],
                                                    TMPB2, PB[:], OP.add),
                    waits=ea, inc="vq")
        tmpb2_cons = eb
        z1_cons[pp] = merge(evr, eb)
        y1_last = p.emit("scalar",
                         lambda e, c=hc: e.dma_start(y1D[c], U[:, c * 1024:(c + 1) * 1024]),
                         waits=eb, inc="sc")

    p.join()

    RAWG = [AR[:, 0:4096].bitcast(I32), AR[:, 4096:8192].bitcast(I32)]
    RAWU = [WOT[:, 0:4096].bitcast(I32), ARF[:, 0:2048].bitcast(I32)]
    GBC = AR[:, 8192:12288]
    UBC = AR[:, 12288:16384]
    GW = [AR[:, 16384:20480], AR[:, 20480:24576]]
    UW = [AR[:, 24576:28672], AR[:, 28672:32768]]
    SILB = ARF[:, 2048:2560].bitcast(BF16)
    HACTB = [ARF[:, 2560:3072].bitcast(BF16), ARF[:, 3072:3584].bitcast(BF16)]
    ZROW = {"g": [ZSG[:, 0:128], ZSG[:, 128:256]],
            "u": [ZSG[:, 256:384], ZSG[:, 384:512]]}
    ZOFF = {"g": [0, 256], "u": [512, 768]}

    ev_yg = {}
    for c in range(32):
        mm(PD[0:1, :], ONES[:], U[:, c * 1024:c * 1024 + 512], start=True, stop=True,
           waits=merge(pd_cons, pe_cons) if c == 0 else pd_cons)
        mm(PD[32:33, :], ONES[:], U[:, c * 1024 + 512:(c + 1) * 1024],
           start=True, stop=True)
        evt = p.force_inc("tensor")
        ea = p.emit("scalar", lambda e: e.activation(STGB[0:1, 0:512], PD[0:1, :], AF.Identity,
                                                     scale=-1.0),
                    waits=merge(evt, ev_yg), inc="cq")
        eb = p.emit("scalar", lambda e: e.activation(STGB[0:1, 512:1024], PD[32:33, :], AF.Identity,
                                                     scale=-1.0),
                    waits=ea, inc="cq")
        pd_cons = eb
        ev_yg = p.emit("scalar", lambda e, cc=c: e.dma_start(YG[cc:cc + 1, :], STGB[:]),
                       waits=eb, inc="sc")

    PSROT = [PA, PB, PC]
    ps_cons = [{}, {}, pc_cons]
    raw_cons = {"g": [{}, {}, {}, {}], "u": [{}, {}, {}, {}]}
    bc_cons = {"g": {}, "u": {}}
    w_cons2 = {"g": [{}, {}], "u": [{}, {}]}
    zs_cons = {"g": [{}, {}], "u": [{}, {}]}
    hact_cons = [{}, {}]
    yh_done = {}
    sil_cons = {}
    stg_cons = pd_cons

    def gu_weights(mat, ic, pg):
        src = dr["ag" if mat == "g" else "au"]
        ssc = dr["sg" if mat == "g" else "su"]
        szr = dr["zg" if mat == "g" else "zu"]
        RAW = RAWG if mat == "g" else RAWU
        BCT = GBC if mat == "g" else UBC
        WT = (GW if mat == "g" else UW)[pg]
        ZR = ZROW[mat][pg]
        off = ZOFF[mat][pg]
        qeng, qsem = ("sync", "sd") if mat == "g" else ("gpsimd", "gw")
        ers = []
        for j in range(2):
            ers.append(p.emit(qeng, lambda e, i=ic, q=j, R=RAW[j]: e.dma_start(
                R.rearrange("p (c j) -> p c j", j=128), src[i, :, q * 16:(q + 1) * 16, :]),
                waits=raw_cons[mat][j], inc=qsem))
        ez1 = p.emit("scalar", lambda e, i=ic, o=off, z=szr: e.dma_start(
            ZSF[:, o:o + 128], z[:, i * 128:(i + 1) * 128]),
            waits=zs_cons[mat][pg], inc="sc")
        ez2 = p.emit("scalar", lambda e, i=ic, o=off, s=ssc: e.dma_start(
            ZSF[:, o + 128:o + 256], s[:, i * 128:(i + 1) * 128]),
            waits=zs_cons[mat][pg], inc="sc")
        ezz = p.emit("vector", lambda e, Z=ZR, o=off: e.scalar_tensor_tensor(
            Z, ZSF[:, o:o + 128], -8.0, ZSF[:, o + 128:o + 256], OP.add, OP.mult),
            waits=merge(ez1, ez2), inc="vq")
        zs_cons[mat][pg] = ezz
        ebc = p.emit("gpsimd", lambda e, i=ic, s=ssc, B=BCT: e.dma_start(
            B.rearrange("p (c j) -> p c j", j=128),
            s[0:32, i * 128:(i + 1) * 128].unsqueeze(0).broadcast_to([128, 32, 128])),
            waits=bc_cons[mat], inc="gw")
        evw = []
        for j in range(2):
            em = p.emit("vector", lambda e, q=j, R=RAW[j], B=BCT, W=WT: e.scalar_tensor_tensor(
                W[:, q * 2048:(q + 1) * 2048], R[:], -8.0,
                B[:, q * 2048:(q + 1) * 2048], OP.add, OP.mult),
                waits=merge(ers[j], ebc, w_cons2[mat][pg]), inc="vq")
            raw_cons[mat][j] = em
            evw.append(em)
        bc_cons[mat] = merge(*evw)
        return merge(*evw), ezz

    for ic in range(NIC):
        pg = ic % 2
        evg, ezg = gu_weights("g", ic, pg)
        evu, ezu = gu_weights("u", ic, pg)
        psg = PSROT[(2 * ic) % 3]
        psu = PSROT[(2 * ic + 1) % 3]
        wv = merge(evg, ps_cons[(2 * ic) % 3])
        for c in range(32):
            for nhf in range(2):
                mm(psg[:, nhf * 512:(nhf + 1) * 512], GW[pg][:, c * 128:(c + 1) * 128],
                   U[:, c * 1024 + nhf * 512:c * 1024 + (nhf + 1) * 512],
                   start=(c == 0), stop=False,
                   waits=wv if (c == 0 and nhf == 0) else ())
        for nhf in range(2):
            mm(psg[:, nhf * 512:(nhf + 1) * 512], ZROW["g"][pg],
               YG[0:32, nhf * 512:(nhf + 1) * 512], start=False, stop=True,
               waits=merge(ezg, ev_yg) if nhf == 0 else ())
        ev_gt = p.force_inc("tensor")
        w_cons2["g"][pg] = ev_gt
        zs_cons["g"][pg] = merge(zs_cons["g"][pg], ev_gt)
        wv = merge(evu, ps_cons[(2 * ic + 1) % 3])
        for c in range(32):
            for nhf in range(2):
                mm(psu[:, nhf * 512:(nhf + 1) * 512], UW[pg][:, c * 128:(c + 1) * 128],
                   U[:, c * 1024 + nhf * 512:c * 1024 + (nhf + 1) * 512],
                   start=(c == 0), stop=False,
                   waits=wv if (c == 0 and nhf == 0) else ())
        for nhf in range(2):
            mm(psu[:, nhf * 512:(nhf + 1) * 512], ZROW["u"][pg],
               YG[0:32, nhf * 512:(nhf + 1) * 512], start=False, stop=True,
               waits=ezu if nhf == 0 else ())
        ev_ut = p.force_inc("tensor")
        w_cons2["u"][pg] = ev_ut
        zs_cons["u"][pg] = merge(zs_cons["u"][pg], ev_ut)
        ev_sil = p.emit("scalar", lambda e, P=psg: e.activation(SILB, P[:], AF.Silu),
                        waits=merge(ev_gt, sil_cons), inc="cq")
        ev_h = p.emit("vector",
                      lambda e, P=psu, b=pg: e.tensor_tensor(HACTB[b], SILB, P[:], OP.mult),
                      waits=merge(ev_sil, ev_ut, hact_cons[pg]), inc="vq")
        sil_cons = ev_h
        ps_cons[(2 * ic) % 3] = ev_h
        ps_cons[(2 * ic + 1) % 3] = ev_h
        ehs = p.emit("scalar", lambda e, i=ic, b=pg: e.dma_start(
            hactD[:, i * 1024:(i + 1) * 1024], HACTB[b]),
            waits=ev_h, inc="sc")
        mm(PD[0:1, :], ONES[:], HACTB[pg][:, 0:512], start=True, stop=True,
           waits=merge(ev_h, pd_cons))
        mm(PD[32:33, :], ONES[:], HACTB[pg][:, 512:1024], start=True, stop=True)
        evt = p.force_inc("tensor")
        ea = p.emit("scalar", lambda e: e.activation(STGB[0:1, 0:512], PD[0:1, :], AF.Identity,
                                                     scale=-1.0),
                    waits=merge(evt, yh_done), inc="cq")
        eb = p.emit("scalar", lambda e: e.activation(STGB[0:1, 512:1024], PD[32:33, :], AF.Identity,
                                                     scale=-1.0),
                    waits=ea, inc="cq")
        pd_cons = eb
        yh_done = p.emit("scalar", lambda e, i=ic: e.dma_start(YH[i:i + 1, :], STGB[:]),
                         waits=eb, inc="sc")
        hact_cons[pg] = merge(ehs, evt)

    p.join()

    DW = [AR[:, 0:4096], AR[:, 4096:8192]]
    DBC = AR[:, 8192:12288]
    RAWD = [AR[:, 12288:16384].bitcast(I32), AR[:, 20480:24576].bitcast(I32),
            AR[:, 24576:28672].bitcast(I32), AR[:, 28672:32768].bitcast(I32)]
    Z2STG = [AR[:, 16384:17408], AR[:, 17408:18432]]
    Y1STG = [AR[:, 18432:19456], AR[:, 19456:20480]]
    DACC = [ARF[:, 0:1024], ARF[:, 1024:2048]]
    DACC2 = [ARF[:, 2048:3072], ARF[:, 3072:4096]]
    TMPF = ARF[:, 4096:5120]

    raw_cons_d = [{} for _ in range(8)]
    rawslot = [0]
    bc_cons_d = {}
    dw_cons = [{}, {}]
    zsd_cons = [{}, {}]
    dacc_cons = [{}, {}]
    dacc2_cons = [{}, {}]
    z2_cons = [{}, {}]
    y1g_cons = [{}, {}]
    pp_cons = [{}, {}]
    slab_cons = {}
    stats2_ev = {}
    tmpf_cons = {}
    daccD_ev = [{} for _ in range(32)]

    for part in range(3):
        g0, nch = DPARTS[part]
        evs = []
        spl = [(i * nch) // 4 for i in range(5)]
        for s in range(4):
            l0, l1 = spl[s], spl[s + 1]
            evs.append(p.emit("sync",
                              lambda e, a=l0, b=l1, g=g0: e.dma_start(
                                  U[:, a * 1024:b * 1024],
                                  hactD[:, (g + a) * 1024:(g + b) * 1024]),
                              waits=slab_cons if s == 0 else (), inc="sd"))
        hact_loaded = merge(*evs)
        for hc in range(32):
            pp = hc % 2
            psd = PA if pp == 0 else PB
            zb = pp * 256
            evw = []
            ers = []
            prs = []
            nq = 2
            for j in range(nq):
                c0, c1 = j * 16, min((j + 1) * 16, nch)
                pr = rawslot[0] % 4
                rawslot[0] += 1
                prs.append(pr)
                ers.append(p.emit("sync", lambda e, c=hc, a=c0, b=c1, R=RAWD[pr], g=g0: e.dma_start(
                    R[:, 0:(b - a) * 128].rearrange("p (c j) -> p c j", j=128),
                    dr["ad"][c, :, g + a:g + b, :]),
                    waits=raw_cons_d[pr], inc="sd"))
            ez1 = p.emit("scalar", lambda e, g=g0, n=nch, c=hc, o=zb: e.dma_start(
                ZSDF[g:g + n, o:o + 128], dr["zdn"][g:g + n, c * 128:(c + 1) * 128]),
                waits=zsd_cons[pp], inc="sc")
            ez2 = p.emit("scalar", lambda e, g=g0, n=nch, c=hc, o=zb: e.dma_start(
                ZSDF[g:g + n, o + 128:o + 256], dr["sdn"][g:g + n, c * 128:(c + 1) * 128]),
                waits=zsd_cons[pp], inc="sc")
            ezz = p.emit("vector", lambda e, b=pp, g=g0, n=nch, o=zb: e.scalar_tensor_tensor(
                ZSD[g:g + n, b * 128:(b + 1) * 128], ZSDF[g:g + n, o:o + 128], -8.0,
                ZSDF[g:g + n, o + 128:o + 256], OP.add, OP.mult),
                waits=merge(ez1, ez2), inc="vq")
            zsd_cons[pp] = ezz
            ebc = p.emit("gpsimd", lambda e, g=g0, n=nch, c=hc: e.dma_start(
                DBC[:, 0:n * 128].rearrange("p (c j) -> p c j", j=128),
                dr["sdn"][g:g + n, c * 128:(c + 1) * 128].unsqueeze(0)
                .broadcast_to([128, n, 128])),
                waits=bc_cons_d, inc="gw")
            for j in range(nq):
                c0, c1 = j * 16, min((j + 1) * 16, nch)
                pr = prs[j]
                em = p.emit("vector", lambda e, a=c0, b=c1, R=RAWD[pr], W=DW[pp]: e.scalar_tensor_tensor(
                    W[:, a * 128:b * 128], R[:, 0:(b - a) * 128], -8.0,
                    DBC[:, a * 128:b * 128], OP.add, OP.mult),
                    waits=merge(ers[j], ebc, dw_cons[pp]), inc="vq")
                raw_cons_d[pr] = em
                evw.append(em)
            bc_cons_d = merge(*evw)
            ev_dw = merge(*evw)
            wv = merge(ev_dw, hact_loaded, pp_cons[pp])
            for c in range(nch):
                for nhf in range(2):
                    mm(psd[:, nhf * 512:(nhf + 1) * 512], DW[pp][:, c * 128:(c + 1) * 128],
                       U[:, c * 1024 + nhf * 512:c * 1024 + (nhf + 1) * 512],
                       start=(c == 0), stop=False,
                       waits=wv if (c == 0 and nhf == 0) else ())
            for nhf in range(2):
                mm(psd[:, nhf * 512:(nhf + 1) * 512], ZSD[g0:g0 + nch, pp * 128:(pp + 1) * 128],
                   YH[g0:g0 + nch, nhf * 512:(nhf + 1) * 512], start=False, stop=True,
                   waits=merge(ezz, yh_done) if nhf == 0 else ())
            evt = p.force_inc("tensor")
            dw_cons[pp] = evt
            zsd_cons[pp] = merge(zsd_cons[pp], evt)
            if part == 0:
                ea = p.emit("vector", lambda e, b=pp, P=psd: e.tensor_copy(DACC[b], P[:]),
                            waits=merge(evt, dacc_cons[pp]), inc="vq")
                pp_cons[pp] = ea
                dacc_cons[pp] = p.emit("scalar",
                                       lambda e, b=pp, c=hc: e.dma_start(daccD[c], DACC[b]),
                                       waits=ea, inc="sc")
                daccD_ev[hc] = dacc_cons[pp]
            elif part == 1:
                er = p.emit("gpsimd", lambda e, b=pp, c=hc: e.dma_start(DACC2[b], daccD[c]),
                            waits=merge(dacc2_cons[pp], daccD_ev[hc]), inc="gw")
                ea = p.emit("vector",
                            lambda e, b=pp, P=psd: e.tensor_tensor(DACC[b], P[:], DACC2[b], OP.add),
                            waits=merge(evt, er, dacc_cons[pp]), inc="vq")
                pp_cons[pp] = ea
                dacc2_cons[pp] = ea
                dacc_cons[pp] = p.emit("scalar",
                                       lambda e, b=pp, c=hc: e.dma_start(daccD[c], DACC[b]),
                                       waits=ea, inc="sc")
                daccD_ev[hc] = dacc_cons[pp]
            else:
                er = p.emit("gpsimd", lambda e, b=pp, c=hc: e.dma_start(DACC2[b], daccD[c]),
                            waits=merge(dacc2_cons[pp], daccD_ev[hc]), inc="gw")
                ey = p.emit("gpsimd", lambda e, b=pp, c=hc: e.dma_start(Y1STG[b], y1D[c]),
                            waits=y1g_cons[pp], inc="gw")
                ea = p.emit("vector",
                            lambda e, b=pp, P=psd: e.tensor_tensor(TMPF, P[:], DACC2[b], OP.add),
                            waits=merge(evt, er, tmpf_cons), inc="vq")
                pp_cons[pp] = ea
                dacc2_cons[pp] = ea
                eb = p.emit("vector",
                            lambda e, b=pp: e.tensor_tensor(Z2STG[b], TMPF, Y1STG[b], OP.add),
                            waits=merge(ea, ey, z2_cons[pp]), inc="vq")
                tmpf_cons = eb
                y1g_cons[pp] = eb
                ez = p.emit("scalar", lambda e, b=pp, c=hc: e.dma_start(z2D[c], Z2STG[b]),
                            waits=eb, inc="sc")
                esq = p.emit("vector",
                             lambda e, b=pp: e.tensor_tensor(SQZ2[:], Z2STG[b], Z2STG[b], OP.mult),
                             waits=eb, inc="vq")
                st = hc == 0
                sp = hc == 31
                mm(PD[0:1, :], ONES[:], Z2STG[pp][:, 0:512], start=st, stop=sp,
                   waits=merge(eb, pd_cons) if st else eb)
                mm(PD[32:33, :], ONES[:], Z2STG[pp][:, 512:1024], start=st, stop=sp)
                mm(PD[64:65, :], ONES[:], SQZ2[:, 0:512], start=st, stop=sp,
                   waits=merge(esq, pe_cons) if st else esq)
                mm(PE_[0:1, :], ONES[:], SQZ2[:, 512:1024], start=st, stop=sp)
                stats2_ev = p.force_inc("tensor")
                z2_cons[pp] = merge(ez, stats2_ev)
        slab_cons = p.force_inc("tensor")

    ev_rows2, _ = ln_finalize(PE_[0:1, :])
    wv = merge(ev_rows2, pp_cons[0], pp_cons[1], pc_cons)
    for nhf in range(2):
        mm(PC[:, nhf * 512:(nhf + 1) * 512], ONESR[:],
           RSTDB[0:1, nhf * 512:(nhf + 1) * 512], start=True, stop=True,
           waits=wv if nhf == 0 else ())
        mm(PA[:, nhf * 512:(nhf + 1) * 512], ONESR[:],
           NMRB[0:1, nhf * 512:(nhf + 1) * 512], start=True, stop=True)
    ev_bc2 = p.force_inc("tensor")
    for hc in range(32):
        pp = hc % 2
        evr = p.emit("sync", lambda e, b=pp, c=hc: e.dma_start(Z2STG[b], z2D[c]),
                     waits=merge(z2_cons[pp], ev_bc2), inc="sd")
        ea = p.emit("vector",
                    lambda e, b=pp: e.tensor_tensor(TMPF, Z2STG[b], PC[:], OP.mult),
                    waits=merge(evr, tmpf_cons), inc="vq")
        tmpf_cons = ea
        eb = p.emit("vector",
                    lambda e, b=pp: e.tensor_tensor(DACC[b], TMPF, PA[:], OP.add),
                    waits=merge(ea, dacc_cons[pp]), inc="vq")
        tmpf_cons = eb
        z2_cons[pp] = merge(evr, eb)
        dacc_cons[pp] = p.emit("scalar",
                               lambda e, b=pp, c=hc: e.dma_start(
                                   outT[c * 128:(c + 1) * 128, :], DACC[b]),
                               waits=eb, inc="sc")

    p.emit("sync", None, waits=p.ev("sd"))
    p.emit("gpsimd", None, waits=p.ev("gw"))
    p.emit("scalar", None, waits=p.ev("sc"))

    p.realize()
    return nc


_CACHED = {}


def _host_stage(inputs):
    f32 = np.float32
    x = np.asarray(inputs["L0_attn_hidden_size"], f32)
    Wq_c = np.asarray(inputs["L0_attn_Wq_c"], f32)
    bq_c = np.asarray(inputs["L0_attn_bq_c"], f32)
    Wq_up = np.asarray(inputs["L0_attn_Wq_up"], f32)
    bq_up = np.asarray(inputs["L0_attn_bq_up"], f32)
    Wkv_c = np.asarray(inputs["L0_attn_Wkv_c"], f32)
    bkv_c = np.asarray(inputs["L0_attn_bkv_c"], f32)
    Wkv_up = np.asarray(inputs["L0_attn_Wkv_up"], f32)
    bkv_up = np.asarray(inputs["L0_attn_bkv_up"], f32)
    Wo = np.asarray(inputs["L0_attn_Wo"], f32)
    bo = np.asarray(inputs["L0_attn_bo"], f32)
    gq = np.asarray(inputs["L0_ffn_gate_qweight"], np.int32)
    gs = np.asarray(inputs["L0_ffn_gate_scales"], f32)
    gz = np.asarray(inputs["L0_ffn_gate_zeros"], f32)
    uq = np.asarray(inputs["L0_ffn_up_qweight"], np.int32)
    us = np.asarray(inputs["L0_ffn_up_scales"], f32)
    uz = np.asarray(inputs["L0_ffn_up_zeros"], f32)
    dq = np.asarray(inputs["L0_ffn_down_qweight"], np.int32)
    ds = np.asarray(inputs["L0_ffn_down_scales"], f32)
    dz = np.asarray(inputs["L0_ffn_down_zeros"], f32)

    C = np.ascontiguousarray
    sh = {}
    sh["aqc"] = C(Wq_c.reshape(12, 128, 32, 128).transpose(0, 3, 2, 1))
    sh["aqup"] = C(Wq_up.reshape(32, 192, 12, 128).transpose(0, 3, 2, 1))
    wkvc_p = np.zeros((640, 4096), f32)
    wkvc_p[0:576] = Wkv_c
    sh["akvc"] = C(wkvc_p.reshape(5, 128, 32, 128).transpose(0, 3, 2, 1))
    sh["akvup"] = C(Wkv_up.reshape(32, 256, 4, 128).transpose(0, 3, 2, 1))
    sh["awo"] = C(Wo.T)
    sh["bqc2"] = C(bq_c.reshape(12, 128).T)
    bquph = np.zeros((128, 64), f32)
    for h in range(32):
        bquph[:, 2 * h] = bq_up[h * 192:h * 192 + 128]
        bquph[0:64, 2 * h + 1] = bq_up[h * 192 + 128:(h + 1) * 192]
    sh["bquph"] = bquph
    bkvc_p = np.zeros(640, f32)
    bkvc_p[0:576] = bkv_c
    sh["bkvc2"] = C(bkvc_p.reshape(5, 128).T)
    bk = bkv_up.reshape(32, 2, 128)
    bkvuk = np.zeros((128, 64), f32)
    for h in range(32):
        bkvuk[:, 2 * h] = bk[h, 0]
    sh["bkvuk"] = bkvuk
    sh["bvrow"] = C(bk[:, 1, :].reshape(1, 4096))
    sh["bo2"] = C(bo.reshape(32, 128).T)
    sh["ag"] = C(gq.reshape(86, 128, 32, 128).transpose(0, 3, 2, 1))
    sh["au"] = C(uq.reshape(86, 128, 32, 128).transpose(0, 3, 2, 1))
    sh["ad"] = C(dq.reshape(32, 128, 86, 128).transpose(0, 3, 2, 1))
    sh["sg"] = C(gs.T)
    sh["zg"] = C(gz.T)
    sh["su"] = C(us.T)
    sh["zu"] = C(uz.T)
    sh["sdn"] = C(ds.T)
    sh["zdn"] = C(dz.T)

    in_maps = []
    for core in range(8):
        seq = core // 4
        tb = core % 4
        d = dict(sh)
        xt = C(x[seq].T)
        ax = xt.reshape(32, 128, 16, 256).transpose(2, 1, 0, 3)
        rot = np.roll(np.arange(16), OWN_QQ0 - tb * 4)
        d["ax"] = C(ax[rot])
        in_maps.append(d)
    return in_maps


def kernel(**inputs):
    if "nc" not in _CACHED:
        _CACHED["nc"] = build_program()
    nc = _CACHED["nc"]
    in_maps = _host_stage(inputs)
    res = run_bass_kernel_spmd(nc, in_maps, core_ids=list(range(8)))
    out = np.empty((2, 4096, 4096), np.float32)
    for core in range(8):
        seq = core // 4
        tb = core % 4
        out[seq, tb * 1024:(tb + 1) * 1024, :] = res.results[core]["outT"].T
    return out


# version 07 (final)
# speedup vs baseline: 1.0695x; 1.0152x over previous
import numpy as np
import concourse.bass as bass
import concourse.mybir as mybir
from concourse.bass_utils import run_bass_kernel_spmd
from concourse.alu_op_type import AluOpType as OP

F32 = mybir.dt.float32
BF16 = mybir.dt.bfloat16
I32 = mybir.dt.int32
AF = mybir.ActivationFunctionType
AX = mybir.AxisListType.X

ENGINES = ("tensor", "vector", "scalar", "sync", "gpsimd")
NH = 32
NIC = 86
ISQ = 0.07216878364870322
OWN_QQ0 = 12
DPARTS = [(0, 32), (32, 32), (64, 22)]
QK = 6
QUEUES = ("sd", "gw", "sc")


class Prog:
    def __init__(self, nc):
        self.nc = nc
        self.ops = {e: [] for e in ENGINES}
        self.sems = {}
        self.counts = {}
        names = ["tq", "vq", "cq"]
        for q in QUEUES:
            names += ["%s%d" % (q, i) for i in range(QK)]
        for name in names:
            self.sems[name] = nc.alloc_semaphore(name)
            self.counts[name] = 0
        self.esem = dict(tensor="tq", vector="vq", scalar="cq")
        self.rr = {q: 0 for q in QUEUES}

    def emit(self, eng, fn, waits=(), inc=None):
        waits = dict(waits)
        if inc in QUEUES:
            k = self.rr[inc]
            self.rr[inc] = (k + 1) % QK
            name = "%s%d" % (inc, k)
            waits[name] = max(waits.get(name, 0), self.counts[name])
            self.counts[name] += 16
            rec = (name, self.counts[name])
        elif inc is not None:
            self.counts[inc] += 1
            rec = (inc, self.counts[inc])
        else:
            rec = None
        self.ops[eng].append([fn, waits, rec])
        return {rec[0]: rec[1]} if rec else {}

    def ev(self, sem):
        if sem in QUEUES:
            return {"%s%d" % (sem, k): self.counts["%s%d" % (sem, k)]
                    for k in range(QK)
                    if self.counts["%s%d" % (sem, k)] > 0}
        return {sem: self.counts[sem]}

    def force_inc(self, eng):
        sem = self.esem[eng]
        for op in reversed(self.ops[eng]):
            if op[0] is None:
                continue
            if op[2] is None:
                self.counts[sem] += 1
                op[2] = (sem, self.counts[sem])
            return {op[2][0]: op[2][1]}
        return {}

    def join(self):
        evs = {}
        for e in ("tensor", "vector", "scalar"):
            evs.update(self.force_inc(e))
        evs = merge(evs, self.ev("sd"), self.ev("gw"), self.ev("sc"))
        for e in ENGINES:
            self.emit(e, None, waits=evs)
        return evs

    def realize(self):
        nc = self.nc
        with nc.Block() as block:
            for name in ENGINES:
                def body(eng, ops=self.ops[name], sems=self.sems):
                    for fn, waits, inc in ops:
                        for s, v in waits.items():
                            eng.wait_ge(sems[s], v)
                        if fn is None:
                            continue
                        ins = fn(eng)
                        if inc is not None:
                            ins.then_inc(sems[inc[0]],
                                         1 if inc[0] in ("tq", "vq", "cq") else 16)
                getattr(block, name)(body)


def merge(*evs):
    out = {}
    for e in evs:
        for k, v in e.items():
            out[k] = max(out.get(k, 0), v)
    return out


def build_program():
    nc = bass.Bass("TRN2", target_bir_lowering=False)

    dr = {}
    def din(name, shape, dt):
        dr[name] = nc.dram_tensor(name, shape, dt, kind="ExternalInput").ap()
    din("ax", [16, 128, 32, 256], F32)
    din("aqc", [12, 128, 32, 128], F32)
    din("akvc", [5, 128, 32, 128], F32)
    din("aqup", [32, 128, 12, 192], F32)
    din("akvup", [32, 128, 4, 256], F32)
    din("awo", [128, 4096], F32)
    din("bqc2", [128, 12], F32)
    din("bquph", [128, 64], F32)
    din("bkvc2", [128, 5], F32)
    din("bkvuk", [128, 64], F32)
    din("bvrow", [1, 4096], F32)
    din("bo2", [128, 32], F32)
    din("ag", [86, 128, 32, 128], I32)
    din("au", [86, 128, 32, 128], I32)
    din("ad", [32, 128, 86, 128], I32)
    din("sg", [32, 11008], F32)
    din("zg", [32, 11008], F32)
    din("su", [32, 11008], F32)
    din("zu", [32, 11008], F32)
    din("sdn", [86, 4096], F32)
    din("zdn", [86, 4096], F32)
    outT = nc.dram_tensor("outT", [4096, 1024], F32, kind="ExternalOutput").ap()

    hactD = nc.dram_tensor("hactD", [128, 88064], BF16).ap()
    z1D = nc.dram_tensor("z1D", [32, 128, 1024], BF16).ap()
    z2D = nc.dram_tensor("z2D", [32, 128, 1024], BF16).ap()
    y1D = nc.dram_tensor("y1D", [32, 128, 1024], BF16).ap()
    daccD = nc.dram_tensor("daccD", [32, 128, 1024], F32).ap()

    U = nc.alloc_sbuf_tensor("U", [128, 32768], BF16)
    AR = nc.alloc_sbuf_tensor("AR", [128, 32768], BF16)
    ARF = nc.alloc_sbuf_tensor("ARF", [128, 5120], F32)
    WOT = nc.alloc_sbuf_tensor("WOT", [128, 4096], BF16)
    OS = nc.alloc_sbuf_tensor("OS", [128, 1024], F32)
    RBUF = nc.alloc_sbuf_tensor("RBUF", [128, 1024], F32)
    RSTDB = nc.alloc_sbuf_tensor("RSTDB", [1, 1024], BF16)
    NMRB = nc.alloc_sbuf_tensor("NMRB", [1, 1024], BF16)
    RECB = nc.alloc_sbuf_tensor("RECB", [1, 1024], BF16)
    STGB = nc.alloc_sbuf_tensor("STGB", [1, 1024], BF16)
    RN = nc.alloc_sbuf_tensor("RN", [1, 4096], BF16)
    BVROW = nc.alloc_sbuf_tensor("BVROW", [1, 4096], BF16)
    YG = nc.alloc_sbuf_tensor("YG", [32, 1024], BF16)
    YH = nc.alloc_sbuf_tensor("YH", [86, 1024], BF16)
    BQC = nc.alloc_sbuf_tensor("BQC", [128, 12], F32)
    BQUP = nc.alloc_sbuf_tensor("BQUP", [128, 64], F32)
    BKVC = nc.alloc_sbuf_tensor("BKVC", [128, 5], F32)
    BKVUK = nc.alloc_sbuf_tensor("BKVUK", [128, 64], F32)
    BO = nc.alloc_sbuf_tensor("BO", [128, 32], F32)
    ONES = nc.alloc_sbuf_tensor("ONES", [128, 1], BF16)
    ONESR = nc.alloc_sbuf_tensor("ONESR", [1, 128], BF16)
    EPS1 = nc.alloc_sbuf_tensor("EPS1", [1, 1], F32)
    MISC = nc.alloc_sbuf_tensor("MISC", [1, 16], F32)
    MISCB = nc.alloc_sbuf_tensor("MISCB", [1, 16], BF16)
    NEGC = nc.alloc_sbuf_tensor("NEGC", [128, 2], F32)
    ZSG = nc.alloc_sbuf_tensor("ZSG", [32, 512], BF16)
    ZSF = nc.alloc_sbuf_tensor("ZSF", [32, 1024], F32)
    ZSD = nc.alloc_sbuf_tensor("ZSD", [86, 256], BF16)
    ZSDF = nc.alloc_sbuf_tensor("ZSDF", [86, 512], F32)
    SQZ2 = nc.alloc_sbuf_tensor("SQZ2", [128, 1024], BF16)

    MROW = ARF[0:1, 0:1024]
    MSQ = ARF[0:1, 1024:2048]
    TROW = ARF[0:1, 2048:3072]

    PA = nc.alloc_psum_tensor("PA", [128, 1024], F32)
    PB = nc.alloc_psum_tensor("PB", [128, 1024], F32)
    PC = nc.alloc_psum_tensor("PC", [128, 1024], F32)
    PD = nc.alloc_psum_tensor("PD", [128, 512], F32)
    PE_ = nc.alloc_psum_tensor("PE", [33, 512], F32)

    p = Prog(nc)

    def mm(ps, lhs, rhs, start, stop, waits=()):
        p.emit("tensor",
               lambda e, a=ps, b=lhs, c=rhs, s1=start, s2=stop:
               e.matmul(a, b, c, start=s1, stop=s2), waits=waits)

    p.emit("vector", lambda e: e.memset(ONES[:], 1.0))
    p.emit("vector", lambda e: e.memset(ONESR[:], 1.0))
    p.emit("vector", lambda e: e.memset(EPS1[:], 1e-5))
    for nm, tgt in (("bqc2", BQC), ("bquph", BQUP), ("bkvc2", BKVC),
                    ("bkvuk", BKVUK), ("bo2", BO)):
        p.emit("sync", lambda e, a=tgt, b=dr[nm]: e.dma_start(a[:], b), inc="sd")
    ev_bias = p.ev("sd")
    ev_bq = p.emit("vector",
                   lambda e: e.tensor_scalar(BQUP[:], BQUP[:], ISQ, None, OP.mult),
                   waits=ev_bias, inc="vq")
    p.emit("gpsimd", lambda e: e.dma_start(BVROW[:], dr["bvrow"]), inc="gw")
    ev_wo = p.emit("gpsimd", lambda e: e.dma_start(WOT[:], dr["awo"]), inc="gw")

    KVC = lambda m: AR[:, m * 4096:(m + 1) * 4096]
    KVCR = AR[0:64, 16384:20480]
    QCC = lambda c: AR[:, 20480 + c * 1024: 20480 + (c + 1) * 1024]
    XQ = [U[:, 0:8192], U[:, 8192:16384]]
    P1W = [U[:, 16384:20480], U[:, 20480:24576]]

    xq_ev = [{}, {}]
    xq_cons = [{}, {}]
    w_cons = {}
    psl_cons = {}
    wslot = 0
    pslot = 0
    for qq in range(16):
        bx = qq % 2
        xq_ev[bx] = p.emit("gpsimd",
                           lambda e, b=bx, q=qq: e.dma_start(
                               XQ[b].rearrange("p (c t) -> p c t", t=256), dr["ax"][q]),
                           waits=xq_cons[bx], inc="gw")
        own = qq >= OWN_QQ0
        mlist = list(range(5)) + (list(range(100, 112)) if own else [])
        for m in mlist:
            isq = m >= 100
            ws = wslot % 2
            wslot += 1
            src = dr["aqc"][m - 100] if isq else dr["akvc"][m]
            wt = P1W[ws]
            w_ev = p.emit("gpsimd",
                          lambda e, a=wt, b=src: e.dma_start(
                              a.rearrange("p (c j) -> p c j", j=128), b),
                          waits=w_cons.get(ws, {}), inc="gw")
            sl = pslot % 4
            pslot += 1
            ps = [PA[:, 0:512], PA[:, 512:1024], PB[:, 0:512], PB[:, 512:1024]][sl]
            wv = merge(w_ev, xq_ev[bx], psl_cons.get(sl, {}))
            for c in range(32):
                mm(ps[:, 0:256], wt[:, c * 128:(c + 1) * 128],
                   XQ[bx][:, c * 256:(c + 1) * 256],
                   start=(c == 0), stop=(c == 31), waits=wv if c == 0 else ())
            ev_t = p.force_inc("tensor")
            w_cons[ws] = ev_t
            xq_cons[bx] = merge(xq_cons[bx], ev_t)
            if isq:
                c = m - 100
                dst = QCC(c)[:, (qq - OWN_QQ0) * 256:(qq - OWN_QQ0 + 1) * 256]
                fn = (lambda e, a=dst, b=ps, bb=BQC[:, c:c + 1]:
                      e.activation(a, b[:, 0:256], AF.Identity, bias=bb))
            elif m < 4:
                dst = KVC(m)[:, qq * 256:(qq + 1) * 256]
                fn = (lambda e, a=dst, b=ps, bb=BKVC[:, m:m + 1]:
                      e.activation(a, b[:, 0:256], AF.Identity, bias=bb))
            else:
                dst = KVCR[:, qq * 256:(qq + 1) * 256]
                fn = (lambda e, a=dst, b=ps, bb=BKVC[0:64, 4:5]:
                      e.activation(a, b[0:64, 0:256], AF.Identity, bias=bb))
            psl_cons[sl] = p.emit("scalar", fn, waits=merge(ev_t, ev_bias), inc="cq")

    p.join()

    SQ = U[:, 22528:23552]
    rn_done = {}
    for sl4 in range(4):
        evq = p.emit("vector",
                     lambda e, s=sl4: e.tensor_tensor(
                         SQ[0:64, :], KVCR[:, s * 1024:(s + 1) * 1024],
                         KVCR[:, s * 1024:(s + 1) * 1024], OP.mult),
                     waits=rn_done, inc="vq")
        for nh2 in range(2):
            mm(PD[0:1, :], ONES[0:64, :], SQ[0:64, nh2 * 512:(nh2 + 1) * 512],
               start=True, stop=True, waits=merge(evq, rn_done))
            evt = p.force_inc("tensor")
            rn_done = p.emit("vector",
                             lambda e, s=sl4, n=nh2: e.tensor_copy(
                                 RN[0:1, s * 1024 + n * 512:s * 1024 + (n + 1) * 512],
                                 PD[0:1, :]),
                             waits=evt, inc="vq")
    p.join()

    VT = [U[:, 0:4096], U[:, 4096:8192]]
    KH = [U[:, 8192:12288], U[:, 12288:16384]]
    QH = [U[:, 16384:17408], U[:, 17408:18432]]
    QR = [U[0:64, 18432:19456], U[0:64, 19456:20480]]
    ET = [U[:, 20480:21504], U[:, 21504:22528]]
    QUPW = [U[:, 23552:25856], U[:, 25856:28160]]
    KVUPW = [U[:, 28160:29184], U[:, 29184:30208]]
    TMPB = U[:, 30208:31232]

    qupw_ev = [{}, {}]
    kvupw_ev = [{}, {}]
    qupw_cons = [{}, {}]
    kvupw_cons = [{}, {}]
    build_cons = [{}, {}]
    pd_cons = {}
    pa_cons = [{}, {}]
    pc_cons = {}
    pe_cons = {}
    et_cons = [{}, {}]
    rbuf_cons = {}
    os_ev = {}

    for hh in range(min(2, NH)):
        ph = hh % 2
        qupw_ev[ph] = p.emit("gpsimd",
                             lambda e, b=ph, h=hh: e.dma_start(
                                 QUPW[b].rearrange("p (c j) -> p c j", j=192),
                                 dr["aqup"][h]), inc="gw")
        kvupw_ev[ph] = p.emit("gpsimd",
                              lambda e, b=ph, h=hh: e.dma_start(
                                  KVUPW[b].rearrange("p (c j) -> p c j", j=256),
                                  dr["akvup"][h]), inc="gw")

    for h in range(NH):
        ph = h % 2
        for ks in range(8):
            wv = merge(kvupw_ev[ph], pd_cons, build_cons[ph] if ks == 0 else {})
            for c in range(4):
                mm(PD[:], KVUPW[ph][:, c * 256:c * 256 + 128],
                   KVC(c)[:, ks * 512:(ks + 1) * 512],
                   start=(c == 0), stop=(c == 3), waits=wv if c == 0 else ())
            evt = p.force_inc("tensor")
            pd_cons = p.emit("scalar",
                             lambda e, s=ks, b=ph, hh=h: e.activation(
                                 KH[b][:, s * 512:(s + 1) * 512], PD[:], AF.Identity,
                                 bias=BKVUK[:, 2 * hh:2 * hh + 1]),
                             waits=evt, inc="cq")
        ev_kh = pd_cons
        for nhf in range(2):
            wv = merge(qupw_ev[ph], pd_cons)
            for c in range(12):
                mm(PD[:], QUPW[ph][:, c * 192:c * 192 + 128],
                   QCC(c)[:, nhf * 512:(nhf + 1) * 512],
                   start=(c == 0), stop=(c == 11), waits=wv if c == 0 else ())
            evt = p.force_inc("tensor")
            pd_cons = p.emit("scalar",
                             lambda e, b=ph, n=nhf, hh=h: e.activation(
                                 QH[b][:, n * 512:(n + 1) * 512], PD[:], AF.Identity,
                                 bias=BQUP[:, 2 * hh:2 * hh + 1], scale=ISQ),
                             waits=merge(evt, ev_bq), inc="cq")
            wv = merge(qupw_ev[ph], pd_cons)
            for c in range(12):
                mm(PD[0:64, :], QUPW[ph][:, c * 192 + 128:c * 192 + 192][0:64, :]
                   if False else QUPW[ph][:, c * 192 + 128:c * 192 + 192],
                   QCC(c)[:, nhf * 512:(nhf + 1) * 512],
                   start=(c == 0), stop=(c == 11), waits=wv if c == 0 else ())
            evt = p.force_inc("tensor")
            pd_cons = p.emit("scalar",
                             lambda e, b=ph, n=nhf, hh=h: e.activation(
                                 QR[b][:, n * 512:(n + 1) * 512], PD[0:64, :], AF.Identity,
                                 bias=BQUP[0:64, 2 * hh + 1:2 * hh + 2], scale=ISQ),
                             waits=merge(evt, ev_bq), inc="cq")
        ev_qh = pd_cons
        qupw_cons[ph] = p.force_inc("tensor")

        ev1 = p.emit("vector",
                     lambda e, b=ph: e.tensor_tensor(SQ[:], QH[b][:], QH[b][:], OP.mult),
                     waits=merge(ev_qh, pe_cons), inc="vq")
        mm(PD[0:1, :], ONES[:], SQ[:, 0:512], start=True, stop=False,
           waits=merge(ev1, pd_cons))
        mm(PD[32:33, :], ONES[:], SQ[:, 512:1024], start=True, stop=False)
        evt = p.force_inc("tensor")
        ev2 = p.emit("vector",
                     lambda e, b=ph: e.tensor_tensor(SQ[0:64, :], QR[b][:], QR[b][:], OP.mult),
                     waits=merge(evt, ev_qh), inc="vq")
        mm(PD[0:1, :], ONES[0:64, :], SQ[0:64, 0:512], start=False, stop=True, waits=ev2)
        mm(PD[32:33, :], ONES[0:64, :], SQ[0:64, 512:1024], start=False, stop=True)
        evt = p.force_inc("tensor")
        ev3 = p.emit("vector", lambda e: e.reduce_max(MISC[0:1, 0:1], PD[0:1, :], axis=AX),
                     waits=evt, inc="vq")
        ev4 = p.emit("vector", lambda e: e.reduce_max(MISC[0:1, 1:2], PD[32:33, :], axis=AX),
                     waits=evt, inc="vq")
        qn_done = p.emit("vector",
                         lambda e: e.tensor_tensor(MISC[0:1, 0:1], MISC[0:1, 0:1],
                                                   MISC[0:1, 1:2], OP.max),
                         waits=merge(ev3, ev4), inc="vq")
        pd_cons = qn_done
        p.emit("vector", lambda e: e.memset(MISC[0:1, 2:3], 0.0))
        ev_kn = {}
        for sl4 in range(4):
            ev5 = p.emit("vector",
                         lambda e, b=ph, s=sl4: e.tensor_tensor(
                             SQ[:], KH[b][:, s * 1024:(s + 1) * 1024],
                             KH[b][:, s * 1024:(s + 1) * 1024], OP.mult),
                         waits=merge(ev_kh, qn_done), inc="vq")
            mm(PD[0:1, :], ONES[:], SQ[:, 0:512], start=True, stop=True,
               waits=merge(ev5, pd_cons))
            mm(PD[32:33, :], ONES[:], SQ[:, 512:1024], start=True, stop=True)
            evt = p.force_inc("tensor")
            e6 = p.emit("vector",
                        lambda e, s=sl4: e.tensor_tensor(
                            TROW[0:1, 0:512], PD[0:1, :],
                            RN[0:1, s * 1024:s * 1024 + 512], OP.add),
                        waits=evt, inc="vq")
            e7 = p.emit("vector",
                        lambda e, s=sl4: e.tensor_tensor(
                            TROW[0:1, 512:1024], PD[32:33, :],
                            RN[0:1, s * 1024 + 512:(s + 1) * 1024], OP.add),
                        waits=evt, inc="vq")
            pd_cons = merge(e6, e7)
            e8 = p.emit("vector",
                        lambda e: e.reduce_max(MISC[0:1, 3:4], TROW[0:1, :], axis=AX),
                        waits=merge(e6, e7), inc="vq")
            ev_kn = p.emit("vector",
                           lambda e: e.tensor_tensor(MISC[0:1, 2:3], MISC[0:1, 2:3],
                                                     MISC[0:1, 3:4], OP.max),
                           waits=e8, inc="vq")
        ev9 = p.emit("vector",
                     lambda e: e.tensor_tensor(MISC[0:1, 4:5], MISC[0:1, 0:1],
                                               MISC[0:1, 2:3], OP.mult),
                     waits=ev_kn, inc="vq")
        ev10 = p.emit("scalar",
                      lambda e: e.activation(MISC[0:1, 5:6], MISC[0:1, 4:5], AF.Ln,
                                             bias=EPS1[:]),
                      waits=ev9, inc="cq")
        ev11 = p.emit("scalar",
                      lambda e: e.activation(MISC[0:1, 6:7], MISC[0:1, 5:6], AF.Exp,
                                             scale=0.5),
                      waits=ev10, inc="cq")
        ev12 = p.emit("vector",
                      lambda e: e.tensor_scalar(MISCB[0:1, 0:1], MISC[0:1, 6:7],
                                                -1.0, None, OP.mult),
                      waits=ev11, inc="vq")
        mm(PD[:, 0:1], ONESR[:], MISCB[0:1, 0:1], start=True, stop=True,
           waits=merge(ev12, pd_cons))
        evt = p.force_inc("tensor")
        ev_negc = p.emit("scalar",
                         lambda e, b=ph: e.activation(NEGC[:, b:b + 1], PD[:, 0:1],
                                                      AF.Identity),
                         waits=evt, inc="cq")
        pd_cons = ev_negc

        for vu in range(8):
            wv = merge(kvupw_ev[ph], pd_cons)
            for tc4 in range(4):
                tc = vu * 4 + tc4
                dst = PD[:, tc4 * 128:(tc4 + 1) * 128]
                for c in range(4):
                    mm(dst, AR[:, c * 4096 + tc * 128:c * 4096 + (tc + 1) * 128],
                       KVUPW[ph][:, c * 256 + 128:(c + 1) * 256],
                       start=(tc4 == 0 and c == 0), stop=False,
                       waits=wv if (tc4 == 0 and c == 0) else ())
                mm(dst, ONESR[:], BVROW[0:1, h * 128:(h + 1) * 128],
                   start=False, stop=(tc4 == 3))
            evt = p.force_inc("tensor")
            pd_cons = p.emit("scalar",
                             lambda e, b=ph, u=vu: e.activation(
                                 VT[b][:, u * 512:(u + 1) * 512], PD[:], AF.Identity),
                             waits=evt, inc="cq")
        ev_vt = pd_cons
        kvupw_cons[ph] = p.force_inc("tensor")

        if h + 2 < NH:
            qupw_ev[ph] = p.emit("gpsimd",
                                 lambda e, b=ph, hh=h + 2: e.dma_start(
                                     QUPW[b].rearrange("p (c j) -> p c j", j=192),
                                     dr["aqup"][hh]),
                                 waits=qupw_cons[ph], inc="gw")
            kvupw_ev[ph] = p.emit("gpsimd",
                                  lambda e, b=ph, hh=h + 2: e.dma_start(
                                      KVUPW[b].rearrange("p (c j) -> p c j", j=256),
                                      dr["akvup"][hh]),
                                  waits=kvupw_cons[ph], inc="gw")

        exp_ev = {}
        scores_ev = {}
        for kt in range(33):
            ep = kt % 2
            if kt < 32:
                psc = PA if ep == 0 else PB
                wv = merge(ev_kh, ev_qh, ev_negc, pa_cons[ep],
                           build_cons[ph] if kt == 0 else {})
                for nhf in range(2):
                    sl = psc[:, nhf * 512:(nhf + 1) * 512]
                    mm(sl, KH[ph][:, kt * 128:(kt + 1) * 128],
                       QH[ph][:, nhf * 512:(nhf + 1) * 512],
                       start=True, stop=False, waits=wv if nhf == 0 else ())
                    mm(sl, KVCR[:, kt * 128:(kt + 1) * 128],
                       QR[ph][:, nhf * 512:(nhf + 1) * 512],
                       start=False, stop=True)
                scores_ev = p.force_inc("tensor")
            if kt >= 1:
                kp = (kt - 1) % 2
                wz = merge(exp_ev, pe_cons if kt == 1 else {})
                mm(PE_[0:1, :], ONES[:], ET[kp][:, 0:512],
                   start=(kt == 1), stop=(kt == 32), waits=wz)
                mm(PE_[32:33, :], ONES[:], ET[kp][:, 512:1024],
                   start=(kt == 1), stop=(kt == 32))
                for nhf in range(2):
                    mm(PC[:, nhf * 512:(nhf + 1) * 512],
                       VT[ph][:, (kt - 1) * 128:kt * 128],
                       ET[kp][:, nhf * 512:(nhf + 1) * 512],
                       start=(kt == 1), stop=(kt == 32),
                       waits=merge(ev_vt, pc_cons) if (kt == 1 and nhf == 0) else ())
                et_cons[kp] = p.force_inc("tensor")
            if kt < 32:
                exp_ev = p.emit("scalar",
                                lambda e, b=ph, ee=ep: e.activation(
                                    ET[ee][:], (PA if ee == 0 else PB)[:], AF.Exp,
                                    bias=NEGC[:, b:b + 1]),
                                waits=merge(scores_ev, et_cons[ep]), inc="cq")
                pa_cons[ep] = exp_ev
        build_cons[ph] = p.force_inc("tensor")

        ez = build_cons[ph]
        r1 = p.emit("vector", lambda e: e.reciprocal(TROW[0:1, 0:512], PE_[0:1, :]),
                    waits=ez, inc="vq")
        r2 = p.emit("vector", lambda e: e.reciprocal(TROW[0:1, 512:1024], PE_[32:33, :]),
                    waits=ez, inc="vq")
        pe_cons = merge(r1, r2)
        r3 = p.emit("vector", lambda e: e.tensor_copy(RECB[:], TROW[:]),
                    waits=pe_cons, inc="vq")
        for nhf in range(2):
            mm(PD[:], ONESR[:], RECB[0:1, nhf * 512:(nhf + 1) * 512],
               start=True, stop=True,
               waits=merge(r3, pd_cons))
            evt = p.force_inc("tensor")
            pd_cons = p.emit("scalar",
                             lambda e, n=nhf: e.activation(
                                 RBUF[:, n * 512:(n + 1) * 512], PD[:], AF.Identity),
                             waits=merge(evt, rbuf_cons), inc="cq")
        if h == 0:
            ev13 = p.emit("vector",
                          lambda e: e.tensor_tensor(OS[:], PC[:], RBUF[:], OP.mult),
                          waits=merge(pd_cons, ez), inc="vq")
            pc_cons = ev13
            rbuf_cons = ev13
            os_ev = ev13
        else:
            ev13 = p.emit("vector",
                          lambda e: e.tensor_tensor(TMPB, PC[:], RBUF[:], OP.mult),
                          waits=merge(pd_cons, ez), inc="vq")
            pc_cons = ev13
            rbuf_cons = ev13
            os_ev = p.emit("vector",
                           lambda e: e.tensor_tensor(OS[:], OS[:], TMPB, OP.add),
                           waits=ev13, inc="vq")

    p.join()

    OSBF = AR[:, 0:1024]
    Z1STG = [AR[:, 1024:2048], AR[:, 2048:3072]]
    SQZ1 = AR[:, 3072:4096]
    TMPB2 = AR[:, 4096:5120]
    XF = [ARF[:, 0:1024], ARF[:, 1024:2048]]

    ev_osbf = p.emit("vector", lambda e: e.tensor_copy(OSBF, OS[:]),
                     waits=os_ev, inc="vq")
    xf_cons = [{}, {}]
    z1_cons = [{}, {}]
    pa2_cons = [{}, {}]
    for hc in range(32):
        pp = hc % 2
        evs = []
        for k4 in range(4):
            evs.append(p.emit("sync",
                              lambda e, b=pp, q=OWN_QQ0 + k4, c=hc, kk=k4: e.dma_start(
                                  XF[b][:, kk * 256:(kk + 1) * 256], dr["ax"][q, :, c, :]),
                              waits=xf_cons[pp] if k4 == 0 else (), inc="sd"))
        xf_ev = merge(*evs)
        psc = PA if pp == 0 else PB
        wv = merge(ev_osbf, ev_wo, pa2_cons[pp])
        for nhf in range(2):
            mm(psc[:, nhf * 512:(nhf + 1) * 512], WOT[:, hc * 128:(hc + 1) * 128],
               OSBF[:, nhf * 512:(nhf + 1) * 512], start=True, stop=True,
               waits=wv if nhf == 0 else ())
        evt = p.force_inc("tensor")
        ev_z1 = p.emit("vector",
                       lambda e, b=pp, c=hc: e.scalar_tensor_tensor(
                           Z1STG[b], (PA if b == 0 else PB)[:], BO[:, c:c + 1], XF[b],
                           OP.add, OP.add),
                       waits=merge(evt, xf_ev, z1_cons[pp]), inc="vq")
        xf_cons[pp] = ev_z1
        pa2_cons[pp] = ev_z1
        esp = p.emit("scalar", lambda e, b=pp, c=hc: e.dma_start(z1D[c], Z1STG[b]),
                     waits=ev_z1, inc="sc")
        ev_sq = p.emit("vector",
                       lambda e, b=pp: e.tensor_tensor(SQZ1, Z1STG[b], Z1STG[b], OP.mult),
                       waits=ev_z1, inc="vq")
        st = hc == 0
        sp = hc == 31
        mm(PD[0:1, :], ONES[:], Z1STG[pp][:, 0:512], start=st, stop=sp,
           waits=merge(ev_z1, pd_cons) if st else ev_z1)
        mm(PD[32:33, :], ONES[:], Z1STG[pp][:, 512:1024], start=st, stop=sp)
        mm(PD[64:65, :], ONES[:], SQZ1[:, 0:512], start=st, stop=sp,
           waits=merge(ev_sq, pe_cons) if st else ev_sq)
        mm(PE_[0:1, :], ONES[:], SQZ1[:, 512:1024], start=st, stop=sp)
        z1_cons[pp] = merge(esp, p.force_inc("tensor"))

    def ln_finalize(pm_sq_hi):
        evt = p.force_inc("tensor")
        e1 = p.emit("vector", lambda e: e.tensor_scalar(
            MROW[0:1, 0:512], PD[0:1, :], 1.0 / 4096, None, OP.mult), waits=evt, inc="vq")
        e2 = p.emit("vector", lambda e: e.tensor_scalar(
            MROW[0:1, 512:1024], PD[32:33, :], 1.0 / 4096, None, OP.mult), waits=evt, inc="vq")
        e3 = p.emit("vector", lambda e: e.tensor_scalar(
            MSQ[0:1, 0:512], PD[64:65, :], 1.0 / 4096, None, OP.mult), waits=evt, inc="vq")
        e4 = p.emit("vector", lambda e: e.tensor_scalar(
            MSQ[0:1, 512:1024], pm_sq_hi, 1.0 / 4096, None, OP.mult), waits=evt, inc="vq")
        e5 = p.emit("vector", lambda e: e.tensor_tensor(TROW[:], MROW[:], MROW[:], OP.mult),
                    waits=merge(e1, e2), inc="vq")
        e6 = p.emit("vector", lambda e: e.tensor_tensor(TROW[:], MSQ[:], TROW[:], OP.subtract),
                    waits=merge(e3, e4, e5), inc="vq")
        e7 = p.emit("scalar", lambda e: e.activation(TROW[:], TROW[:], AF.Ln, bias=EPS1[:]),
                    waits=e6, inc="cq")
        e8 = p.emit("scalar", lambda e: e.activation(TROW[:], TROW[:], AF.Exp, scale=-0.5),
                    waits=e7, inc="cq")
        e9 = p.emit("vector", lambda e: e.tensor_copy(RSTDB[:], TROW[:]), waits=e8, inc="vq")
        e10 = p.emit("vector", lambda e: e.tensor_tensor(MROW[:], MROW[:], TROW[:], OP.mult),
                     waits=e8, inc="vq")
        e11 = p.emit("vector", lambda e: e.tensor_scalar(NMRB[:], MROW[:], -1.0, None, OP.mult),
                     waits=e10, inc="vq")
        return merge(e9, e11), p.ev("tq")

    ev_rows, _ = ln_finalize(PE_[0:1, :])
    wv = merge(ev_rows, pa2_cons[0], pa2_cons[1])
    for nhf in range(2):
        mm(PA[:, nhf * 512:(nhf + 1) * 512], ONESR[:],
           RSTDB[0:1, nhf * 512:(nhf + 1) * 512], start=True, stop=True,
           waits=wv if nhf == 0 else ())
        mm(PB[:, nhf * 512:(nhf + 1) * 512], ONESR[:],
           NMRB[0:1, nhf * 512:(nhf + 1) * 512], start=True, stop=True)
    ev_bc = p.force_inc("tensor")
    y1_last = {}
    tmpb2_cons = {}
    for hc in range(32):
        pp = hc % 2
        evr = p.emit("sync", lambda e, b=pp, c=hc: e.dma_start(Z1STG[b], z1D[c]),
                     waits=merge(z1_cons[pp], ev_bc), inc="sd")
        ea = p.emit("vector",
                    lambda e, b=pp: e.tensor_tensor(TMPB2, Z1STG[b], PA[:], OP.mult),
                    waits=merge(evr, tmpb2_cons), inc="vq")
        eb = p.emit("vector",
                    lambda e, c=hc: e.tensor_tensor(U[:, c * 1024:(c + 1) * 1024],
                                                    TMPB2, PB[:], OP.add),
                    waits=ea, inc="vq")
        tmpb2_cons = eb
        z1_cons[pp] = merge(evr, eb)
        y1_last = p.emit("scalar",
                         lambda e, c=hc: e.dma_start(y1D[c], U[:, c * 1024:(c + 1) * 1024]),
                         waits=eb, inc="sc")

    p.join()

    RAWG = [AR[:, 0:4096].bitcast(I32), AR[:, 4096:8192].bitcast(I32)]
    RAWU = [WOT[:, 0:4096].bitcast(I32), ARF[:, 0:2048].bitcast(I32)]
    GBC = AR[:, 8192:12288]
    UBC = AR[:, 12288:16384]
    GW = [AR[:, 16384:20480], AR[:, 20480:24576]]
    UW = [AR[:, 24576:28672], AR[:, 28672:32768]]
    SILB = ARF[:, 2048:2560].bitcast(BF16)
    HACTB = [ARF[:, 2560:3072].bitcast(BF16), ARF[:, 3072:3584].bitcast(BF16)]
    ZROW = {"g": [ZSG[:, 0:128], ZSG[:, 128:256]],
            "u": [ZSG[:, 256:384], ZSG[:, 384:512]]}
    ZOFF = {"g": [0, 256], "u": [512, 768]}

    ev_yg = {}
    for c in range(32):
        mm(PD[0:1, :], ONES[:], U[:, c * 1024:c * 1024 + 512], start=True, stop=True,
           waits=merge(pd_cons, pe_cons) if c == 0 else pd_cons)
        mm(PD[32:33, :], ONES[:], U[:, c * 1024 + 512:(c + 1) * 1024],
           start=True, stop=True)
        evt = p.force_inc("tensor")
        ea = p.emit("scalar", lambda e: e.activation(STGB[0:1, 0:512], PD[0:1, :], AF.Identity,
                                                     scale=-1.0),
                    waits=merge(evt, ev_yg), inc="cq")
        eb = p.emit("scalar", lambda e: e.activation(STGB[0:1, 512:1024], PD[32:33, :], AF.Identity,
                                                     scale=-1.0),
                    waits=ea, inc="cq")
        pd_cons = eb
        ev_yg = p.emit("scalar", lambda e, cc=c: e.dma_start(YG[cc:cc + 1, :], STGB[:]),
                       waits=eb, inc="sc")

    PSROT = [PA, PB, PC]
    ps_cons = [{}, {}, pc_cons]
    raw_cons = {"g": [{}, {}, {}, {}], "u": [{}, {}, {}, {}]}
    bc_cons = {"g": {}, "u": {}}
    w_cons2 = {"g": [{}, {}], "u": [{}, {}]}
    zs_cons = {"g": [{}, {}], "u": [{}, {}]}
    hact_cons = [{}, {}]
    yh_done = {}
    sil_cons = {}
    stg_cons = pd_cons

    def gu_weights(mat, ic, pg):
        src = dr["ag" if mat == "g" else "au"]
        ssc = dr["sg" if mat == "g" else "su"]
        szr = dr["zg" if mat == "g" else "zu"]
        RAW = RAWG if mat == "g" else RAWU
        BCT = GBC if mat == "g" else UBC
        WT = (GW if mat == "g" else UW)[pg]
        ZR = ZROW[mat][pg]
        off = ZOFF[mat][pg]
        qeng, qsem = ("sync", "sd") if mat == "g" else ("gpsimd", "gw")
        ers = []
        for j in range(2):
            ers.append(p.emit(qeng, lambda e, i=ic, q=j, R=RAW[j]: e.dma_start(
                R.rearrange("p (c j) -> p c j", j=128), src[i, :, q * 16:(q + 1) * 16, :]),
                waits=raw_cons[mat][j], inc=qsem))
        ez1 = p.emit("scalar", lambda e, i=ic, o=off, z=szr: e.dma_start(
            ZSF[:, o:o + 128], z[:, i * 128:(i + 1) * 128]),
            waits=zs_cons[mat][pg], inc="sc")
        ez2 = p.emit("scalar", lambda e, i=ic, o=off, s=ssc: e.dma_start(
            ZSF[:, o + 128:o + 256], s[:, i * 128:(i + 1) * 128]),
            waits=zs_cons[mat][pg], inc="sc")
        ezz = p.emit("vector", lambda e, Z=ZR, o=off: e.scalar_tensor_tensor(
            Z, ZSF[:, o:o + 128], -8.0, ZSF[:, o + 128:o + 256], OP.add, OP.mult),
            waits=merge(ez1, ez2), inc="vq")
        zs_cons[mat][pg] = ezz
        ebc = p.emit("gpsimd", lambda e, i=ic, s=ssc, B=BCT: e.dma_start(
            B.rearrange("p (c j) -> p c j", j=128),
            s[0:32, i * 128:(i + 1) * 128].unsqueeze(0).broadcast_to([128, 32, 128])),
            waits=bc_cons[mat], inc="gw")
        evw = []
        for j in range(2):
            em = p.emit("vector", lambda e, q=j, R=RAW[j], B=BCT, W=WT: e.scalar_tensor_tensor(
                W[:, q * 2048:(q + 1) * 2048], R[:], -8.0,
                B[:, q * 2048:(q + 1) * 2048], OP.add, OP.mult),
                waits=merge(ers[j], ebc, w_cons2[mat][pg]), inc="vq")
            raw_cons[mat][j] = em
            evw.append(em)
        bc_cons[mat] = merge(*evw)
        return merge(*evw), ezz

    for ic in range(NIC):
        pg = ic % 2
        evg, ezg = gu_weights("g", ic, pg)
        evu, ezu = gu_weights("u", ic, pg)
        psg = PSROT[(2 * ic) % 3]
        psu = PSROT[(2 * ic + 1) % 3]
        wv = merge(evg, ps_cons[(2 * ic) % 3])
        for c in range(32):
            for nhf in range(2):
                mm(psg[:, nhf * 512:(nhf + 1) * 512], GW[pg][:, c * 128:(c + 1) * 128],
                   U[:, c * 1024 + nhf * 512:c * 1024 + (nhf + 1) * 512],
                   start=(c == 0), stop=False,
                   waits=wv if (c == 0 and nhf == 0) else ())
        for nhf in range(2):
            mm(psg[:, nhf * 512:(nhf + 1) * 512], ZROW["g"][pg],
               YG[0:32, nhf * 512:(nhf + 1) * 512], start=False, stop=True,
               waits=merge(ezg, ev_yg) if nhf == 0 else ())
        ev_gt = p.force_inc("tensor")
        w_cons2["g"][pg] = ev_gt
        zs_cons["g"][pg] = merge(zs_cons["g"][pg], ev_gt)
        wv = merge(evu, ps_cons[(2 * ic + 1) % 3])
        for c in range(32):
            for nhf in range(2):
                mm(psu[:, nhf * 512:(nhf + 1) * 512], UW[pg][:, c * 128:(c + 1) * 128],
                   U[:, c * 1024 + nhf * 512:c * 1024 + (nhf + 1) * 512],
                   start=(c == 0), stop=False,
                   waits=wv if (c == 0 and nhf == 0) else ())
        for nhf in range(2):
            mm(psu[:, nhf * 512:(nhf + 1) * 512], ZROW["u"][pg],
               YG[0:32, nhf * 512:(nhf + 1) * 512], start=False, stop=True,
               waits=ezu if nhf == 0 else ())
        ev_ut = p.force_inc("tensor")
        w_cons2["u"][pg] = ev_ut
        zs_cons["u"][pg] = merge(zs_cons["u"][pg], ev_ut)
        ev_sil = p.emit("scalar", lambda e, P=psg: e.activation(SILB, P[:], AF.Silu),
                        waits=merge(ev_gt, sil_cons), inc="cq")
        ev_h = p.emit("vector",
                      lambda e, P=psu, b=pg: e.tensor_tensor(HACTB[b], SILB, P[:], OP.mult),
                      waits=merge(ev_sil, ev_ut, hact_cons[pg]), inc="vq")
        sil_cons = ev_h
        ps_cons[(2 * ic) % 3] = ev_h
        ps_cons[(2 * ic + 1) % 3] = ev_h
        ehs = p.emit("scalar", lambda e, i=ic, b=pg: e.dma_start(
            hactD[:, i * 1024:(i + 1) * 1024], HACTB[b]),
            waits=ev_h, inc="sc")
        mm(PD[0:1, :], ONES[:], HACTB[pg][:, 0:512], start=True, stop=True,
           waits=merge(ev_h, pd_cons))
        mm(PD[32:33, :], ONES[:], HACTB[pg][:, 512:1024], start=True, stop=True)
        evt = p.force_inc("tensor")
        ea = p.emit("scalar", lambda e: e.activation(STGB[0:1, 0:512], PD[0:1, :], AF.Identity,
                                                     scale=-1.0),
                    waits=merge(evt, yh_done), inc="cq")
        eb = p.emit("scalar", lambda e: e.activation(STGB[0:1, 512:1024], PD[32:33, :], AF.Identity,
                                                     scale=-1.0),
                    waits=ea, inc="cq")
        pd_cons = eb
        yh_done = p.emit("scalar", lambda e, i=ic: e.dma_start(YH[i:i + 1, :], STGB[:]),
                         waits=eb, inc="sc")
        hact_cons[pg] = merge(ehs, evt)

    p.join()

    DW = [AR[:, 0:4096], AR[:, 4096:8192]]
    DBC = AR[:, 8192:12288]
    RAWD = [AR[:, 12288:16384].bitcast(I32), AR[:, 20480:24576].bitcast(I32),
            AR[:, 24576:28672].bitcast(I32), AR[:, 28672:32768].bitcast(I32)]
    Z2STG = [AR[:, 16384:17408], AR[:, 17408:18432]]
    Y1STG = [AR[:, 18432:19456], AR[:, 19456:20480]]
    DACC = [ARF[:, 0:1024], ARF[:, 1024:2048]]
    DACC2 = [ARF[:, 2048:3072], ARF[:, 3072:4096]]
    TMPF = ARF[:, 4096:5120]

    raw_cons_d = [{} for _ in range(8)]
    rawslot = [0]
    bc_cons_d = {}
    dw_cons = [{}, {}]
    zsd_cons = [{}, {}]
    dacc_cons = [{}, {}]
    dacc2_cons = [{}, {}]
    z2_cons = [{}, {}]
    y1g_cons = [{}, {}]
    pp_cons = [{}, {}]
    slab_cons = {}
    stats2_ev = {}
    tmpf_cons = {}
    daccD_ev = [{} for _ in range(32)]

    for part in range(3):
        g0, nch = DPARTS[part]
        evs = []
        spl = [(i * nch) // 4 for i in range(5)]
        for s in range(4):
            l0, l1 = spl[s], spl[s + 1]
            evs.append(p.emit("sync",
                              lambda e, a=l0, b=l1, g=g0: e.dma_start(
                                  U[:, a * 1024:b * 1024],
                                  hactD[:, (g + a) * 1024:(g + b) * 1024]),
                              waits=slab_cons if s == 0 else (), inc="sd"))
        hact_loaded = merge(*evs)
        for hc in range(32):
            pp = hc % 2
            psd = PA if pp == 0 else PB
            zb = pp * 256
            evw = []
            ers = []
            prs = []
            nq = 2
            for j in range(nq):
                c0, c1 = j * 16, min((j + 1) * 16, nch)
                pr = rawslot[0] % 4
                rawslot[0] += 1
                prs.append(pr)
                ers.append(p.emit("sync", lambda e, c=hc, a=c0, b=c1, R=RAWD[pr], g=g0: e.dma_start(
                    R[:, 0:(b - a) * 128].rearrange("p (c j) -> p c j", j=128),
                    dr["ad"][c, :, g + a:g + b, :]),
                    waits=raw_cons_d[pr], inc="sd"))
            ez1 = p.emit("scalar", lambda e, g=g0, n=nch, c=hc, o=zb: e.dma_start(
                ZSDF[g:g + n, o:o + 128], dr["zdn"][g:g + n, c * 128:(c + 1) * 128]),
                waits=zsd_cons[pp], inc="sc")
            ez2 = p.emit("scalar", lambda e, g=g0, n=nch, c=hc, o=zb: e.dma_start(
                ZSDF[g:g + n, o + 128:o + 256], dr["sdn"][g:g + n, c * 128:(c + 1) * 128]),
                waits=zsd_cons[pp], inc="sc")
            ezz = p.emit("vector", lambda e, b=pp, g=g0, n=nch, o=zb: e.scalar_tensor_tensor(
                ZSD[g:g + n, b * 128:(b + 1) * 128], ZSDF[g:g + n, o:o + 128], -8.0,
                ZSDF[g:g + n, o + 128:o + 256], OP.add, OP.mult),
                waits=merge(ez1, ez2), inc="vq")
            zsd_cons[pp] = ezz
            ebc = p.emit("gpsimd", lambda e, g=g0, n=nch, c=hc: e.dma_start(
                DBC[:, 0:n * 128].rearrange("p (c j) -> p c j", j=128),
                dr["sdn"][g:g + n, c * 128:(c + 1) * 128].unsqueeze(0)
                .broadcast_to([128, n, 128])),
                waits=bc_cons_d, inc="gw")
            for j in range(nq):
                c0, c1 = j * 16, min((j + 1) * 16, nch)
                pr = prs[j]
                em = p.emit("vector", lambda e, a=c0, b=c1, R=RAWD[pr], W=DW[pp]: e.scalar_tensor_tensor(
                    W[:, a * 128:b * 128], R[:, 0:(b - a) * 128], -8.0,
                    DBC[:, a * 128:b * 128], OP.add, OP.mult),
                    waits=merge(ers[j], ebc, dw_cons[pp]), inc="vq")
                raw_cons_d[pr] = em
                evw.append(em)
            bc_cons_d = merge(*evw)
            ev_dw = merge(*evw)
            wv = merge(ev_dw, hact_loaded, pp_cons[pp])
            for c in range(nch):
                for nhf in range(2):
                    mm(psd[:, nhf * 512:(nhf + 1) * 512], DW[pp][:, c * 128:(c + 1) * 128],
                       U[:, c * 1024 + nhf * 512:c * 1024 + (nhf + 1) * 512],
                       start=(c == 0), stop=False,
                       waits=wv if (c == 0 and nhf == 0) else ())
            for nhf in range(2):
                mm(psd[:, nhf * 512:(nhf + 1) * 512], ZSD[g0:g0 + nch, pp * 128:(pp + 1) * 128],
                   YH[g0:g0 + nch, nhf * 512:(nhf + 1) * 512], start=False, stop=True,
                   waits=merge(ezz, yh_done) if nhf == 0 else ())
            evt = p.force_inc("tensor")
            dw_cons[pp] = evt
            zsd_cons[pp] = merge(zsd_cons[pp], evt)
            if part == 0:
                ea = p.emit("vector", lambda e, b=pp, P=psd: e.tensor_copy(DACC[b], P[:]),
                            waits=merge(evt, dacc_cons[pp]), inc="vq")
                pp_cons[pp] = ea
                dacc_cons[pp] = p.emit("scalar",
                                       lambda e, b=pp, c=hc: e.dma_start(daccD[c], DACC[b]),
                                       waits=ea, inc="sc")
                daccD_ev[hc] = dacc_cons[pp]
            elif part == 1:
                er = p.emit("gpsimd", lambda e, b=pp, c=hc: e.dma_start(DACC2[b], daccD[c]),
                            waits=merge(dacc2_cons[pp], daccD_ev[hc]), inc="gw")
                ea = p.emit("vector",
                            lambda e, b=pp, P=psd: e.tensor_tensor(DACC[b], P[:], DACC2[b], OP.add),
                            waits=merge(evt, er, dacc_cons[pp]), inc="vq")
                pp_cons[pp] = ea
                dacc2_cons[pp] = ea
                dacc_cons[pp] = p.emit("scalar",
                                       lambda e, b=pp, c=hc: e.dma_start(daccD[c], DACC[b]),
                                       waits=ea, inc="sc")
                daccD_ev[hc] = dacc_cons[pp]
            else:
                er = p.emit("gpsimd", lambda e, b=pp, c=hc: e.dma_start(DACC2[b], daccD[c]),
                            waits=merge(dacc2_cons[pp], daccD_ev[hc]), inc="gw")
                ey = p.emit("gpsimd", lambda e, b=pp, c=hc: e.dma_start(Y1STG[b], y1D[c]),
                            waits=y1g_cons[pp], inc="gw")
                ea = p.emit("vector",
                            lambda e, b=pp, P=psd: e.tensor_tensor(TMPF, P[:], DACC2[b], OP.add),
                            waits=merge(evt, er, tmpf_cons), inc="vq")
                pp_cons[pp] = ea
                dacc2_cons[pp] = ea
                eb = p.emit("vector",
                            lambda e, b=pp: e.tensor_tensor(Z2STG[b], TMPF, Y1STG[b], OP.add),
                            waits=merge(ea, ey, z2_cons[pp]), inc="vq")
                tmpf_cons = eb
                y1g_cons[pp] = eb
                ez = p.emit("scalar", lambda e, b=pp, c=hc: e.dma_start(z2D[c], Z2STG[b]),
                            waits=eb, inc="sc")
                esq = p.emit("vector",
                             lambda e, b=pp: e.tensor_tensor(SQZ2[:], Z2STG[b], Z2STG[b], OP.mult),
                             waits=eb, inc="vq")
                st = hc == 0
                sp = hc == 31
                mm(PD[0:1, :], ONES[:], Z2STG[pp][:, 0:512], start=st, stop=sp,
                   waits=merge(eb, pd_cons) if st else eb)
                mm(PD[32:33, :], ONES[:], Z2STG[pp][:, 512:1024], start=st, stop=sp)
                mm(PD[64:65, :], ONES[:], SQZ2[:, 0:512], start=st, stop=sp,
                   waits=merge(esq, pe_cons) if st else esq)
                mm(PE_[0:1, :], ONES[:], SQZ2[:, 512:1024], start=st, stop=sp)
                stats2_ev = p.force_inc("tensor")
                z2_cons[pp] = merge(ez, stats2_ev)
        slab_cons = p.force_inc("tensor")

    ev_rows2, _ = ln_finalize(PE_[0:1, :])
    wv = merge(ev_rows2, pp_cons[0], pp_cons[1], pc_cons)
    for nhf in range(2):
        mm(PC[:, nhf * 512:(nhf + 1) * 512], ONESR[:],
           RSTDB[0:1, nhf * 512:(nhf + 1) * 512], start=True, stop=True,
           waits=wv if nhf == 0 else ())
        mm(PA[:, nhf * 512:(nhf + 1) * 512], ONESR[:],
           NMRB[0:1, nhf * 512:(nhf + 1) * 512], start=True, stop=True)
    ev_bc2 = p.force_inc("tensor")
    for hc in range(32):
        pp = hc % 2
        evr = p.emit("sync", lambda e, b=pp, c=hc: e.dma_start(Z2STG[b], z2D[c]),
                     waits=merge(z2_cons[pp], ev_bc2), inc="sd")
        ea = p.emit("vector",
                    lambda e, b=pp: e.tensor_tensor(TMPF, Z2STG[b], PC[:], OP.mult),
                    waits=merge(evr, tmpf_cons), inc="vq")
        tmpf_cons = ea
        eb = p.emit("vector",
                    lambda e, b=pp: e.tensor_tensor(DACC[b], TMPF, PA[:], OP.add),
                    waits=merge(ea, dacc_cons[pp]), inc="vq")
        tmpf_cons = eb
        z2_cons[pp] = merge(evr, eb)
        dacc_cons[pp] = p.emit("scalar",
                               lambda e, b=pp, c=hc: e.dma_start(
                                   outT[c * 128:(c + 1) * 128, :], DACC[b]),
                               waits=eb, inc="sc")

    p.emit("sync", None, waits=p.ev("sd"))
    p.emit("gpsimd", None, waits=p.ev("gw"))
    p.emit("scalar", None, waits=p.ev("sc"))

    p.realize()
    return nc


_CACHED = {}


def _host_stage(inputs):
    f32 = np.float32
    x = np.asarray(inputs["L0_attn_hidden_size"], f32)
    Wq_c = np.asarray(inputs["L0_attn_Wq_c"], f32)
    bq_c = np.asarray(inputs["L0_attn_bq_c"], f32)
    Wq_up = np.asarray(inputs["L0_attn_Wq_up"], f32)
    bq_up = np.asarray(inputs["L0_attn_bq_up"], f32)
    Wkv_c = np.asarray(inputs["L0_attn_Wkv_c"], f32)
    bkv_c = np.asarray(inputs["L0_attn_bkv_c"], f32)
    Wkv_up = np.asarray(inputs["L0_attn_Wkv_up"], f32)
    bkv_up = np.asarray(inputs["L0_attn_bkv_up"], f32)
    Wo = np.asarray(inputs["L0_attn_Wo"], f32)
    bo = np.asarray(inputs["L0_attn_bo"], f32)
    gq = np.asarray(inputs["L0_ffn_gate_qweight"], np.int32)
    gs = np.asarray(inputs["L0_ffn_gate_scales"], f32)
    gz = np.asarray(inputs["L0_ffn_gate_zeros"], f32)
    uq = np.asarray(inputs["L0_ffn_up_qweight"], np.int32)
    us = np.asarray(inputs["L0_ffn_up_scales"], f32)
    uz = np.asarray(inputs["L0_ffn_up_zeros"], f32)
    dq = np.asarray(inputs["L0_ffn_down_qweight"], np.int32)
    ds = np.asarray(inputs["L0_ffn_down_scales"], f32)
    dz = np.asarray(inputs["L0_ffn_down_zeros"], f32)

    C = np.ascontiguousarray
    sh = {}
    sh["aqc"] = C(Wq_c.reshape(12, 128, 32, 128).transpose(0, 3, 2, 1))
    sh["aqup"] = C(Wq_up.reshape(32, 192, 12, 128).transpose(0, 3, 2, 1))
    wkvc_p = np.zeros((640, 4096), f32)
    wkvc_p[0:576] = Wkv_c
    sh["akvc"] = C(wkvc_p.reshape(5, 128, 32, 128).transpose(0, 3, 2, 1))
    sh["akvup"] = C(Wkv_up.reshape(32, 256, 4, 128).transpose(0, 3, 2, 1))
    sh["awo"] = C(Wo.T)
    sh["bqc2"] = C(bq_c.reshape(12, 128).T)
    bquph = np.zeros((128, 64), f32)
    for h in range(32):
        bquph[:, 2 * h] = bq_up[h * 192:h * 192 + 128]
        bquph[0:64, 2 * h + 1] = bq_up[h * 192 + 128:(h + 1) * 192]
    sh["bquph"] = bquph
    bkvc_p = np.zeros(640, f32)
    bkvc_p[0:576] = bkv_c
    sh["bkvc2"] = C(bkvc_p.reshape(5, 128).T)
    bk = bkv_up.reshape(32, 2, 128)
    bkvuk = np.zeros((128, 64), f32)
    for h in range(32):
        bkvuk[:, 2 * h] = bk[h, 0]
    sh["bkvuk"] = bkvuk
    sh["bvrow"] = C(bk[:, 1, :].reshape(1, 4096))
    sh["bo2"] = C(bo.reshape(32, 128).T)
    sh["ag"] = C(gq.reshape(86, 128, 32, 128).transpose(0, 3, 2, 1))
    sh["au"] = C(uq.reshape(86, 128, 32, 128).transpose(0, 3, 2, 1))
    sh["ad"] = C(dq.reshape(32, 128, 86, 128).transpose(0, 3, 2, 1))
    sh["sg"] = C(gs.T)
    sh["zg"] = C(gz.T)
    sh["su"] = C(us.T)
    sh["zu"] = C(uz.T)
    sh["sdn"] = C(ds.T)
    sh["zdn"] = C(dz.T)

    in_maps = []
    for core in range(8):
        seq = core // 4
        tb = core % 4
        d = dict(sh)
        xt = C(x[seq].T)
        ax = xt.reshape(32, 128, 16, 256).transpose(2, 1, 0, 3)
        rot = np.roll(np.arange(16), OWN_QQ0 - tb * 4)
        d["ax"] = C(ax[rot])
        in_maps.append(d)
    return in_maps


def kernel(**inputs):
    if "nc" not in _CACHED:
        _CACHED["nc"] = build_program()
    nc = _CACHED["nc"]
    in_maps = _host_stage(inputs)
    res = run_bass_kernel_spmd(nc, in_maps, core_ids=list(range(8)))
    out = np.empty((2, 4096, 4096), np.float32)
    for core in range(8):
        seq = core // 4
        tb = core % 4
        out[seq, tb * 1024:(tb + 1) * 1024, :] = res.results[core]["outT"].T
    return out
